# version 02
# baseline (speedup 1.0000x reference)
import numpy as np
import ml_dtypes
import concourse.bass as bass
import concourse.mybir as mybir
from concourse.bass_utils import run_bass_kernel_spmd

F32 = mybir.dt.float32
BF16 = mybir.dt.bfloat16
I32 = mybir.dt.int32
AF = mybir.ActivationFunctionType
ALU = mybir.AluOpType
AX = mybir.AxisListType

DM = 1024
H = 16
CK = 128
NS = 8
INNER = 4096
MAGIC = 12582912.0
DELTA = 0.4995
EPS = 1e-5
C_EPS, C_MAGIC, C_NMAGIC, C_ONE = 0, 1, 2, 3
SLABW = 2052


def build_program(NC=16, phases=5, w1=True, p4sub=99):
    import os
    G = lambda k: os.environ.get(k, "1") == "1"
    nc = bass.Bass("TRN2", target_bir_lowering=False, detect_race_conditions=False)
    T = NC * CK

    dx = nc.dram_tensor("x_sl", [T, DM], F32, kind="ExternalInput").ap()
    dxprev = nc.dram_tensor("x_prev", [1, DM], F32, kind="ExternalInput").ap()
    dWw = nc.dram_tensor("WwT", [DM, DM], F32, kind="ExternalInput").ap()
    dWgba = nc.dram_tensor("WgbaT", [DM, 48], F32, kind="ExternalInput").ap()
    dWo = nc.dram_tensor("WoT", [DM, DM], F32, kind="ExternalInput").ap()
    dwg = nc.dram_tensor("wgT", [DM, INNER], F32, kind="ExternalInput").ap()
    dwu = nc.dram_tensor("wuT", [DM, INNER], F32, kind="ExternalInput").ap()
    dwd = nc.dram_tensor("wdT", [INNER, DM], F32, kind="ExternalInput").ap()
    da16 = nc.dram_tensor("a16", [16, 1], F32, kind="ExternalInput").ap()
    ddtb = nc.dram_tensor("dtb16", [16, 1], F32, kind="ExternalInput").ap()
    dconsts = nc.dram_tensor("consts", [128, 16], F32, kind="ExternalInput").ap()
    dident = nc.dram_tensor("identb", [128, 128], BF16, kind="ExternalInput").ap()
    didentf = nc.dram_tensor("identf", [128, 128], F32, kind="ExternalInput").ap()
    dmask = nc.dram_tensor("maskst", [128, 128], F32, kind="ExternalInput").ap()
    dSELH = nc.dram_tensor("SELH", [128, H * CK], F32, kind="ExternalInput").ap()
    dQC = nc.dram_tensor("QC32", [16, 128], F32, kind="ExternalInput").ap()
    dM8 = nc.dram_tensor("M8c", [16, 8], F32, kind="ExternalInput").ap()
    dSEL16 = nc.dram_tensor("SEL16b", [128, NS * 128], BF16, kind="ExternalInput").ap()
    dones1 = nc.dram_tensor("ones1f", [1, 128], F32, kind="ExternalInput").ap()
    dones128 = nc.dram_tensor("ones128", [128, 1], F32, kind="ExternalInput").ap()
    dmprev = nc.dram_tensor("mprev16", [16, 8], F32, kind="ExternalInput").ap()
    dy = nc.dram_tensor("y_sl", [T, DM], F32, kind="ExternalOutput").ap()

    scr_g = nc.dram_tensor("scr_g", [DM, INNER], BF16, kind="Internal")
    scr_u = nc.dram_tensor("scr_u", [DM, INNER], BF16, kind="Internal")
    scr_d = nc.dram_tensor("scr_d", [INNER, DM], BF16, kind="Internal")
    GN = 128 * 512 + 16
    gin = nc.dram_tensor("gin", [1, GN], F32, kind="Internal")
    gout = nc.dram_tensor("gout", [8, GN], F32, kind="Internal")

    base0 = (nc.sbuf_base + 63) & ~63
    cur = [base0]

    def alo(name, shape, dtype, at=None):
        if at is None:
            at = cur[0]
            nbytes = int(np.prod(shape[1:])) * mybir.dt.size(dtype)
            cur[0] = (at + nbytes + 63) & ~63
        return nc.alloc_sbuf_tensor_at(name, shape, dtype, offset=at)

    stream = alo("stream", [128, NC * 1024], F32)
    consts = alo("consts_sb", [128, 16], F32)
    gate_tm = alo("gate_tm", [128, NC * 16], F32)
    cgv = alo("cgv", [128, NC], F32)
    cuv = alo("cuv", [128, NC], F32)
    cdv = alo("cdv", [128, NC], F32)
    mwth = alo("mwth", [128, 8], F32)
    nthv = alo("nthv", [128, 4], F32)
    identb = alo("identb_sb", [128, 128], BF16)
    identf = alo("identf_sb", [128, 128], F32)
    QC32 = alo("QC32_sb", [16, 128], F32)
    M8c = alo("M8c_sb", [16, 8], F32)
    ones1f = alo("ones1f_sb", [1, 128], F32)
    ones128 = alo("ones128_sb", [128, 1], F32)
    mprev16 = alo("mprev_sb", [16, 8], F32)
    a16 = alo("a16_sb", [16, 1], F32)
    dtb16 = alo("dtb16_sb", [16, 1], F32)
    negeA = alo("negeA", [16, 1], F32)
    wabs = alo("wabs", [128, 192], F32)
    wsum3 = alo("wsum3", [128, 4], F32)
    t1a = alo("t1a", [128, 1], F32)
    t1b = alo("t1b", [128, 1], F32)
    t1c = alo("t1c", [128, 1], F32)
    t1d = alo("t1d", [128, 1], F32)
    tp1 = alo("tp1", [1, 8], F32)

    attn_base = cur[0]
    xn_fm = alo("xn_fm", [128, NS * SLABW], BF16)
    vb_tm = alo("vb_tm", [128, NC * 1024], BF16)
    states = alo("states", [128, NC * 512], BF16)
    dec_fm = alo("dec_fm", [128, NC * CK], F32)
    Wsh = alo("Wsh", [128, NS * 1024], BF16)
    Wgb = alo("Wgb", [128, NS * 48], BF16)
    maskst = nc.alloc_sbuf_tensor_at("maskst_sb", [128, 128], F32,
                                     offset=nc.lookup_mloc(Wgb).addr)
    S_run = alo("S_run", [128, 2 * 512], F32)
    Sg = nc.alloc_sbuf_tensor_at("Sg", [128, 512], F32,
                                 offset=nc.lookup_mloc(S_run).addr)
    if NC % 2 == 0:
        SEL16 = nc.alloc_sbuf_tensor_at("SEL16_sb", [128, NS * 128], BF16,
                                        offset=nc.lookup_mloc(S_run).addr + 2048)
    else:
        SEL16 = alo("SEL16_sb", [128, NS * 128], BF16)
    S0 = alo("S0", [128, 2 * 512], F32)
    t16 = alo("t16", [16, 1024], F32)
    xprev = nc.alloc_sbuf_tensor_at("xprev", [1, DM], F32,
                                    offset=nc.lookup_mloc(t16).addr)
    declast = alo("declast", [16, NC], F32)
    pn16 = alo("pn16", [16, NC], F32)
    DgT = alo("DgT", [16, 8], F32)
    G16 = alo("G16", [16, 8], F32)
    P16 = alo("P16", [16, 8], F32)
    cj16 = alo("cj16", [16, 8], F32)
    R16 = alo("R16", [16, 8], F32)

    pool_base = cur[0]
    wk_tm = alo("wk_tm", [128, 1024], BF16)
    vbdw = alo("vbdw", [128, 1024], BF16)
    xn_tmp = alo("xn_tmp", [128, 1024], F32)
    xnb = alo("xnb", [128, 1024], BF16)
    beta_t = alo("beta_t", [128, 16], F32)
    apre = alo("apre", [128, 16], F32)
    dwT = alo("dwT", [128, 16], F32)
    Earr = alo("Earr", [128, 8], F32)
    p2_end = cur[0]
    cur[0] = pool_base
    intra_sb = alo("intra_sb", [128, H * CK], BF16)
    Lh_sb = alo("Lh_sb", [128, 2 * 128], F32)
    Lraw = alo("Lraw", [128, 2 * 128], F32)
    o_tm = alo("o_tm", [128, 1024], BF16)
    o_fm = alo("o_fm", [128, 1024], BF16)
    rkdx = alo("rkdx", [128, NS * 128], BF16)
    Strue = alo("Strue", [128, 512], BF16)
    negdecT = alo("negdecT", [128, 16], F32)
    pnb = alo("pnb", [128, 8], F32)
    dexpb = alo("dexpb", [128, 128], BF16)
    cbj = alo("cbj", [128, 8], F32)
    p4_end = cur[0]

    cur[0] = attn_base
    h_arena = alo("h_arena", [128, 4 * INNER], F32)
    hq_tm = alo("hq_tm", [128, INNER], BF16)
    q1_fm = alo("q1_fm", [128, NS * 512], BF16)
    tg_sb = alo("tg_sb", [128, NS * 512], BF16)
    tu_sb = alo("tu_sb", [128, NS * 512], BF16)
    td_sb = nc.alloc_sbuf_tensor_at("td_sb", [128, 4 * 1024], BF16,
                                    offset=nc.lookup_mloc(tg_sb).addr)
    y_sb = alo("y_sb", [128, 1024], F32)
    xc = alo("xc", [128, 1024], F32)
    f_tmp = alo("f_tmp", [128, 1024], F32)
    f_bf = alo("f_bf", [128, 1024], BF16)
    q5a = alo("q5a", [128, 1], F32)
    q5b = alo("q5b", [128, 1], F32)
    q5c = alo("q5c", [128, 1], F32)
    q5d = alo("q5d", [128, 1], F32)
    ffn_end = cur[0]

    cur[0] = max(p2_end, p4_end, ffn_end)
    stag = alo("stag", [128, 2 * 512], F32)
    ternst = alo("ternst", [128, 2 * 512], BF16)
    SELH = nc.alloc_sbuf_tensor_at("SELH_sb", [128, H * CK], F32,
                                   offset=nc.lookup_mloc(stag).addr)
    assert cur[0] <= nc.sbuf_top, f"SBUF over: {cur[0]} > {nc.sbuf_top}"

    nc.const_aps.register(F32, MAGIC, consts[:, C_MAGIC:C_MAGIC + 1])
    nc.const_aps.register(F32, -MAGIC, consts[:, C_NMAGIC:C_NMAGIC + 1])
    PS = nc.alloc_psum_tensor("PSA", [128, 4096], F32)

    def psb(a, b):
        return PS[:, a:b].bitcast(BF16)

    XT = (1792, 3872)
    WKT = (1664, 3584)
    OTT = (1664, 0)
    QT = (2048, 2560)

    from contextlib import ExitStack
    ctx = ExitStack()
    sem = {}
    for s in ("ld", "ldc", "gd", "pe", "dve", "act", "gpc", "cc", "st", "ph"):
        sem[s] = ctx.enter_context(nc.semaphore("s_" + s))
    block = ctx.enter_context(nc.Block())
    cnt = {k: 0 for k in sem}
    tags = {}

    def inc(instr, name, n=1):
        instr.then_inc(sem[name], n)
        cnt[name] += n
        return cnt[name]

    def dinc(instr, name):
        return inc(instr, name, 16)

    def w(eng, name, tag):
        if tag is not None:
            eng.wait_ge(sem[name], tag)

    _pe_last = [None]

    class _TEW:
        def __init__(self, e):
            self._e = e

        def matmul(self, *a, **k):
            r = self._e.matmul(*a, **k)
            _pe_last[0] = r
            return r

        def transpose(self, *a, **k):
            r = self._e.transpose(*a, **k)
            _pe_last[0] = None
            return r

        def drain(self):
            if _pe_last[0] is None:
                return self._e.drain()
            r = _pe_last[0]
            _pe_last[0] = None
            return r

        def __getattr__(self, k):
            return getattr(self._e, k)

    SY, VE, SC, GP = block.sync, block.vector, block.scalar, block.gpsimd
    TE = lambda f: block.tensor(lambda te: f(_TEW(te)))

    def barrier():
        target = cnt["ph"] + 5
        for s in (SY, VE, SC, TE, GP):
            def f(eng, target=target):
                eng.sem_inc(sem["ph"], 1)
                eng.wait_ge(sem["ph"], target)
            s(f)
        cnt["ph"] = target

    def p0(sync):
        for name, dst, src in (
            ("consts", consts, dconsts), ("identb", identb, dident),
            ("identf", identf, didentf),
            ("QC32", QC32, dQC), ("M8c", M8c, dM8),
            ("ones1f", ones1f, dones1),
            ("ones128", ones128, dones128), ("mprev", mprev16, dmprev),
            ("a16", a16, da16), ("dtb16", dtb16, ddtb), ("xprev", xprev, dxprev),
        ):
            tags["ld_" + name] = dinc(sync.dma_start(out=dst[:], in_=src), "ldc")
    SY(p0)
    P0C = cnt["ldc"]

    def p0g(gp):
        tags["gd_Ww"] = dinc(gp.dma_start(
            out=Wsh[:].rearrange("p (s n) -> p s n", s=NS),
            in_=dWw.rearrange("(s p) n -> p s n", p=128)), "gd")
        tags["gd_Wgb"] = dinc(gp.dma_start(
            out=Wgb[:].rearrange("p (s n) -> p s n", s=NS),
            in_=dWgba.rearrange("(s p) n -> p s n", p=128)), "gd")
    GP(p0g)

    def p0v(v):
        v.memset(dexpb[:], 0.0)
        v.memset(S_run[:], 0.0)
        v.memset(S0[:], 0.0)
        if os.environ.get("DBG", "0") == "1":
            for tns in (xn_fm, vb_tm, states, rkdx, Strue, o_tm, o_fm,
                        intra_sb, negdecT, pnb, Lh_sb, Lraw, gate_tm,
                        declast, pn16, DgT, cj16):
                v.memset(tns[:], 0.0)
        tags["z_dec"] = inc(v.memset(dec_fm[:], 0.0), "dve")
    VE(p0v)

    def p0s(sc):
        w(sc, "ldc", P0C)
        sc.activation(negeA[:], a16[:], AF.Exp)
        sc.activation(xnb[0:1, :], xprev[:], AF.Square, accum_out=tp1[:, 1:2])
        tags["a_prev"] = inc(sc.drain(), "act")
    SC(p0s)

    def p0v2(v):
        w(v, "act", tags["a_prev"])
        v.drain()
        v.tensor_scalar(negeA[:], negeA[:], -1.0, None, ALU.mult)
        v.reduce_sum(tp1[:, 0:1], xprev[:], axis=AX.X)
        v.drain()
        v.tensor_scalar(tp1[:, 2:3], tp1[:, 0:1], 1.0 / DM, None, ALU.mult)
        v.drain()
        v.scalar_tensor_tensor(tp1[:, 3:4], tp1[:, 2:3], -1.0, tp1[:, 2:3], ALU.mult, ALU.mult)
        v.drain()
        v.scalar_tensor_tensor(tp1[:, 4:5], tp1[:, 1:2], 1.0 / DM, tp1[:, 3:4], ALU.mult, ALU.add)
        tags["v_prev1"] = inc(v.drain(), "dve")
    VE(p0v2)

    def p0s2(sc):
        w(sc, "dve", tags["v_prev1"])
        w(sc, "ldc", P0C)
        sc.activation(tp1[:, 5:6], tp1[:, 4:5], AF.Ln, bias=consts[0:1, C_EPS:C_EPS + 1])
        sc.drain()
        tags["a_prev2"] = inc(
            sc.activation(tp1[:, 6:7], tp1[:, 5:6], AF.Exp, scale=-0.5), "act")
    SC(p0s2)

    def p0v3(v):
        w(v, "act", tags["a_prev2"])
        v.drain()
        v.tensor_scalar(xprev[:], xprev[:], tp1[:, 2:3], tp1[:, 6:7], ALU.subtract, ALU.mult)
        v.drain()
        tags["v_prev2"] = inc(v.tensor_copy(xnb[0:1, :], xprev[:]), "dve")
        if os.environ.get("DBG", "0") == "1":
            v.memset(t16[:], 0.0)
    VE(p0v3)

    for s in range(NS):
        def p0t(te, s=s):
            if s == 0:
                w(te, "dve", tags["v_prev2"])
                w(te, "ldc", P0C)
            if s >= 2:
                w(te, "dve", tags[f"v_prev3_{s - 2}"])
            slot = s % 2
            te.transpose(psb(XT[slot], XT[slot] + 64)[:, 0:1],
                         xnb[0:1, 128 * s:128 * (s + 1)], identb[0:1, 0:1])
            tags[f"pe_prev{s}"] = inc(te.drain(), "pe")
        TE(p0t)

        def p0v4(v, s=s):
            w(v, "pe", tags[f"pe_prev{s}"])
            slot = s % 2
            v.tensor_copy(xn_fm[:, SLABW * s:SLABW * s + 1],
                          psb(XT[slot], XT[slot] + 64)[:, 0:1])
            tags[f"v_prev3_{s}"] = inc(v.drain(), "dve")
        VE(p0v4)
    tags["v_prev3"] = tags[f"v_prev3_{NS - 1}"]

    wslabs = []
    for mi, dsrc in enumerate((dwg, dwu)):
        for a in range(8):
            for b in range(8):
                wslabs.append((dsrc[128 * a:128 * (a + 1), 512 * b:512 * (b + 1)],
                               64 * mi + 8 * a + b, mi))
    for a in range(32):
        for b in range(2):
            wslabs.append((dwd[128 * a:128 * (a + 1), 512 * b:512 * (b + 1)],
                           128 + 2 * a + b, 2))

    def scr_dst(k):
        _, idx, mi = wslabs[k]
        if mi == 0:
            r = idx
            return scr_g.ap()[128 * (r // 8):128 * (r // 8 + 1),
                              512 * (r % 8):512 * (r % 8 + 1)]
        if mi == 1:
            r = idx - 64
            return scr_u.ap()[128 * (r // 8):128 * (r // 8 + 1),
                              512 * (r % 8):512 * (r % 8 + 1)]
        r = idx - 128
        return scr_d.ap()[128 * (r // 2):128 * (r // 2 + 1),
                          512 * (r % 2):512 * (r % 2 + 1)]

    wl = [0]

    def p1_w_loads(sync, budget):
        for _ in range(budget):
            if wl[0] >= len(wslabs):
                return
            k = wl[0]
            src, idx, mi = wslabs[k]
            slot = k % 2
            w(sync, "act", tags.get(f"a_w{k - 2}"))
            tags[f"ld_w{k}"] = dinc(sync.dma_start(
                out=stag[:, 512 * slot:512 * (slot + 1)], in_=src), "ld")
            wl[0] += 1

    wd_ = [0]

    def p1_w_abs(sc):
        while wd_[0] < wl[0]:
            k = wd_[0]
            w(sc, "ld", tags[f"ld_w{k}"])
            slot = k % 2
            _, idx, mi = wslabs[k]
            sc.activation(xnb[:, 0:512], stag[:, 512 * slot:512 * (slot + 1)],
                          AF.Abs, accum_out=wabs[:, idx:idx + 1])
            tags[f"a_w{k}"] = inc(sc.drain(), "act")
            wd_[0] += 1

    for t in range(NC):
        def p1_sy(sync, t=t):
            tags[f"ld_x{t}"] = dinc(sync.dma_start(
                out=stream[:, 1024 * t:1024 * (t + 1)],
                in_=dx[128 * t:128 * (t + 1), :]), "ld")
            if w1:
                p1_w_loads(sync, 6)
        SY(p1_sy)

        def p1_sc(sc, t=t):
            w(sc, "ld", tags[f"ld_x{t}"])
            if t > 0:
                w(sc, "dve", tags[f"v_st{t - 1}"])
            if t > 0:
                w(sc, "pe", tags[f"pe_xn{t - 1}_{NS - 1}"])
            else:
                w(sc, "pe", tags[f"pe_prev{NS - 1}"])
            sc.activation(xnb[:], stream[:, 1024 * t:1024 * (t + 1)],
                          AF.Square, accum_out=t1b[:])
            tags[f"a_sq{t}"] = inc(sc.drain(), "act")
            if w1:
                p1_w_abs(sc)
        SC(p1_sc)

        def p1_ve(v, t=t):
            w(v, "ld", tags[f"ld_x{t}"])
            v.reduce_sum(t1a[:], stream[:, 1024 * t:1024 * (t + 1)], axis=AX.X)
            w(v, "act", tags[f"a_sq{t}"])
            v.drain()
            v.tensor_scalar(t1c[:], t1a[:], 1.0 / DM, None, ALU.mult)
            v.drain()
            v.scalar_tensor_tensor(t1d[:], t1c[:], -1.0, t1c[:], ALU.mult, ALU.mult)
            v.drain()
            v.scalar_tensor_tensor(t1d[:], t1b[:], 1.0 / DM, t1d[:], ALU.mult, ALU.add)
            tags[f"v_st{t}"] = inc(v.drain(), "dve")
        VE(p1_ve)

        def p1_sc2(sc, t=t):
            w(sc, "dve", tags[f"v_st{t}"])
            sc.activation(t1a[:], t1d[:], AF.Ln, bias=consts[:, C_EPS:C_EPS + 1])
            sc.drain()
            tags[f"a_r{t}"] = inc(sc.activation(t1d[:], t1a[:], AF.Exp, scale=-0.5), "act")
        SC(p1_sc2)

        def p1_ve2(v, t=t):
            w(v, "act", tags[f"a_r{t}"])
            v.tensor_scalar(xn_tmp[:], stream[:, 1024 * t:1024 * (t + 1)],
                            t1c[:], t1d[:], ALU.subtract, ALU.mult)
            v.drain()
            v.tensor_tensor(stream[:, 1024 * t:1024 * (t + 1)],
                            stream[:, 1024 * t:1024 * (t + 1)], xn_tmp[:], ALU.add)
            if t > 0:
                w(v, "pe", tags[f"pe_xn{t - 1}_{NS - 1}"])
            tags[f"v_xn{t}"] = inc(v.tensor_copy(xnb[:], xn_tmp[:]), "dve")
        VE(p1_ve2)

        for s in range(NS):
            def p1_te(te, t=t, s=s):
                gi = t * NS + s
                if s == 0:
                    w(te, "dve", tags[f"v_xn{t}"])
                if gi >= 2:
                    w(te, "dve", tags.get(f"v_ev{gi - 2}"))
                slot = gi % 2
                te.transpose(psb(XT[slot], XT[slot] + 64),
                             xnb[:, 128 * s:128 * (s + 1)], identb[:])
                tags[f"pe_xn{t}_{s}"] = inc(te.drain(), "pe")
            TE(p1_te)

            def p1_ve3(v, t=t, s=s):
                gi = t * NS + s
                w(v, "pe", tags[f"pe_xn{t}_{s}"])
                slot = gi % 2
                v.tensor_copy(
                    xn_fm[:, SLABW * s + 1 + 128 * t:SLABW * s + 1 + 128 * (t + 1)],
                    psb(XT[slot], XT[slot] + 64))
                tags[f"v_ev{gi}"] = inc(v.drain(), "dve")
            VE(p1_ve3)

    def p1_w_guard(sc):
        w(sc, "pe", tags[f"pe_xn{NC - 1}_{NS - 1}"])
    if w1 and wl[0] < len(wslabs):
        SC(p1_w_guard)
    while w1 and wl[0] < len(wslabs):
        SY(lambda sync: p1_w_loads(sync, 6))
        SC(p1_w_abs)

    def p1f_ve(v):
        if not w1:
            v.memset(wabs[:], 0.02)
        else:
            w(v, "act", tags[f"a_w{len(wslabs) - 1}"])
        for m in range(3):
            v.reduce_sum(t1a[:], wabs[:, 64 * m:64 * (m + 1)], axis=AX.X)
            v.drain()
            v.tensor_copy(wsum3[:, m:m + 1], t1a[:])
        tags["v_wsum"] = inc(v.drain(), "dve")
    VE(p1f_ve)

    def p1f_te(te):
        w(te, "dve", tags["v_wsum"])
        for m in range(3):
            te.matmul(PS[0:1, 3968 + m:3969 + m], wsum3[:, m:m + 1],
                      ones128[:], start=True, stop=True)
        tags["pe_wsum"] = inc(te.drain(), "pe")
    TE(p1f_te)

    def p1f_ve2(v):
        w(v, "pe", tags["pe_wsum"])
        for m in range(3):
            v.tensor_scalar(t16[0:1, 40 + m:41 + m], PS[0:1, 3968 + m:3969 + m],
                            1.0 / (INNER * DM), 1e-5, ALU.mult, ALU.max)
        tags["v_wmean"] = inc(v.drain(), "dve")
    VE(p1f_ve2)

    def p1f_te2(te):
        w(te, "dve", tags["v_wmean"])
        for m in range(3):
            te.matmul(PS[:, 3972 + m:3973 + m], ones1f[:], t16[0:1, 40 + m:41 + m],
                      start=True, stop=True)
        tags["pe_wb"] = inc(te.drain(), "pe")
    TE(p1f_te2)

    def p1f_ve3(v):
        w(v, "pe", tags["pe_wb"])
        for m in range(3):
            v.tensor_copy(mwth[:, m:m + 1], PS[:, 3972 + m:3973 + m])
            v.tensor_scalar(mwth[:, 3 + m:4 + m], PS[:, 3972 + m:3973 + m],
                            0.5, None, ALU.mult)
            v.tensor_scalar(nthv[:, m:m + 1], PS[:, 3972 + m:3973 + m],
                            -0.5, None, ALU.mult)
        tags["v_wth"] = inc(v.drain(), "dve")
    VE(p1f_ve3)

    dbg_specs = []
    if os.environ.get("DBG", "0") == "1":
        for nm, src in (("S0", lambda: S0[:, 0:512]), ("cj16", lambda: cj16[:]),
                        ("DgT", lambda: DgT[:]), ("pn16", lambda: pn16[:]),
                        ("mprev", lambda: mprev16[:]), ("decl", lambda: declast[:]),
                        ("st0", lambda: states[:, 0:512]),
                        ("rkdx", lambda: rkdx[:]), ("Strue", lambda: Strue[:]),
                        ("otm", lambda: o_tm[:]), ("ofm", lambda: o_fm[:]),
                        ("intra", lambda: intra_sb[:]), ("dexpb", lambda: dexpb[:]),
                        ("ndT", lambda: negdecT[:]), ("pnb", lambda: pnb[:]),
                        ("decfm", lambda: dec_fm[:]), ("Lh", lambda: Lh_sb[:]),
                        ("Lraw", lambda: Lraw[:]), ("gatetm", lambda: gate_tm[:]),
                        ("vbtm", lambda: vb_tm[:]), ("xnfm", lambda: xn_fm[:]),
                        ("negeA", lambda: negeA[:]), ("a16d", lambda: a16[:]),
                        ("dtb", lambda: dtb16[:]),
                        ("t16d", lambda: t16[:])):
            dbg_specs.append((nm, src))

    def debug_tail():
        barrier()
        def dt_gp(gp):
            for t in range(NC):
                tags[f"st_dbg{t}"] = dinc(gp.dma_start(
                    out=dy[128 * t:128 * (t + 1), :],
                    in_=stream[:, 1024 * t:1024 * (t + 1)]), "st")
            for nm, src in dbg_specs:
                ap = src()
                d = nc.dram_tensor("dbg_" + nm, list(ap.shape), ap.dtype,
                                   kind="ExternalOutput").ap()
                dinc(gp.dma_start(out=d, in_=ap), "st")
            gp.wait_ge(sem["st"], cnt["st"])
        GP(dt_gp)
        ctx.close()
        return nc

    if phases < 2:
        return debug_tail()

    for n in range(NC):
        def te_v(te, n=n):
            if n == 0:
                w(te, "gd", tags["gd_Ww"])
                w(te, "gd", tags["gd_Wgb"])
                w(te, "dve", tags["v_prev3"])
            else:
                w(te, "dve", tags[f"v_vb{n - 1}"])
                w(te, "dve", tags[f"v_ap{n - 1}"])
                w(te, "act", tags[f"a_gb{n - 1}"])
            for s in range(NS):
                lhs = xn_fm[:, SLABW * s + 1 + 128 * n:SLABW * s + 1 + 128 * (n + 1)]
                te.matmul(PS[:, 0:512], lhs, Wsh[:, 1024 * s:1024 * s + 512],
                          start=(s == 0), stop=(s == NS - 1), skip_group_check=True)
                te.matmul(PS[:, 512:1024], lhs, Wsh[:, 1024 * s + 512:1024 * (s + 1)],
                          start=(s == 0), stop=(s == NS - 1), skip_group_check=True)
            tags[f"pe_v{n}"] = inc(te.drain(), "pe")
            for s in range(NS):
                lhs = xn_fm[:, SLABW * s + 1 + 128 * n:SLABW * s + 1 + 128 * (n + 1)]
                te.matmul(PS[:, 1024:1072], lhs, Wgb[:, 48 * s:48 * (s + 1)],
                          start=(s == 0), stop=(s == NS - 1), skip_group_check=True)
            tags[f"pe_gba{n}"] = inc(te.drain(), "pe")
        TE(te_v)

        def sc_g(sc, n=n):
            w(sc, "pe", tags[f"pe_gba{n}"])
            sc.activation(gate_tm[:, 16 * n:16 * (n + 1)], PS[:, 1024:1040], AF.Sigmoid)
            sc.activation(beta_t[:], PS[:, 1040:1056], AF.Sigmoid)
            tags[f"a_gb{n}"] = inc(sc.drain(), "act")
        SC(sc_g)

        def ve_a(v, n=n):
            w(v, "pe", tags[f"pe_gba{n}"])
            if n > 0:
                w(v, "pe", tags[f"pe_at{n - 1}"])
            v.tensor_copy(apre[:], PS[:, 1056:1072])
            tags[f"v_ap{n}"] = inc(v.drain(), "dve")
        VE(ve_a)

        def te_at(te, n=n):
            w(te, "dve", tags[f"v_ap{n}"])
            if n > 0:
                w(te, "dve", tags[f"v_z{n - 1}"])
            te.transpose(PS[0:16, 1536:1664], apre[:], identf[:])
            tags[f"pe_at{n}"] = inc(te.drain(), "pe")
        TE(te_at)

        def ve_z(v, n=n):
            w(v, "pe", tags[f"pe_at{n}"])
            v.tensor_scalar(t16[:, 0:128], PS[0:16, 1536:1664], dtb16[:, 0:1],
                            None, ALU.add)
            tags[f"v_z{n}"] = inc(v.drain(), "dve")
        VE(ve_z)

        def sc_sp(sc, n=n):
            w(sc, "dve", tags[f"v_z{n}"])
            sc.activation(t16[:, 128:256], t16[0:16, 0:128], AF.Abs)
            sc.drain()
            sc.activation(t16[:, 256:384], t16[0:16, 128:256], AF.Exp, scale=-1.0)
            sc.drain()
            sc.activation(t16[:, 384:512], t16[0:16, 256:384], AF.Ln,
                          bias=consts[0:16, C_ONE:C_ONE + 1])
            tags[f"a_sp{n}"] = inc(sc.drain(), "act")
        SC(sc_sp)

        def ve_dec(v, n=n):
            w(v, "act", tags[f"a_sp{n}"])
            if n == 0:
                w(v, "dve", tags["z_dec"])
            v.tensor_scalar(t16[:, 512:640], t16[0:16, 0:128], 0.0, None, ALU.max)
            v.drain()
            v.tensor_tensor(t16[:, 512:640], t16[0:16, 512:640],
                            t16[0:16, 384:512], ALU.add)
            v.drain()
            v.tensor_scalar(t16[:, 512:640], t16[0:16, 512:640],
                            negeA[:, 0:1], None, ALU.mult)
            v.drain()
            v.tensor_tensor_scan(dec_fm[0:16, CK * n:CK * (n + 1)],
                                 t16[0:16, 512:640], t16[0:16, 512:640],
                                 0.0, ALU.add, ALU.bypass)
            v.drain()
            v.tensor_copy(declast[:, n:n + 1], dec_fm[0:16, CK * n + 127:CK * n + 128])
            tags[f"v_dec{n}"] = inc(v.drain(), "dve")
        VE(ve_dec)

        def sc_dw(sc, n=n):
            w(sc, "dve", tags[f"v_dec{n}"])
            sc.activation(t16[:, 640:641], declast[0:16, n:n + 1], AF.Exp)
            sc.activation(t16[:, 644:772], dec_fm[0:16, CK * n:CK * (n + 1)],
                          AF.Exp, scale=-1.0, bias=declast[0:16, n:n + 1])
            tags[f"a_dw{n}"] = inc(sc.drain(), "act")
        SC(sc_dw)

        def ve_vb(v, n=n):
            w(v, "pe", tags[f"pe_v{n}"])
            w(v, "act", tags[f"a_gb{n}"])
            for h in range(H):
                v.tensor_scalar(vb_tm[:, 1024 * n + 64 * h:1024 * n + 64 * (h + 1)],
                                PS[:, 64 * h:64 * (h + 1)],
                                beta_t[:, h:h + 1], None, ALU.mult)
            tags[f"v_vb{n}"] = inc(v.drain(), "dve")
            w(v, "act", tags[f"a_dw{n}"])
            v.tensor_copy(t16[:, 772:836].bitcast(BF16), t16[0:16, 644:772])
            v.tensor_scalar(R16[:], M8c[:], t16[0:16, 640:641], None, ALU.mult)
            tags[f"v_dw{n}"] = inc(v.drain(), "dve")
        VE(ve_vb)

        def te_dwt(te, n=n):
            w(te, "dve", tags[f"v_dw{n}"])
            if n > 0:
                w(te, "dve", tags[f"v_dwt{n - 1}"])
            te.transpose(psb(1800, 1808)[:, 0:16], t16[0:16, 772:836].bitcast(BF16),
                         identb[0:16, 0:16])
            te.matmul(PS[:, 2560:2568], QC32[:], R16[:], start=True, stop=True,
                      skip_group_check=True)
            tags[f"pe_dwt{n}"] = inc(te.drain(), "pe")
        TE(te_dwt)

        def ve_vbdw(v, n=n):
            w(v, "pe", tags[f"pe_dwt{n}"])
            if n > 0:
                w(v, "pe", tags[f"pe_u{n - 1}_{NS - 1}"])
            v.tensor_copy(dwT[:], psb(1800, 1808)[:, 0:16])
            v.tensor_copy(Earr[:], PS[:, 2560:2568])
            tags[f"v_dwt{n}"] = inc(v.drain(), "dve")
            for h in range(H):
                v.tensor_scalar(vbdw[:, 64 * h:64 * (h + 1)],
                                vb_tm[:, 1024 * n + 64 * h:1024 * n + 64 * (h + 1)],
                                dwT[:, h:h + 1], None, ALU.mult)
            tags[f"v_vbdw{n}"] = inc(v.drain(), "dve")
        VE(ve_vbdw)

        def ve_snap(v, n=n):
            v.tensor_copy(states[:, 512 * n:512 * (n + 1)],
                          S_run[:, 512 * (n % 2):512 * (n % 2) + 512])
            tags[f"v_snap{n}"] = inc(v.drain(), "dve")
        VE(ve_snap)

        for s in range(NS):
            def te_wk(te, n=n, s=s):
                if s == 0:
                    w(te, "dve", tags[f"v_vbdw{n}"])
                if n > 0 and s == 0:
                    w(te, "pe", tags[f"pe_u{n - 1}_{NS - 1}"])
                if s >= 2:
                    w(te, "dve", tags[f"v_wkev{n}_{s - 2}"])
                slot = s % 2
                te.transpose(psb(WKT[slot], WKT[slot] + 64),
                             xn_fm[:, SLABW * s + 128 * n:SLABW * s + 128 * (n + 1)],
                             identb[:])
                tags[f"pe_wk{n}_{s}"] = inc(te.drain(), "pe")
            TE(te_wk)

            def ve_wkev(v, n=n, s=s):
                w(v, "pe", tags[f"pe_wk{n}_{s}"])
                slot = s % 2
                v.tensor_copy(wk_tm[:, 128 * s:128 * (s + 1)],
                              psb(WKT[slot], WKT[slot] + 64))
                tags[f"v_wkev{n}_{s}"] = inc(v.drain(), "dve")
            VE(ve_wkev)

            def te_u(te, n=n, s=s):
                w(te, "dve", tags[f"v_wkev{n}_{s}"])
                if s >= 2:
                    w(te, "dve", tags[f"v_su{n}_{s - 2}"])
                slot = s % 2
                te.matmul(PS[:, 2048 + 128 * slot:2176 + 128 * slot],
                          wk_tm[:, 128 * s:128 * (s + 1)],
                          vbdw[:, 128 * s:128 * (s + 1)],
                          start=True, stop=True, skip_group_check=True)
                tags[f"pe_u{n}_{s}"] = inc(te.drain(), "pe")
            TE(te_u)

            def ve_srun(v, n=n, s=s):
                if s == 0:
                    w(v, "dve", tags[f"v_snap{n}"])
                w(v, "pe", tags[f"pe_u{n}_{s}"])
                slot = s % 2
                bc = 2048 + 128 * slot
                rd, wr = 512 * (n % 2), 512 * ((n + 1) % 2)
                v.scalar_tensor_tensor(S_run[0:64, wr + 64 * s:wr + 64 * (s + 1)],
                                       S_run[0:64, rd + 64 * s:rd + 64 * (s + 1)],
                                       Earr[0:64, s:s + 1],
                                       PS[0:64, bc:bc + 64], ALU.mult, ALU.add)
                v.scalar_tensor_tensor(S_run[64:128, wr + 64 * s:wr + 64 * (s + 1)],
                                       S_run[64:128, rd + 64 * s:rd + 64 * (s + 1)],
                                       Earr[64:128, s:s + 1],
                                       PS[64:128, bc + 64:bc + 128], ALU.mult, ALU.add)
                tags[f"v_su{n}_{s}"] = inc(v.drain(), "dve")
            VE(ve_srun)

    barrier()

    def p3_ve(v):
        v.reduce_sum(t16[:, 900:901], declast[0:16, 0:NC], axis=AX.X)
        v.tensor_tensor_scan(t16[:, 904:904 + NC], declast[0:16, 0:NC],
                             declast[0:16, 0:NC], 0.0, ALU.add, ALU.bypass)
        v.drain()
        v.tensor_tensor(t16[:, 960:960 + NC], t16[0:16, 904:904 + NC],
                        declast[0:16, 0:NC], ALU.subtract)
        tags["v_p3a"] = inc(v.drain(), "dve")
    VE(p3_ve)

    def p3_sc(sc):
        w(sc, "dve", tags["v_p3a"])
        sc.activation(t16[:, 940:941], t16[0:16, 900:901], AF.Exp)
        sc.activation(pn16[:, 0:NC], t16[0:16, 960:960 + NC], AF.Exp)
        tags["a_p3"] = inc(sc.drain(), "act")
    SC(p3_sc)

    def p3_gp(gp):
        w(gp, "act", tags["a_p3"])
        tags["gd_g1"] = dinc(gp.dma_start(
            out=gin.ap()[0:1, 0:65536].rearrange("a (p c) -> (a p) c", p=128),
            in_=S_run[:, 512 * (NC % 2):512 * (NC % 2) + 512]), "gd")
        tags["gd_g2"] = dinc(gp.dma_start(
            out=gin.ap()[0:1, 65536:65552].rearrange("a (p b) -> (a p) b", p=16),
            in_=t16[0:16, 940:941]), "gd")
        gp.wait_ge(sem["gd"], tags["gd_g2"])
        tags["cc"] = inc(gp.collective_compute(
            "AllGather", ALU.bypass, replica_groups=[list(range(8))],
            ins=[gin.ap()], outs=[gout.ap()]), "cc")
    GP(p3_gp)

    for k in range(len(wslabs)):
        def p3w_sy(sync, k=k):
            slot = k % 2
            w(sync, "gd", tags.get(f"gd_t{k - 2}"))
            tags[f"ld2_w{k}"] = dinc(sync.dma_start(
                out=stag[:, 512 * slot:512 * (slot + 1)], in_=wslabs[k][0]), "ld")
        SY(p3w_sy)

        def p3w_ve(v, k=k):
            w(v, "ld", tags[f"ld2_w{k}"])
            if k == 0:
                w(v, "dve", tags["v_wth"])
            w(v, "gd", tags.get(f"gd_t{k - 2}"))
            slot = k % 2
            mi = wslabs[k][2]
            v.tensor_scalar(xn_tmp[:, 0:512], stag[:, 512 * slot:512 * (slot + 1)],
                            nthv[:, mi:mi + 1], None, ALU.is_lt)
            v.scalar_tensor_tensor(ternst[:, 512 * slot:512 * (slot + 1)],
                                   stag[:, 512 * slot:512 * (slot + 1)],
                                   mwth[:, 3 + mi:4 + mi], xn_tmp[:, 0:512],
                                   ALU.is_gt, ALU.subtract)
            tags[f"v_t{k}"] = inc(v.drain(), "dve")
        VE(p3w_ve)

        def p3w_gp(gp, k=k):
            w(gp, "dve", tags[f"v_t{k}"])
            slot = k % 2
            tags[f"gd_t{k}"] = dinc(gp.dma_start(
                out=scr_dst(k), in_=ternst[:, 512 * slot:512 * (slot + 1)]), "gd")
        GP(p3w_gp)

    def p3c_sy(sync):
        w(sync, "dve", tags[f"v_t{len(wslabs) - 1}"])
        w(sync, "gd", tags[f"gd_t{len(wslabs) - 1}"])
        tags["ld_SELH"] = dinc(sync.dma_start(out=SELH[:], in_=dSELH), "ldc")
        w(sync, "gd", tags["gd_g1"])
        tags["ld_SEL16"] = dinc(sync.dma_start(out=SEL16[:], in_=dSEL16), "ldc")
        tags["ld_mask"] = dinc(sync.dma_start(out=maskst[:], in_=dmask), "ldc")
    SY(p3c_sy)
    LDC_ALL = cnt["ldc"]

    def p3r_gp(gp):
        w(gp, "cc", tags["cc"])
        for j in range(8):
            tags[f"gd_Dg{j}"] = dinc(gp.dma_start(
                out=DgT[:, j:j + 1],
                in_=gout.ap()[j:j + 1, 65536:65552].rearrange(
                    "a (p b) -> (a p) b", p=16)), "gd")
    GP(p3r_gp)

    def p3r_ve(v):
        w(v, "gd", tags["gd_Dg7"])
        v.tensor_scalar(G16[:], DgT[:], 1.0, None, ALU.subtract)
        v.drain()
        v.tensor_tensor(G16[:], G16[:], mprev16[:], ALU.mult)
        v.drain()
        v.tensor_scalar(G16[:], G16[:], 1.0, None, ALU.add)
        v.memset(P16[:, 7:8], 1.0)
        v.drain()
        for j in range(6, -1, -1):
            v.tensor_tensor(P16[:, j:j + 1], P16[:, j + 1:j + 2],
                            G16[:, j + 1:j + 2], ALU.mult)
            v.drain()
        v.tensor_tensor(cj16[:], P16[:], mprev16[:], ALU.mult)
        tags["v_cj"] = inc(v.drain(), "dve")
    VE(p3r_ve)

    for j in range(8):
        def p3s_gp(gp, j=j):
            if j > 0:
                w(gp, "dve", tags[f"v_s0_{j - 1}"])
            tags[f"gd_Sg{j}"] = dinc(gp.dma_start(
                out=Sg[:], in_=gout.ap()[j:j + 1, 0:65536].rearrange(
                    "a (p c) -> (a p) c", p=128)), "gd")
        GP(p3s_gp)

        def p3s_ve(v, j=j):
            if j == 0:
                w(v, "dve", tags["v_cj"])
            else:
                w(v, "pe", tags[f"pe_cb{j - 1}"])
            v.tensor_scalar(R16[:], M8c[:], cj16[:, j:j + 1], None, ALU.mult)
            tags[f"v_r{j}"] = inc(v.drain(), "dve")
        VE(p3s_ve)

        def p3s_te(te, j=j):
            w(te, "dve", tags[f"v_r{j}"])
            te.matmul(PS[:, 2560:2568], QC32[:], R16[:], start=True, stop=True,
                      skip_group_check=True)
            tags[f"pe_cb{j}"] = inc(te.drain(), "pe")
        TE(p3s_te)

        def p3s_ve2(v, j=j):
            w(v, "pe", tags[f"pe_cb{j}"])
            v.tensor_copy(cbj[:], PS[:, 2560:2568])
            w(v, "gd", tags[f"gd_Sg{j}"])
            v.drain()
            rd, wr = 512 * (j % 2), 512 * ((j + 1) % 2)
            for p in range(8):
                v.scalar_tensor_tensor(S0[:, wr + 64 * p:wr + 64 * (p + 1)],
                                       Sg[:, 64 * p:64 * (p + 1)],
                                       cbj[:, p:p + 1],
                                       S0[:, rd + 64 * p:rd + 64 * (p + 1)],
                                       ALU.mult, ALU.add)
            tags[f"v_s0_{j}"] = inc(v.drain(), "dve")
        VE(p3s_ve2)

    barrier()
    if phases < 4:
        return debug_tail()

    def p4_wo(gp):
        tags["gd_Wo"] = dinc(gp.dma_start(
            out=Wsh[:].rearrange("p (s n) -> p s n", s=NS),
            in_=dWo.rearrange("(s p) n -> p s n", p=128)), "gd")
    GP(p4_wo)

    for n in range(NC):
        def ve_pn(v, n=n):
            if not G("GPN"):
                tags[f"v_pn{n}"] = cnt["dve"]
                tags[f"pe_dt{n}"] = cnt["pe"]
                tags[f"v_ndt{n}"] = cnt["dve"]
                tags[f"v_str{n}"] = inc(v.drain(), "dve")
                return
            if n > 0:
                w(v, "pe", tags[f"pe_dt{n - 1}"])
            v.tensor_scalar(R16[:], M8c[:], pn16[:, n:n + 1], None, ALU.mult)
            tags[f"v_pn{n}"] = inc(v.drain(), "dve")
        VE(ve_pn)

        def te_dt(te, n=n):
            if not G("GPN"):
                return
            w(te, "dve", tags[f"v_pn{n}"])
            if n > 0:
                w(te, "dve", tags[f"v_ndt{n - 1}"])
            if G("GDT"):
                te.transpose(PS[:, 384:512], dec_fm[:, CK * n:CK * (n + 1)],
                             identf[:])
                te.matmul(PS[:, 640:648], QC32[:], R16[:], start=True, stop=True,
                          skip_group_check=True)
            tags[f"pe_dt{n}"] = inc(te.drain(), "pe")
        TE(te_dt)

        def ve_b(v, n=n):
            if not G("GPN"):
                return
            w(v, "pe", tags[f"pe_dt{n}"])
            if n >= 1:
                w(v, "pe", tags[f"pe_o{n - 1}_{H - 1}"])
            v.tensor_scalar(negdecT[:], PS[:, 384:400], -1.0, None, ALU.mult)
            v.tensor_copy(pnb[:], PS[:, 640:648])
            tags[f"v_ndt{n}"] = inc(v.drain(), "dve")
            if G("GST"):
                for p in range(8):
                    so = 64 * p
                    v.scalar_tensor_tensor(Strue[:, so:so + 64],
                                           S0[:, 64 * p:64 * (p + 1)],
                                           pnb[:, p:p + 1],
                                           states[:, 512 * n + 64 * p:512 * n + 64 * (p + 1)],
                                           ALU.mult, ALU.add)
            tags[f"v_str{n}"] = inc(v.drain(), "dve")
        VE(ve_b)

        def sc_de(sc, n=n):
            if not G("GDE"):
                tags[f"a_de{n}"] = cnt["act"]
                return
            w(sc, "dve", tags[f"v_ndt{n}"])
            sc.activation(t16[:, 644:772], dec_fm[0:16, CK * n:CK * (n + 1)], AF.Exp)
            tags[f"a_de{n}"] = inc(sc.drain(), "act")
        SC(sc_de)

        def ve_deb(v, n=n):
            w(v, "act", tags[f"a_de{n}"])
            v.tensor_copy(dexpb[0:16, :], t16[0:16, 644:772])
            tags[f"v_deb{n}"] = inc(v.drain(), "dve")
        VE(ve_deb)

        for s in range(NS):
            def te_rk(te, n=n, s=s):
                if s == 0:
                    w(te, "dve", tags[f"v_deb{n}"])
                    if n == 0:
                        w(te, "ldc", LDC_ALL)
                if s >= 2:
                    w(te, "dve", tags[f"v_rk{n}_{s - 2}"])
                slot = s % 2
                if G("GRKM"):
                    rkc = 256 + 1024 * slot
                    te.matmul(PS[:, rkc:rkc + 128],
                              SEL16[:, 128 * s:128 * (s + 1)], dexpb[:],
                              start=True, stop=True, skip_group_check=True)
                tags[f"pe_rk{n}_{s}"] = inc(te.drain(), "pe")
            TE(te_rk)

            def ve_rk(v, n=n, s=s):
                if s == 0 and n > 0:
                    w(v, "pe", tags[f"pe_o{n - 1}_{H - 1}"])
                w(v, "pe", tags[f"pe_rk{n}_{s}"])
                slot = s % 2
                rkc = 256 + 1024 * slot
                if G("GRKE"):
                    if G("GRKC"):
                        v.tensor_tensor(rkdx[:, 128 * s:128 * (s + 1)],
                                        xn_fm[:, SLABW * s + 1 + 128 * n:SLABW * s + 1 + 128 * (n + 1)],
                                        PS[:, rkc:rkc + 128], ALU.mult)
                    else:
                        v.tensor_copy(rkdx[:, 128 * s:128 * (s + 1)],
                                      PS[:, rkc:rkc + 128])
                tags[f"v_rk{n}_{s}"] = inc(v.drain(), "dve")
            VE(ve_rk)

        ordq = [hh for hh in range(H) if hh % 2 == 0] + \
               [hh for hh in range(H) if hh % 2 == 1]
        if p4sub < 2:
            def ve_skipA(v, n=n):
                v.memset(intra_sb[:], 0.0)
                for hh in range(H):
                    tags[f"v_iev{n}_{hh}"] = inc(v.drain(), "dve")
                    tags[f"a_L{n}_{hh}"] = cnt["act"]
                    tags[f"v_La{n}_{hh}"] = tags[f"v_iev{n}_{hh}"]
                    if p4sub < 3:
                        tags[f"pe_o{n}_{hh}"] = cnt["pe"]
            VE(ve_skipA)
        for q, hh in enumerate(ordq):
            if p4sub < 2:
                break
            par = hh % 2
            sl = hh // 2
            r0 = 64 * par
            islot = q % 4
            lslot = q % 2

            def te_hd(te, n=n, q=q, hh=hh, par=par, sl=sl, r0=r0,
                      islot=islot, lslot=lslot):
                if q == 0:
                    w(te, "dve", tags[f"v_rk{n}_{NS - 1}"])
                    w(te, "dve", tags[f"v_str{n}"])
                    if n == 0:
                        w(te, "ldc", LDC_ALL)
                    if n > 0:
                        w(te, "dve", tags[f"v_oev{n - 1}"])
                pL = (tags.get(f"v_La{n}_{ordq[q - 2]}") if q >= 2
                      else (tags.get(f"v_La{n - 1}_{ordq[q + 14]}") if n > 0 else None))
                w(te, "act", None)
                w(te, "dve", pL)
                pI = (tags.get(f"v_iev{n}_{ordq[q - 2]}") if q >= 2
                      else (tags.get(f"v_iev{n - 1}_{ordq[q + 14]}") if n > 0 else None))
                w(te, "dve", pI)
                lc = 512 + 1024 * lslot
                te.matmul(PS[:, lc:lc + 128],
                          SELH[:, 128 * hh:128 * (hh + 1)],
                          dec_fm[:, CK * n:CK * (n + 1)],
                          start=True, stop=True, skip_group_check=True)
                tags[f"pe_L{n}_{hh}"] = inc(te.drain(), "pe")
                ic = 1024 * (islot % 2) + 128 * (islot // 2)
                te.matmul(PS[:, ic:ic + 128],
                          xn_fm[r0:r0 + 64, SLABW * sl + 128 * n:SLABW * sl + 128 * (n + 1)],
                          xn_fm[r0:r0 + 64, SLABW * sl + 1 + 128 * n:SLABW * sl + 1 + 128 * (n + 1)],
                          start=True, stop=True, skip_group_check=True)
                tags[f"pe_i{n}_{hh}"] = inc(te.drain(), "pe")
            TE(te_hd)

            def ve_La(v, n=n, q=q, hh=hh, lslot=lslot):
                w(v, "pe", tags[f"pe_L{n}_{hh}"])
                if n == 0 and q == 0:
                    w(v, "ldc", LDC_ALL)
                pS = (tags.get(f"a_L{n}_{ordq[q - 2]}") if q >= 2
                      else (tags.get(f"a_L{n - 1}_{ordq[q + 14]}") if n > 0 else None))
                w(v, "act", pS)
                lc = 512 + 1024 * lslot
                v.tensor_tensor(Lraw[:, 128 * lslot:128 * (lslot + 1)],
                                PS[:, lc:lc + 128],
                                maskst[:], ALU.add)
                tags[f"v_La{n}_{hh}"] = inc(v.drain(), "dve")
            VE(ve_La)

            def sc_hd(sc, n=n, q=q, hh=hh, lslot=lslot):
                w(sc, "dve", tags[f"v_La{n}_{hh}"])
                pS = (tags.get(f"v_iev{n}_{ordq[q - 2]}") if q >= 2
                      else (tags.get(f"v_iev{n - 1}_{ordq[q + 14]}") if n > 0 else None))
                w(sc, "dve", pS)
                sc.activation(Lh_sb[:, 128 * lslot:128 * (lslot + 1)],
                              Lraw[:, 128 * lslot:128 * (lslot + 1)],
                              AF.Exp, bias=negdecT[:, hh:hh + 1])
                tags[f"a_L{n}_{hh}"] = inc(sc.drain(), "act")
            SC(sc_hd)

            def ve_hd(v, n=n, hh=hh, islot=islot, lslot=lslot):
                w(v, "pe", tags[f"pe_i{n}_{hh}"])
                w(v, "act", tags[f"a_L{n}_{hh}"])
                if n > 0:
                    w(v, "pe", tags[f"pe_o{n - 1}_{hh}"])
                ic = 1024 * (islot % 2) + 128 * (islot // 2)
                v.tensor_tensor(intra_sb[:, 128 * hh:128 * (hh + 1)],
                                PS[:, ic:ic + 128],
                                Lh_sb[:, 128 * lslot:128 * (lslot + 1)], ALU.mult)
                tags[f"v_iev{n}_{hh}"] = inc(v.drain(), "dve")
            VE(ve_hd)

            def te_o1(te, n=n, hh=hh, par=par, sl=sl, r0=r0):
                if p4sub < 3:
                    tags[f"pe_o{n}_{hh}"] = cnt["pe"]
                    return
                w(te, "dve", tags[f"v_iev{n}_{hh}"])
                te.matmul(PS[:, 2048 + 64 * hh:2048 + 64 * (hh + 1)],
                          intra_sb[:, 128 * hh:128 * (hh + 1)],
                          vb_tm[:, 1024 * n + 64 * hh:1024 * n + 64 * (hh + 1)],
                          start=True, stop=False, skip_group_check=True)
                te.matmul(PS[:, 2048 + 64 * hh:2048 + 64 * (hh + 1)],
                          rkdx[r0:r0 + 64, 128 * sl:128 * (sl + 1)],
                          Strue[r0:r0 + 64, 64 * sl:64 * (sl + 1)],
                          start=False, stop=True, skip_group_check=True)
                tags[f"pe_o{n}_{hh}"] = inc(te.drain(), "pe")
            TE(te_o1)

        def ve_oev(v, n=n):
            if n > 0:
                w(v, "pe", tags[f"pe_ot{n - 1}_{NS - 1}"])
            if p4sub >= 3:
                w(v, "pe", tags[f"pe_o{n}_{ordq[-1]}"])
            for hh in range(H):
                if p4sub >= 3:
                    v.tensor_scalar(o_tm[:, 64 * hh:64 * (hh + 1)],
                                    PS[:, 2048 + 64 * hh:2048 + 64 * (hh + 1)],
                                    gate_tm[:, 16 * n + hh:16 * n + hh + 1], None, ALU.mult)
                else:
                    v.memset(o_tm[:, 64 * hh:64 * (hh + 1)], 0.0)
            tags[f"v_oev{n}"] = inc(v.drain(), "dve")
        VE(ve_oev)

        for s in range(NS):
            if p4sub < 4:
                def ve_skipC(v, n=n, s=s):
                    w(v, "dve", tags[f"v_oev{n}"])
                    tags[f"pe_ot{n}_{s}"] = cnt["pe"]
                    tags[f"v_ot{n}_{s}"] = inc(v.drain(), "dve")
                VE(ve_skipC)
                continue
            def te_ot(te, n=n, s=s):
                if s == 0:
                    w(te, "dve", tags[f"v_oev{n}"])
                    if n == 0:
                        w(te, "gd", tags["gd_Wo"])
                if s >= 2:
                    w(te, "dve", tags[f"v_ot{n}_{s - 2}"])
                slot = s % 2
                te.transpose(psb(OTT[slot], OTT[slot] + 64),
                             o_tm[:, 128 * s:128 * (s + 1)], identb[:])
                tags[f"pe_ot{n}_{s}"] = inc(te.drain(), "pe")
            TE(te_ot)

            def ve_ot(v, n=n, s=s):
                if s == 0 and n > 0:
                    w(v, "pe", tags[f"pe_wo{n - 1}"])
                w(v, "pe", tags[f"pe_ot{n}_{s}"])
                slot = s % 2
                v.tensor_copy(o_fm[:, 128 * s:128 * (s + 1)],
                              psb(OTT[slot], OTT[slot] + 64))
                tags[f"v_ot{n}_{s}"] = inc(v.drain(), "dve")
            VE(ve_ot)

        def te_wo(te, n=n):
            if p4sub < 4:
                tags[f"pe_wo{n}"] = cnt["pe"]
                return
            if n > 0:
                w(te, "dve", tags[f"v_x1_{n - 1}"])
            for s in range(NS):
                w(te, "dve", tags[f"v_ot{n}_{s}"])
                te.matmul(PS[:, 3072:3584], o_fm[:, 128 * s:128 * (s + 1)],
                          Wsh[:, 1024 * s:1024 * s + 512],
                          start=(s == 0), stop=(s == NS - 1), skip_group_check=True)
                te.matmul(PS[:, 3584:4096], o_fm[:, 128 * s:128 * (s + 1)],
                          Wsh[:, 1024 * s + 512:1024 * (s + 1)],
                          start=(s == 0), stop=(s == NS - 1), skip_group_check=True)
            tags[f"pe_wo{n}"] = inc(te.drain(), "pe")
        TE(te_wo)

        def ve_x1(v, n=n):
            if p4sub >= 4:
                w(v, "pe", tags[f"pe_wo{n}"])
            v.drain()
            if p4sub >= 4:
                v.tensor_tensor(stream[:, 1024 * n:1024 * (n + 1)],
                                stream[:, 1024 * n:1024 * (n + 1)],
                                PS[:, 3072:4096], ALU.add)
            tags[f"v_x1_{n}"] = inc(v.drain(), "dve")
        VE(ve_x1)

    barrier()
    if phases < 5:
        return debug_tail()

    STT_ = 4
    n_st = (NC + STT_ - 1) // STT_

    for st in range(n_st):
        tiles = list(range(STT_ * st, min(STT_ * (st + 1), NC)))

        for i, t in enumerate(tiles):
            def ln2_ve(v, t=t, i=i, st=st):
                if i == 0 and st > 0:
                    w(v, "dve", tags[f"v_y{STT_ * st - 1}"])
                if t > 0:
                    w(v, "dve", tags[f"v_q1{t - 1}"])
                v.reduce_sum(t1a[:], stream[:, 1024 * t:1024 * (t + 1)], axis=AX.X)
                v.drain()
                v.tensor_scalar(t1c[:], t1a[:], 1.0 / DM, None, ALU.mult)
                v.drain()
                v.tensor_scalar(xc[:], stream[:, 1024 * t:1024 * (t + 1)],
                                t1c[:], None, ALU.subtract)
                v.reduce_max(q5a[:], xc[:], axis=AX.X, apply_absolute_value=True)
                tags[f"v_l2a{t}"] = inc(v.drain(), "dve")
            VE(ln2_ve)

            def ln2_sc(sc, t=t):
                w(sc, "dve", tags[f"v_l2a{t}"])
                if t > 0:
                    w(sc, "dve", tags[f"v_q1{t - 1}"])
                    w(sc, "pe", tags[f"pe_q1{t - 1}_{NS - 1}"])
                sc.activation(f_bf[:], xc[:], AF.Square, accum_out=t1b[:])
                sc.activation(f_bf[:], xc[:], AF.Sign)
                tags[f"a_l2{t}"] = inc(sc.drain(), "act")
            SC(ln2_sc)

            def ln2_ve2(v, t=t):
                w(v, "act", tags[f"a_l2{t}"])
                v.tensor_scalar(t1d[:], t1b[:], 1.0 / DM, None, ALU.mult)
                tags[f"v_l2b{t}"] = inc(v.drain(), "dve")
            VE(ln2_ve2)

            def ln2_sc2(sc, t=t):
                w(sc, "dve", tags[f"v_l2b{t}"])
                sc.activation(t1a[:], t1d[:], AF.Ln, bias=consts[:, C_EPS:C_EPS + 1])
                sc.drain()
                sc.activation(t1d[:], t1a[:], AF.Exp, scale=-0.5)
                tags[f"a_l2b{t}"] = inc(sc.drain(), "act")
            SC(ln2_sc2)

            def ln2_ve3(v, t=t):
                w(v, "act", tags[f"a_l2b{t}"])
                v.tensor_tensor(q5b[:], q5a[:], t1d[:], ALU.mult)
                v.drain()
                v.tensor_scalar(q5b[:], q5b[:], 1e-5, None, ALU.max)
                v.drain()
                v.reciprocal(q5c[:], q5b[:])
                v.drain()
                v.tensor_tensor(q5d[:], q5c[:], t1d[:], ALU.mult)
                v.drain()
                v.tensor_scalar(q5d[:], q5d[:], 127.0, None, ALU.mult)
                v.scalar_tensor_tensor(cgv[:, t:t + 1], q5b[:], 1.0 / 127.0,
                                       mwth[:, 0:1], ALU.mult, ALU.mult)
                v.scalar_tensor_tensor(cuv[:, t:t + 1], q5b[:], 1.0 / 127.0,
                                       mwth[:, 1:2], ALU.mult, ALU.mult)
                tags[f"v_l2c{t}"] = inc(v.drain(), "dve")
            VE(ln2_ve3)

            def q1_sc(sc, t=t):
                w(sc, "dve", tags[f"v_l2c{t}"])
                sc.activation(f_tmp[:], xc[:], AF.Copy, scale=q5d[:, 0:1],
                              bias=MAGIC)
                sc.drain()
                sc.activation(xc[:], f_tmp[:], AF.Copy, bias=-MAGIC)
                tags[f"a_q1{t}"] = inc(sc.drain(), "act")
            SC(q1_sc)

            def q1_ve(v, t=t):
                w(v, "act", tags[f"a_q1{t}"])
                v.scalar_tensor_tensor(f_tmp[:], f_bf[:], DELTA, xc[:],
                                       ALU.mult, ALU.add)
                v.tensor_copy(xc[:].bitcast(I32), f_tmp[:])
                v.tensor_copy(f_bf[:], xc[:].bitcast(I32))
                tags[f"v_q1{t}"] = inc(v.drain(), "dve")
            VE(q1_ve)

            for s in range(NS):
                def q1_te(te, t=t, i=i, st=st, s=s):
                    gi = t * NS + s
                    if s == 0:
                        w(te, "dve", tags[f"v_q1{t}"])
                        if i == 0 and st > 0:
                            w(te, "dve", tags[f"v_y{STT_ * st - 1}"])
                    if s >= 2:
                        w(te, "dve", tags[f"v_q1e{t}_{s - 2}"])
                    slot = gi % 2
                    te.transpose(psb(QT[slot], QT[slot] + 64),
                                 f_bf[:, 128 * s:128 * (s + 1)], identb[:])
                    tags[f"pe_q1{t}_{s}"] = inc(te.drain(), "pe")
                TE(q1_te)

                def q1_ve2(v, t=t, i=i, s=s):
                    gi = t * NS + s
                    w(v, "pe", tags[f"pe_q1{t}_{s}"])
                    slot = gi % 2
                    v.tensor_copy(q1_fm[:, 512 * s + 128 * i:512 * s + 128 * (i + 1)],
                                  psb(QT[slot], QT[slot] + 64))
                    tags[f"v_q1e{t}_{s}"] = inc(v.drain(), "dve")
                VE(q1_ve2)

        for m8 in range(8):
            def gu_sy(sync, m8=m8, st=st):
                if m8 > 0:
                    w(sync, "pe", tags[f"pe_gu{st}_{m8 - 1}"])
                elif st > 0:
                    w(sync, "pe", tags[f"pe_dn{st - 1}_7"])
                w(sync, "gd", tags[f"gd_t{len(wslabs) - 1}"])
                tags[f"ld_tg{st}_{m8}"] = dinc(sync.dma_start(
                    out=tg_sb[:].rearrange("p (s n) -> p s n", s=NS),
                    in_=scr_g.ap()[:, 512 * m8:512 * (m8 + 1)].rearrange(
                        "(s p) n -> p s n", p=128)), "ld")
                tags[f"ld_tu{st}_{m8}"] = dinc(sync.dma_start(
                    out=tu_sb[:].rearrange("p (s n) -> p s n", s=NS),
                    in_=scr_u.ap()[:, 512 * m8:512 * (m8 + 1)].rearrange(
                        "(s p) n -> p s n", p=128)), "ld")
            SY(gu_sy)

            for i, t in enumerate(tiles):
                def gu_te(te, m8=m8, i=i, t=t, st=st, tiles=tiles):
                    if m8 == 0 and i == 0:
                        for t2 in tiles:
                            w(te, "dve", tags[f"v_q1e{t2}_{NS - 1}"])
                        if st > 0:
                            w(te, "dve", tags[f"v_y{STT_ * st - 1}"])
                    w(te, "ld", tags[f"ld_tu{st}_{m8}"])
                    if i >= 2:
                        w(te, "dve", tags.get(f"v_h{st}_{m8}_{tiles[i - 2]}"))
                    elif m8 > 0:
                        pi = tiles[i + 2] if i + 2 < len(tiles) else tiles[-1]
                        w(te, "dve", tags.get(f"v_h{st}_{m8 - 1}_{pi}"))
                    ga = 1024 * (i % 2)
                    gb = ga + 512
                    for s in range(NS):
                        lhs = q1_fm[:, 512 * s + 128 * i:512 * s + 128 * (i + 1)]
                        te.matmul(PS[:, ga:ga + 512], lhs,
                                  tg_sb[:, 512 * s:512 * (s + 1)],
                                  start=(s == 0), stop=(s == NS - 1),
                                  skip_group_check=True)
                    for s in range(NS):
                        lhs = q1_fm[:, 512 * s + 128 * i:512 * s + 128 * (i + 1)]
                        te.matmul(PS[:, gb:gb + 512], lhs,
                                  tu_sb[:, 512 * s:512 * (s + 1)],
                                  start=(s == 0), stop=(s == NS - 1),
                                  skip_group_check=True)
                    tags[f"pe_g{st}_{m8}_{t}"] = inc(te.drain(), "pe")
                    if i == len(tiles) - 1:
                        tags[f"pe_gu{st}_{m8}"] = tags[f"pe_g{st}_{m8}_{t}"]
                TE(gu_te)

                def gu_sc(sc, m8=m8, i=i, t=t, st=st, tiles=tiles):
                    w(sc, "pe", tags[f"pe_g{st}_{m8}_{t}"])
                    if i >= 2:
                        w(sc, "dve", tags.get(f"v_h{st}_{m8}_{tiles[i - 2]}"))
                    elif m8 > 0:
                        pi = tiles[i + 2] if i + 2 < len(tiles) else tiles[-1]
                        w(sc, "dve", tags.get(f"v_h{st}_{m8 - 1}_{pi}"))
                    ga = 1024 * (i % 2)
                    sc.activation(xc[:, 512 * (i % 2):512 * (i % 2) + 512],
                                  PS[:, ga:ga + 512], AF.Silu, scale=cgv[:, t:t + 1])
                    tags[f"a_h{st}_{m8}_{t}"] = inc(sc.drain(), "act")
                SC(gu_sc)

                def gu_ve(v, m8=m8, i=i, t=t, st=st):
                    w(v, "act", tags[f"a_h{st}_{m8}_{t}"])
                    gb = 1024 * (i % 2) + 512
                    v.scalar_tensor_tensor(
                        h_arena[:, INNER * i + 512 * m8:INNER * i + 512 * (m8 + 1)],
                        PS[:, gb:gb + 512], cuv[:, t:t + 1],
                        xc[:, 512 * (i % 2):512 * (i % 2) + 512], ALU.mult, ALU.mult)
                    tags[f"v_h{st}_{m8}_{t}"] = inc(v.drain(), "dve")
                VE(gu_ve)

        for i, t in enumerate(tiles):
            def hq_ve(v, i=i, t=t, st=st):
                w(v, "dve", tags[f"v_h{st}_7_{t}"])
                if t > 0:
                    w(v, "act", tags.get(f"a_hz{t - 1}_3"))
                v.reduce_max(q5a[:], h_arena[:, INNER * i:INNER * (i + 1)],
                             axis=AX.X, apply_absolute_value=True)
                v.drain()
                v.tensor_scalar(q5b[:], q5a[:], 1e-5, None, ALU.max)
                v.drain()
                v.reciprocal(q5c[:], q5b[:])
                v.drain()
                v.tensor_scalar(q5d[:], q5c[:], 127.0, None, ALU.mult)
                v.scalar_tensor_tensor(cdv[:, t:t + 1], q5b[:], 1.0 / 127.0,
                                       mwth[:, 2:3], ALU.mult, ALU.mult)
                tags[f"v_hq{t}"] = inc(v.drain(), "dve")
            VE(hq_ve)

            for qr in range(4):
                def hqq_sc(sc, i=i, t=t, qr=qr):
                    hsl = h_arena[:, INNER * i + 1024 * qr:INNER * i + 1024 * (qr + 1)]
                    if qr == 0:
                        w(sc, "dve", tags[f"v_hq{t}"])
                    else:
                        w(sc, "dve", tags[f"v_hqq{t}_{qr - 1}"])
                    sc.activation(f_bf[:], hsl, AF.Sign)
                    sc.activation(f_tmp[:], hsl, AF.Copy, scale=q5d[:, 0:1],
                                  bias=MAGIC)
                    sc.drain()
                    sc.activation(xc[:], f_tmp[:], AF.Copy, bias=-MAGIC)
                    tags[f"a_hz{t}_{qr}"] = inc(sc.drain(), "act")
                SC(hqq_sc)

                def hqq_ve(v, i=i, t=t, qr=qr):
                    w(v, "act", tags[f"a_hz{t}_{qr}"])
                    if qr == 0 and t > 0:
                        w(v, "pe", tags.get(f"pe_hf{t - 1}_31"))
                    v.scalar_tensor_tensor(f_tmp[:], f_bf[:], DELTA, xc[:],
                                           ALU.mult, ALU.add)
                    v.tensor_copy(xc[:].bitcast(I32), f_tmp[:])
                    v.tensor_copy(hq_tm[:, 1024 * qr:1024 * (qr + 1)],
                                  xc[:].bitcast(I32))
                    tags[f"v_hqq{t}_{qr}"] = inc(v.drain(), "dve")
                VE(hqq_ve)

            for s2 in range(32):
                def hq_te(te, i=i, t=t, s2=s2):
                    if s2 == 0:
                        w(te, "dve", tags[f"v_hqq{t}_3"])
                    if s2 >= 2:
                        w(te, "dve", tags[f"v_hf{t}_{s2 - 2}"])
                    slot = s2 % 2
                    te.transpose(psb(QT[slot], QT[slot] + 64),
                                 hq_tm[:, 128 * s2:128 * (s2 + 1)], identb[:])
                    tags[f"pe_hf{t}_{s2}"] = inc(te.drain(), "pe")
                TE(hq_te)

                def hq_ve3(v, i=i, t=t, s2=s2):
                    hq_fm = h_arena[:, INNER * i:INNER * i + 2048].bitcast(BF16)
                    w(v, "pe", tags[f"pe_hf{t}_{s2}"])
                    slot = s2 % 2
                    v.tensor_copy(hq_fm[:, 128 * s2:128 * (s2 + 1)],
                                  psb(QT[slot], QT[slot] + 64))
                    tags[f"v_hf{t}_{s2}"] = inc(v.drain(), "dve")
                VE(hq_ve3)

        for m8 in range(8):
            def dn_sy(sync, m8=m8, st=st):
                if m8 > 0:
                    w(sync, "pe", tags[f"pe_dn{st}_{m8 - 1}"])
                else:
                    w(sync, "pe", tags[f"pe_gu{st}_7"])
                tags[f"ld_td{st}_{m8}"] = dinc(sync.dma_start(
                    out=td_sb[:].rearrange("p (s n) -> p s n", s=4),
                    in_=scr_d.ap()[512 * m8:512 * (m8 + 1), :].rearrange(
                        "(s p) n -> p s n", p=128)), "ld")
            SY(dn_sy)

            def dn_te(te, m8=m8, st=st, tiles=tiles):
                w(te, "ld", tags[f"ld_td{st}_{m8}"])
                if m8 == 0:
                    for t2 in tiles:
                        w(te, "dve", tags[f"v_hf{t2}_31"])
                        w(te, "dve", tags[f"v_h{st}_7_{t2}"])
                for i2, t2 in enumerate(tiles):
                    hq_fm = h_arena[:, INNER * i2:INNER * i2 + 2048].bitcast(BF16)
                    for o2 in range(2):
                        for s4 in range(4):
                            s2 = 4 * m8 + s4
                            te.matmul(
                                PS[:, 1024 * i2 + 512 * o2:1024 * i2 + 512 * (o2 + 1)],
                                hq_fm[:, 128 * s2:128 * (s2 + 1)],
                                td_sb[:, 1024 * s4 + 512 * o2:1024 * s4 + 512 * (o2 + 1)],
                                start=(m8 == 0 and s4 == 0),
                                stop=(m8 == 7 and s4 == 3),
                                skip_group_check=True)
                tags[f"pe_dn{st}_{m8}"] = inc(te.drain(), "pe")
            TE(dn_te)

        for i, t in enumerate(tiles):
            def y_ve(v, i=i, t=t, st=st):
                w(v, "pe", tags[f"pe_dn{st}_7"])
                w(v, "st", tags.get(f"st_y{t - 1}"))
                for o2 in range(2):
                    v.scalar_tensor_tensor(
                        y_sb[:, 512 * o2:512 * (o2 + 1)],
                        PS[:, 1024 * i + 512 * o2:1024 * i + 512 * (o2 + 1)],
                        cdv[:, t:t + 1],
                        stream[:, 1024 * t + 512 * o2:1024 * t + 512 * (o2 + 1)],
                        ALU.mult, ALU.add)
                tags[f"v_y{t}"] = inc(v.drain(), "dve")
            VE(y_ve)

            def y_gp(gp, i=i, t=t):
                w(gp, "dve", tags[f"v_y{t}"])
                tags[f"st_y{t}"] = dinc(gp.dma_start(
                    out=dy[128 * t:128 * (t + 1), :], in_=y_sb[:]), "st")
            GP(y_gp)

    def fin(gp):
        gp.wait_ge(sem["st"], cnt["st"])
    GP(fin)

    ctx.close()
    return nc



def _host_consts():
    c = np.zeros((128, 16), np.float32)
    c[:, C_EPS] = EPS
    c[:, C_MAGIC] = MAGIC
    c[:, C_NMAGIC] = -MAGIC
    c[:, C_ONE] = 1.0
    return c


def _selector_consts():
    SELHv = np.zeros((128, H * CK), np.float32)
    for h in range(H):
        SELHv[h, 128 * h:128 * (h + 1)] = 1.0
    QCv = np.zeros((16, 128), np.float32)
    for h in range(16):
        QCv[h, 64 * (h % 2):64 * (h % 2) + 64] = 1.0
    M8v = np.zeros((16, 8), np.float32)
    for h in range(16):
        M8v[h, h // 2] = 1.0
    S16v = np.zeros((128, NS * 128), np.float32)
    for s in range(NS):
        for p in range(128):
            S16v[2 * s + p // 64, 128 * s + p] = 1.0
    maskv = np.full((128, 128), -1e9, np.float32)
    for k in range(128):
        maskv[k, k + 1:] = 0.0
    return SELHv, QCv, M8v, S16v.astype(ml_dtypes.bfloat16), maskv


def make_in_maps(inputs, NC=16, n_cores=8):
    x = np.asarray(inputs["L0_attn_out"], np.float32)
    B = x.shape[0]
    spb = n_cores // B
    TPC = NC * CK
    WwT = np.ascontiguousarray(np.asarray(inputs["L0_attn_W_write"], np.float32).T)
    WgbaT = np.ascontiguousarray(np.concatenate(
        [np.asarray(inputs["L0_attn_W_gate"], np.float32),
         np.asarray(inputs["L0_attn_W_beta"], np.float32),
         np.asarray(inputs["L0_attn_W_alpha"], np.float32)], axis=0).T)
    WoT = np.ascontiguousarray(np.asarray(inputs["L0_attn_W_out"], np.float32).T)
    wgT = np.ascontiguousarray(np.asarray(inputs["L0_ffn_w_gate"], np.float32).T)
    wuT = np.ascontiguousarray(np.asarray(inputs["L0_ffn_w_up"], np.float32).T)
    wdT = np.ascontiguousarray(np.asarray(inputs["L0_ffn_w_down"], np.float32).T)
    a16v = np.ascontiguousarray(
        np.asarray(inputs["L0_attn_A_log"], np.float32).reshape(16, 1))
    dtbv = np.ascontiguousarray(
        np.asarray(inputs["L0_attn_dt_bias"], np.float32).reshape(16, 1))
    SELHv, QCv, M8v, S16v, maskv = _selector_consts()
    common = dict(
        WwT=WwT, WgbaT=WgbaT, WoT=WoT, wgT=wgT, wuT=wuT, wdT=wdT,
        a16=a16v, dtb16=dtbv, consts=_host_consts(),
        identb=np.eye(128, dtype=ml_dtypes.bfloat16),
        identf=np.eye(128, dtype=np.float32),
        maskst=maskv, SELH=SELHv, QC32=QCv, M8c=M8v, SEL16b=S16v,
        ones1f=np.ones((1, 128), np.float32),
        ones128=np.ones((128, 1), np.float32),
    )
    in_maps = []
    for c in range(n_cores):
        b, s = c // spb, c % spb
        x_sl = np.ascontiguousarray(x[b, s * TPC:(s + 1) * TPC])
        x_prev = (np.zeros((1, DM), np.float32) if s == 0
                  else np.ascontiguousarray(x[b, s * TPC - 1:s * TPC]))
        mprev = np.zeros((1, 8), np.float32)
        for j in range(n_cores):
            if j // spb == b and j % spb < s:
                mprev[0, j] = 1.0
        m = dict(common)
        m.update(x_sl=x_sl, x_prev=x_prev,
                 mprev16=np.ascontiguousarray(np.tile(mprev, (16, 1))))
        in_maps.append(m)
    return in_maps


def kernel(**inputs):
    x = np.asarray(inputs["L0_attn_out"])
    B, T_full, _ = x.shape
    n_cores = 8
    NC = (B * T_full) // (n_cores * CK)
    nc = build_program(NC)
    in_maps = make_in_maps(inputs, NC=NC, n_cores=n_cores)
    res = run_bass_kernel_spmd(nc, in_maps, core_ids=list(range(n_cores)))
    TPC = NC * CK
    out = np.empty((B, T_full, DM), np.float32)
    spb = n_cores // B
    for c in range(n_cores):
        b, s = c // spb, c % spb
        out[b, s * TPC:(s + 1) * TPC] = res.results[c]["y_sl"]
    return out


# version 03
# speedup vs baseline: 1.1674x; 1.1674x over previous
import numpy as np
import ml_dtypes
import concourse.bass as bass
import concourse.mybir as mybir
from concourse.bass_utils import run_bass_kernel_spmd

F32 = mybir.dt.float32
BF16 = mybir.dt.bfloat16
I32 = mybir.dt.int32
AF = mybir.ActivationFunctionType
ALU = mybir.AluOpType
AX = mybir.AxisListType

DM = 1024
H = 16
CK = 128
NS = 8
INNER = 4096
MAGIC = 12582912.0
DELTA = 0.4995
EPS = 1e-5
C_EPS, C_MAGIC, C_NMAGIC, C_ONE = 0, 1, 2, 3
SLABW = 2052


def build_program(NC=16, phases=5, w1=True, p4sub=99):
    import os
    G = lambda k: os.environ.get(k, "1") == "1"
    nc = bass.Bass("TRN2", target_bir_lowering=False, detect_race_conditions=False)
    T = NC * CK

    dx = nc.dram_tensor("x_sl", [T, DM], F32, kind="ExternalInput").ap()
    dxprev = nc.dram_tensor("x_prev", [1, DM], F32, kind="ExternalInput").ap()
    dWw = nc.dram_tensor("WwT", [DM, DM], F32, kind="ExternalInput").ap()
    dWgba = nc.dram_tensor("WgbaT", [DM, 48], F32, kind="ExternalInput").ap()
    dWo = nc.dram_tensor("WoT", [DM, DM], F32, kind="ExternalInput").ap()
    dwg = nc.dram_tensor("wgT", [DM, INNER], F32, kind="ExternalInput").ap()
    dwu = nc.dram_tensor("wuT", [DM, INNER], F32, kind="ExternalInput").ap()
    dwd = nc.dram_tensor("wdT", [INNER, DM], F32, kind="ExternalInput").ap()
    da16 = nc.dram_tensor("a16", [16, 1], F32, kind="ExternalInput").ap()
    ddtb = nc.dram_tensor("dtb16", [16, 1], F32, kind="ExternalInput").ap()
    dconsts = nc.dram_tensor("consts", [128, 16], F32, kind="ExternalInput").ap()
    dident = nc.dram_tensor("identb", [128, 128], BF16, kind="ExternalInput").ap()
    didentf = nc.dram_tensor("identf", [128, 128], F32, kind="ExternalInput").ap()
    dmask = nc.dram_tensor("maskst", [128, 128], F32, kind="ExternalInput").ap()
    dSELH = nc.dram_tensor("SELH", [128, H * CK], F32, kind="ExternalInput").ap()
    dQC = nc.dram_tensor("QC32", [16, 128], F32, kind="ExternalInput").ap()
    dM8 = nc.dram_tensor("M8c", [16, 8], F32, kind="ExternalInput").ap()
    dSEL16 = nc.dram_tensor("SEL16b", [128, NS * 128], BF16, kind="ExternalInput").ap()
    dones1 = nc.dram_tensor("ones1f", [1, 128], F32, kind="ExternalInput").ap()
    dones128 = nc.dram_tensor("ones128", [128, 1], F32, kind="ExternalInput").ap()
    dmprev = nc.dram_tensor("mprev16", [16, 8], F32, kind="ExternalInput").ap()
    dy = nc.dram_tensor("y_sl", [T, DM], F32, kind="ExternalOutput").ap()

    scr_g = nc.dram_tensor("scr_g", [DM, INNER], BF16, kind="Internal")
    scr_u = nc.dram_tensor("scr_u", [DM, INNER], BF16, kind="Internal")
    scr_d = nc.dram_tensor("scr_d", [INNER, DM], BF16, kind="Internal")
    GN = 128 * 512 + 16
    gin = nc.dram_tensor("gin", [1, GN], F32, kind="Internal")
    gout = nc.dram_tensor("gout", [8, GN], F32, kind="Internal")

    base0 = (nc.sbuf_base + 63) & ~63
    cur = [base0]

    def alo(name, shape, dtype, at=None):
        if at is None:
            at = cur[0]
            nbytes = int(np.prod(shape[1:])) * mybir.dt.size(dtype)
            cur[0] = (at + nbytes + 63) & ~63
        return nc.alloc_sbuf_tensor_at(name, shape, dtype, offset=at)

    stream = alo("stream", [128, NC * 1024], F32)
    consts = alo("consts_sb", [128, 16], F32)
    gate_tm = alo("gate_tm", [128, NC * 16], F32)
    cgv = alo("cgv", [128, NC], F32)
    cuv = alo("cuv", [128, NC], F32)
    cdv = alo("cdv", [128, NC], F32)
    mwth = alo("mwth", [128, 8], F32)
    nthv = alo("nthv", [128, 4], F32)
    identb = alo("identb_sb", [128, 128], BF16)
    identf = alo("identf_sb", [128, 128], F32)
    QC32 = alo("QC32_sb", [16, 128], F32)
    M8c = alo("M8c_sb", [16, 8], F32)
    ones1f = alo("ones1f_sb", [1, 128], F32)
    ones128 = alo("ones128_sb", [128, 1], F32)
    mprev16 = alo("mprev_sb", [16, 8], F32)
    a16 = alo("a16_sb", [16, 1], F32)
    dtb16 = alo("dtb16_sb", [16, 1], F32)
    negeA = alo("negeA", [16, 1], F32)
    wabs = alo("wabs", [128, 192], F32)
    wsum3 = alo("wsum3", [128, 4], F32)
    t1a = alo("t1a", [128, 1], F32)
    t1b = alo("t1b", [128, 1], F32)
    t1c = alo("t1c", [128, 1], F32)
    t1d = alo("t1d", [128, 1], F32)
    tp1 = alo("tp1", [1, 8], F32)

    attn_base = cur[0]
    xn_fm = alo("xn_fm", [128, NS * SLABW], BF16)
    vb_tm = alo("vb_tm", [128, NC * 1024], BF16)
    states = alo("states", [128, NC * 512], BF16)
    dec_fm = alo("dec_fm", [128, NC * CK], F32)
    Wsh = alo("Wsh", [128, NS * 1024], BF16)
    Wgb = alo("Wgb", [128, NS * 48], BF16)
    maskst = nc.alloc_sbuf_tensor_at("maskst_sb", [128, 128], F32,
                                     offset=nc.lookup_mloc(Wgb).addr)
    S_run = alo("S_run", [128, 2 * 512], F32)
    Sg = nc.alloc_sbuf_tensor_at("Sg", [128, 512], F32,
                                 offset=nc.lookup_mloc(S_run).addr)
    if NC % 2 == 0:
        SEL16 = nc.alloc_sbuf_tensor_at("SEL16_sb", [128, NS * 128], BF16,
                                        offset=nc.lookup_mloc(S_run).addr + 2048)
    else:
        SEL16 = alo("SEL16_sb", [128, NS * 128], BF16)
    S0 = alo("S0", [128, 2 * 512], F32)
    t16 = alo("t16", [16, 1024], F32)
    xprev = nc.alloc_sbuf_tensor_at("xprev", [1, DM], F32,
                                    offset=nc.lookup_mloc(t16).addr)
    declast = alo("declast", [16, NC], F32)
    pn16 = alo("pn16", [16, NC], F32)
    DgT = alo("DgT", [16, 8], F32)
    G16 = alo("G16", [16, 8], F32)
    P16 = alo("P16", [16, 8], F32)
    cj16 = alo("cj16", [16, 8], F32)
    R16 = alo("R16", [16, 8], F32)

    pool_base = cur[0]
    wk_tm = alo("wk_tm", [128, 1024], BF16)
    vbdw = alo("vbdw", [128, 1024], BF16)
    xn_tmp = alo("xn_tmp", [128, 1024], F32)
    xnb = alo("xnb", [128, 1024], BF16)
    beta_t = alo("beta_t", [128, 16], F32)
    apre = alo("apre", [128, 16], F32)
    dwT = alo("dwT", [128, 16], F32)
    Earr = alo("Earr", [128, 8], F32)
    p2_end = cur[0]
    cur[0] = pool_base
    intra_sb = alo("intra_sb", [128, H * CK], BF16)
    Lh_sb = alo("Lh_sb", [128, 2 * 128], F32)
    Lraw = alo("Lraw", [128, 2 * 128], F32)
    o_tm = alo("o_tm", [128, 1024], BF16)
    o_fm = alo("o_fm", [128, 1024], BF16)
    rkdx = alo("rkdx", [128, NS * 128], BF16)
    Strue = alo("Strue", [128, 512], BF16)
    negdecT = alo("negdecT", [128, 16], F32)
    pnb = alo("pnb", [128, 8], F32)
    dexpb = alo("dexpb", [128, 128], BF16)
    cbj = alo("cbj", [128, 8], F32)
    p4_end = cur[0]

    cur[0] = attn_base
    h_arena = alo("h_arena", [128, 4 * INNER], F32)
    hq_tm = alo("hq_tm", [128, INNER], BF16)
    q1_fm = alo("q1_fm", [128, NS * 512], BF16)
    tg_sb = alo("tg_sb", [128, NS * 512], BF16)
    tu_sb = alo("tu_sb", [128, NS * 512], BF16)
    td_sb = nc.alloc_sbuf_tensor_at("td_sb", [128, 4 * 1024], BF16,
                                    offset=nc.lookup_mloc(tg_sb).addr)
    y_sb = alo("y_sb", [128, 1024], F32)
    xc = alo("xc", [128, 1024], F32)
    f_tmp = alo("f_tmp", [128, 1024], F32)
    f_bf = alo("f_bf", [128, 1024], BF16)
    q5a = alo("q5a", [128, 1], F32)
    q5b = alo("q5b", [128, 1], F32)
    q5c = alo("q5c", [128, 1], F32)
    q5d = alo("q5d", [128, 1], F32)
    ffn_end = cur[0]

    cur[0] = max(p2_end, p4_end, ffn_end)
    stag = alo("stag", [128, 2 * 512], F32)
    ternst = alo("ternst", [128, 2 * 512], BF16)
    SELH = nc.alloc_sbuf_tensor_at("SELH_sb", [128, H * CK], F32,
                                   offset=nc.lookup_mloc(stag).addr)
    assert cur[0] <= nc.sbuf_top, f"SBUF over: {cur[0]} > {nc.sbuf_top}"

    nc.const_aps.register(F32, MAGIC, consts[:, C_MAGIC:C_MAGIC + 1])
    nc.const_aps.register(F32, -MAGIC, consts[:, C_NMAGIC:C_NMAGIC + 1])
    PS = nc.alloc_psum_tensor("PSA", [128, 4096], F32)

    def psb(a, b):
        return PS[:, a:b].bitcast(BF16)

    XT = (1792, 3872)
    WKT = (1664, 3584)
    OTT = (1664, 0)
    QT = (2048, 2560)

    from contextlib import ExitStack
    ctx = ExitStack()
    sem = {}
    for s in ("ld", "ldc", "gd", "pe", "dve", "act", "gpc", "cc", "st", "ph"):
        sem[s] = ctx.enter_context(nc.semaphore("s_" + s))
    block = ctx.enter_context(nc.Block())
    cnt = {k: 0 for k in sem}
    tags = {}

    def inc(instr, name, n=1):
        instr.then_inc(sem[name], n)
        cnt[name] += n
        return cnt[name]

    def dinc(instr, name):
        return inc(instr, name, 16)

    def w(eng, name, tag):
        if tag is not None:
            eng.wait_ge(sem[name], tag)

    _pe_last = [None]

    class _TEW:
        def __init__(self, e):
            self._e = e

        def matmul(self, *a, **k):
            r = self._e.matmul(*a, **k)
            _pe_last[0] = r
            return r

        def transpose(self, *a, **k):
            r = self._e.transpose(*a, **k)
            _pe_last[0] = None
            return r

        def drain(self):
            if _pe_last[0] is None:
                return self._e.drain()
            r = _pe_last[0]
            _pe_last[0] = None
            return r

        def __getattr__(self, k):
            return getattr(self._e, k)

    SY, VE, SC, GP = block.sync, block.vector, block.scalar, block.gpsimd
    TE = lambda f: block.tensor(lambda te: f(_TEW(te)))

    def barrier():
        target = cnt["ph"] + 5
        for s in (SY, VE, SC, TE, GP):
            def f(eng, target=target):
                eng.sem_inc(sem["ph"], 1)
                eng.wait_ge(sem["ph"], target)
            s(f)
        cnt["ph"] = target

    def p0(sync):
        for name, dst, src in (
            ("consts", consts, dconsts), ("identb", identb, dident),
            ("identf", identf, didentf),
            ("QC32", QC32, dQC), ("M8c", M8c, dM8),
            ("ones1f", ones1f, dones1),
            ("ones128", ones128, dones128), ("mprev", mprev16, dmprev),
            ("a16", a16, da16), ("dtb16", dtb16, ddtb), ("xprev", xprev, dxprev),
        ):
            tags["ld_" + name] = dinc(sync.dma_start(out=dst[:], in_=src), "ldc")
    SY(p0)
    P0C = cnt["ldc"]

    def p0g(gp):
        tags["gd_Ww"] = dinc(gp.dma_start(
            out=Wsh[:].rearrange("p (s n) -> p s n", s=NS),
            in_=dWw.rearrange("(s p) n -> p s n", p=128)), "gd")
        tags["gd_Wgb"] = dinc(gp.dma_start(
            out=Wgb[:].rearrange("p (s n) -> p s n", s=NS),
            in_=dWgba.rearrange("(s p) n -> p s n", p=128)), "gd")
    GP(p0g)

    def p0v(v):
        v.memset(dexpb[:], 0.0)
        v.memset(S_run[:], 0.0)
        v.memset(S0[:], 0.0)
        if os.environ.get("DBG", "0") == "1":
            for tns in (xn_fm, vb_tm, states, rkdx, Strue, o_tm, o_fm,
                        intra_sb, negdecT, pnb, Lh_sb, Lraw, gate_tm,
                        declast, pn16, DgT, cj16):
                v.memset(tns[:], 0.0)
        tags["z_dec"] = inc(v.memset(dec_fm[:], 0.0), "dve")
    VE(p0v)

    def p0s(sc):
        w(sc, "ldc", P0C)
        sc.activation(negeA[:], a16[:], AF.Exp)
        sc.activation(xnb[0:1, :], xprev[:], AF.Square, accum_out=tp1[:, 1:2])
        tags["a_prev"] = inc(sc.drain(), "act")
    SC(p0s)

    def p0v2(v):
        w(v, "act", tags["a_prev"])
        v.drain()
        v.tensor_scalar(negeA[:], negeA[:], -1.0, None, ALU.mult)
        v.reduce_sum(tp1[:, 0:1], xprev[:], axis=AX.X)
        v.drain()
        v.tensor_scalar(tp1[:, 2:3], tp1[:, 0:1], 1.0 / DM, None, ALU.mult)
        v.drain()
        v.scalar_tensor_tensor(tp1[:, 3:4], tp1[:, 2:3], -1.0, tp1[:, 2:3], ALU.mult, ALU.mult)
        v.drain()
        v.scalar_tensor_tensor(tp1[:, 4:5], tp1[:, 1:2], 1.0 / DM, tp1[:, 3:4], ALU.mult, ALU.add)
        tags["v_prev1"] = inc(v.drain(), "dve")
    VE(p0v2)

    def p0s2(sc):
        w(sc, "dve", tags["v_prev1"])
        w(sc, "ldc", P0C)
        sc.activation(tp1[:, 5:6], tp1[:, 4:5], AF.Ln, bias=consts[0:1, C_EPS:C_EPS + 1])
        sc.drain()
        tags["a_prev2"] = inc(
            sc.activation(tp1[:, 6:7], tp1[:, 5:6], AF.Exp, scale=-0.5), "act")
    SC(p0s2)

    def p0v3(v):
        w(v, "act", tags["a_prev2"])
        v.drain()
        v.tensor_scalar(xprev[:], xprev[:], tp1[:, 2:3], tp1[:, 6:7], ALU.subtract, ALU.mult)
        v.drain()
        tags["v_prev2"] = inc(v.tensor_copy(xnb[0:1, :], xprev[:]), "dve")
        if os.environ.get("DBG", "0") == "1":
            v.memset(t16[:], 0.0)
    VE(p0v3)

    for s in range(NS):
        def p0t(te, s=s):
            if s == 0:
                w(te, "dve", tags["v_prev2"])
                w(te, "ldc", P0C)
            if s >= 2:
                w(te, "dve", tags[f"v_prev3_{s - 2}"])
            slot = s % 2
            te.transpose(psb(XT[slot], XT[slot] + 64)[:, 0:1],
                         xnb[0:1, 128 * s:128 * (s + 1)], identb[0:1, 0:1])
            tags[f"pe_prev{s}"] = inc(te.drain(), "pe")
        TE(p0t)

        def p0v4(v, s=s):
            w(v, "pe", tags[f"pe_prev{s}"])
            slot = s % 2
            v.tensor_copy(xn_fm[:, SLABW * s:SLABW * s + 1],
                          psb(XT[slot], XT[slot] + 64)[:, 0:1])
            tags[f"v_prev3_{s}"] = inc(v.drain(), "dve")
        VE(p0v4)
    tags["v_prev3"] = tags[f"v_prev3_{NS - 1}"]

    wslabs = []
    for mi, dsrc in enumerate((dwg, dwu)):
        for a in range(8):
            for b in range(8):
                wslabs.append((dsrc[128 * a:128 * (a + 1), 512 * b:512 * (b + 1)],
                               64 * mi + 8 * a + b, mi))
    for a in range(32):
        for b in range(2):
            wslabs.append((dwd[128 * a:128 * (a + 1), 512 * b:512 * (b + 1)],
                           128 + 2 * a + b, 2))

    def scr_dst(k):
        _, idx, mi = wslabs[k]
        if mi == 0:
            r = idx
            return scr_g.ap()[128 * (r // 8):128 * (r // 8 + 1),
                              512 * (r % 8):512 * (r % 8 + 1)]
        if mi == 1:
            r = idx - 64
            return scr_u.ap()[128 * (r // 8):128 * (r // 8 + 1),
                              512 * (r % 8):512 * (r % 8 + 1)]
        r = idx - 128
        return scr_d.ap()[128 * (r // 2):128 * (r // 2 + 1),
                          512 * (r % 2):512 * (r % 2 + 1)]

    wlA = [0]
    wlB = [0]

    def p1_w_loads(sync, budget):
        for _ in range(budget):
            k = 2 * wlA[0]
            if k >= len(wslabs):
                return
            w(sync, "act", tags.get(f"a_w{k - 2}"))
            tags[f"ld_w{k}"] = dinc(sync.dma_start(
                out=stag[:, 0:512], in_=wslabs[k][0]), "ld")
            wlA[0] += 1

    def p1_w_loads_gp(gp, budget):
        for _ in range(budget):
            k = 2 * wlB[0] + 1
            if k >= len(wslabs):
                return
            w(gp, "act", tags.get(f"a_w{k - 2}"))
            tags[f"ld_w{k}"] = dinc(gp.dma_start(
                out=stag[:, 512:1024], in_=wslabs[k][0]), "gd")
            wlB[0] += 1

    wd_ = [0]

    def p1_w_abs(sc):
        while wd_[0] < len(wslabs) and f"ld_w{wd_[0]}" in tags:
            k = wd_[0]
            w(sc, "ld" if k % 2 == 0 else "gd", tags[f"ld_w{k}"])
            slot = k % 2
            _, idx, mi = wslabs[k]
            sc.activation(xnb[:, 0:512], stag[:, 512 * slot:512 * (slot + 1)],
                          AF.Abs, accum_out=wabs[:, idx:idx + 1])
            tags[f"a_w{k}"] = inc(sc.drain(), "act")
            wd_[0] += 1

    for t in range(NC):
        def p1_sy(sync, t=t):
            tags[f"ld_x{t}"] = dinc(sync.dma_start(
                out=stream[:, 1024 * t:1024 * (t + 1)],
                in_=dx[128 * t:128 * (t + 1), :]), "ld")
            if w1:
                p1_w_loads(sync, 3)
        SY(p1_sy)
        if w1:
            GP(lambda gp: p1_w_loads_gp(gp, 3))

        def p1_sc(sc, t=t):
            w(sc, "ld", tags[f"ld_x{t}"])
            if t > 0:
                w(sc, "dve", tags[f"v_st{t - 1}"])
            if t > 0:
                w(sc, "pe", tags[f"pe_xn{t - 1}_{NS - 1}"])
            else:
                w(sc, "pe", tags[f"pe_prev{NS - 1}"])
            sc.activation(xnb[:], stream[:, 1024 * t:1024 * (t + 1)],
                          AF.Square, accum_out=t1b[:])
            tags[f"a_sq{t}"] = inc(sc.drain(), "act")
            if w1:
                p1_w_abs(sc)
        SC(p1_sc)

        def p1_ve(v, t=t):
            w(v, "ld", tags[f"ld_x{t}"])
            v.reduce_sum(t1a[:], stream[:, 1024 * t:1024 * (t + 1)], axis=AX.X)
            w(v, "act", tags[f"a_sq{t}"])
            v.drain()
            v.tensor_scalar(t1c[:], t1a[:], 1.0 / DM, None, ALU.mult)
            v.drain()
            v.scalar_tensor_tensor(t1d[:], t1c[:], -1.0, t1c[:], ALU.mult, ALU.mult)
            v.drain()
            v.scalar_tensor_tensor(t1d[:], t1b[:], 1.0 / DM, t1d[:], ALU.mult, ALU.add)
            tags[f"v_st{t}"] = inc(v.drain(), "dve")
        VE(p1_ve)

        def p1_sc2(sc, t=t):
            w(sc, "dve", tags[f"v_st{t}"])
            sc.activation(t1a[:], t1d[:], AF.Ln, bias=consts[:, C_EPS:C_EPS + 1])
            sc.drain()
            tags[f"a_r{t}"] = inc(sc.activation(t1d[:], t1a[:], AF.Exp, scale=-0.5), "act")
        SC(p1_sc2)

        def p1_ve2(v, t=t):
            w(v, "act", tags[f"a_r{t}"])
            v.tensor_scalar(xn_tmp[:], stream[:, 1024 * t:1024 * (t + 1)],
                            t1c[:], t1d[:], ALU.subtract, ALU.mult)
            v.drain()
            v.tensor_tensor(stream[:, 1024 * t:1024 * (t + 1)],
                            stream[:, 1024 * t:1024 * (t + 1)], xn_tmp[:], ALU.add)
            if t > 0:
                w(v, "pe", tags[f"pe_xn{t - 1}_{NS - 1}"])
            tags[f"v_xn{t}"] = inc(v.tensor_copy(xnb[:], xn_tmp[:]), "dve")
        VE(p1_ve2)

        for s in range(NS):
            def p1_te(te, t=t, s=s):
                gi = t * NS + s
                if s == 0:
                    w(te, "dve", tags[f"v_xn{t}"])
                if gi >= 2:
                    w(te, "dve", tags.get(f"v_ev{gi - 2}"))
                slot = gi % 2
                te.transpose(psb(XT[slot], XT[slot] + 64),
                             xnb[:, 128 * s:128 * (s + 1)], identb[:])
                tags[f"pe_xn{t}_{s}"] = inc(te.drain(), "pe")
            TE(p1_te)

            def p1_ve3(v, t=t, s=s):
                gi = t * NS + s
                w(v, "pe", tags[f"pe_xn{t}_{s}"])
                slot = gi % 2
                v.tensor_copy(
                    xn_fm[:, SLABW * s + 1 + 128 * t:SLABW * s + 1 + 128 * (t + 1)],
                    psb(XT[slot], XT[slot] + 64))
                tags[f"v_ev{gi}"] = inc(v.drain(), "dve")
            VE(p1_ve3)

    def p1_w_guard(sc):
        w(sc, "pe", tags[f"pe_xn{NC - 1}_{NS - 1}"])
    if w1 and wd_[0] < len(wslabs):
        SC(p1_w_guard)
    while w1 and wd_[0] < len(wslabs):
        SY(lambda sync: p1_w_loads(sync, 3))
        GP(lambda gp: p1_w_loads_gp(gp, 3))
        SC(p1_w_abs)

    def p1f_ve(v):
        if not w1:
            v.memset(wabs[:], 0.02)
        else:
            w(v, "act", tags[f"a_w{len(wslabs) - 1}"])
        for m in range(3):
            v.reduce_sum(t1a[:], wabs[:, 64 * m:64 * (m + 1)], axis=AX.X)
            v.drain()
            v.tensor_copy(wsum3[:, m:m + 1], t1a[:])
        tags["v_wsum"] = inc(v.drain(), "dve")
    VE(p1f_ve)

    def p1f_te(te):
        w(te, "dve", tags["v_wsum"])
        for m in range(3):
            te.matmul(PS[0:1, 3968 + m:3969 + m], wsum3[:, m:m + 1],
                      ones128[:], start=True, stop=True)
        tags["pe_wsum"] = inc(te.drain(), "pe")
    TE(p1f_te)

    def p1f_ve2(v):
        w(v, "pe", tags["pe_wsum"])
        for m in range(3):
            v.tensor_scalar(t16[0:1, 40 + m:41 + m], PS[0:1, 3968 + m:3969 + m],
                            1.0 / (INNER * DM), 1e-5, ALU.mult, ALU.max)
        tags["v_wmean"] = inc(v.drain(), "dve")
    VE(p1f_ve2)

    def p1f_te2(te):
        w(te, "dve", tags["v_wmean"])
        for m in range(3):
            te.matmul(PS[:, 3972 + m:3973 + m], ones1f[:], t16[0:1, 40 + m:41 + m],
                      start=True, stop=True)
        tags["pe_wb"] = inc(te.drain(), "pe")
    TE(p1f_te2)

    def p1f_ve3(v):
        w(v, "pe", tags["pe_wb"])
        for m in range(3):
            v.tensor_copy(mwth[:, m:m + 1], PS[:, 3972 + m:3973 + m])
            v.tensor_scalar(mwth[:, 3 + m:4 + m], PS[:, 3972 + m:3973 + m],
                            0.5, None, ALU.mult)
            v.tensor_scalar(nthv[:, m:m + 1], PS[:, 3972 + m:3973 + m],
                            -0.5, None, ALU.mult)
        tags["v_wth"] = inc(v.drain(), "dve")
    VE(p1f_ve3)

    dbg_specs = []
    if os.environ.get("DBG", "0") == "1":
        for nm, src in (("S0", lambda: S0[:, 0:512]), ("cj16", lambda: cj16[:]),
                        ("DgT", lambda: DgT[:]), ("pn16", lambda: pn16[:]),
                        ("mprev", lambda: mprev16[:]), ("decl", lambda: declast[:]),
                        ("st0", lambda: states[:, 0:512]),
                        ("rkdx", lambda: rkdx[:]), ("Strue", lambda: Strue[:]),
                        ("otm", lambda: o_tm[:]), ("ofm", lambda: o_fm[:]),
                        ("intra", lambda: intra_sb[:]), ("dexpb", lambda: dexpb[:]),
                        ("ndT", lambda: negdecT[:]), ("pnb", lambda: pnb[:]),
                        ("decfm", lambda: dec_fm[:]), ("Lh", lambda: Lh_sb[:]),
                        ("Lraw", lambda: Lraw[:]), ("gatetm", lambda: gate_tm[:]),
                        ("vbtm", lambda: vb_tm[:]), ("xnfm", lambda: xn_fm[:]),
                        ("negeA", lambda: negeA[:]), ("a16d", lambda: a16[:]),
                        ("dtb", lambda: dtb16[:]),
                        ("t16d", lambda: t16[:])):
            dbg_specs.append((nm, src))

    def debug_tail():
        barrier()
        def dt_gp(gp):
            for t in range(NC):
                tags[f"st_dbg{t}"] = dinc(gp.dma_start(
                    out=dy[128 * t:128 * (t + 1), :],
                    in_=stream[:, 1024 * t:1024 * (t + 1)]), "st")
            for nm, src in dbg_specs:
                ap = src()
                d = nc.dram_tensor("dbg_" + nm, list(ap.shape), ap.dtype,
                                   kind="ExternalOutput").ap()
                dinc(gp.dma_start(out=d, in_=ap), "st")
            gp.wait_ge(sem["st"], cnt["st"])
        GP(dt_gp)
        ctx.close()
        return nc

    if phases < 2:
        return debug_tail()

    for n in range(NC):
        def te_v(te, n=n):
            if n == 0:
                w(te, "gd", tags["gd_Ww"])
                w(te, "gd", tags["gd_Wgb"])
                w(te, "dve", tags["v_prev3"])
            else:
                w(te, "dve", tags[f"v_vb{n - 1}"])
                w(te, "dve", tags[f"v_ap{n - 1}"])
                w(te, "act", tags[f"a_gb{n - 1}"])
            for s in range(NS):
                lhs = xn_fm[:, SLABW * s + 1 + 128 * n:SLABW * s + 1 + 128 * (n + 1)]
                te.matmul(PS[:, 0:512], lhs, Wsh[:, 1024 * s:1024 * s + 512],
                          start=(s == 0), stop=(s == NS - 1), skip_group_check=True)
                te.matmul(PS[:, 512:1024], lhs, Wsh[:, 1024 * s + 512:1024 * (s + 1)],
                          start=(s == 0), stop=(s == NS - 1), skip_group_check=True)
            tags[f"pe_v{n}"] = inc(te.drain(), "pe")
            for s in range(NS):
                lhs = xn_fm[:, SLABW * s + 1 + 128 * n:SLABW * s + 1 + 128 * (n + 1)]
                te.matmul(PS[:, 1024:1072], lhs, Wgb[:, 48 * s:48 * (s + 1)],
                          start=(s == 0), stop=(s == NS - 1), skip_group_check=True)
            tags[f"pe_gba{n}"] = inc(te.drain(), "pe")
        TE(te_v)

        def sc_g(sc, n=n):
            w(sc, "pe", tags[f"pe_gba{n}"])
            sc.activation(gate_tm[:, 16 * n:16 * (n + 1)], PS[:, 1024:1040], AF.Sigmoid)
            sc.activation(beta_t[:], PS[:, 1040:1056], AF.Sigmoid)
            tags[f"a_gb{n}"] = inc(sc.drain(), "act")
        SC(sc_g)

        def ve_a(v, n=n):
            w(v, "pe", tags[f"pe_gba{n}"])
            if n > 0:
                w(v, "pe", tags[f"pe_at{n - 1}"])
            v.tensor_copy(apre[:], PS[:, 1056:1072])
            tags[f"v_ap{n}"] = inc(v.drain(), "dve")
        VE(ve_a)

        def te_at(te, n=n):
            w(te, "dve", tags[f"v_ap{n}"])
            if n > 0:
                w(te, "dve", tags[f"v_z{n - 1}"])
            te.transpose(PS[0:16, 1536:1664], apre[:], identf[:])
            tags[f"pe_at{n}"] = inc(te.drain(), "pe")
        TE(te_at)

        def ve_z(v, n=n):
            w(v, "pe", tags[f"pe_at{n}"])
            v.tensor_scalar(t16[:, 0:128], PS[0:16, 1536:1664], dtb16[:, 0:1],
                            None, ALU.add)
            tags[f"v_z{n}"] = inc(v.drain(), "dve")
        VE(ve_z)

        def sc_sp(sc, n=n):
            w(sc, "dve", tags[f"v_z{n}"])
            sc.activation(t16[:, 128:256], t16[0:16, 0:128], AF.Abs)
            sc.drain()
            sc.activation(t16[:, 256:384], t16[0:16, 128:256], AF.Exp, scale=-1.0)
            sc.drain()
            sc.activation(t16[:, 384:512], t16[0:16, 256:384], AF.Ln,
                          bias=consts[0:16, C_ONE:C_ONE + 1])
            tags[f"a_sp{n}"] = inc(sc.drain(), "act")
        SC(sc_sp)

        def ve_dec(v, n=n):
            w(v, "act", tags[f"a_sp{n}"])
            if n == 0:
                w(v, "dve", tags["z_dec"])
            v.tensor_scalar(t16[:, 512:640], t16[0:16, 0:128], 0.0, None, ALU.max)
            v.drain()
            v.tensor_tensor(t16[:, 512:640], t16[0:16, 512:640],
                            t16[0:16, 384:512], ALU.add)
            v.drain()
            v.tensor_scalar(t16[:, 512:640], t16[0:16, 512:640],
                            negeA[:, 0:1], None, ALU.mult)
            v.drain()
            v.tensor_tensor_scan(dec_fm[0:16, CK * n:CK * (n + 1)],
                                 t16[0:16, 512:640], t16[0:16, 512:640],
                                 0.0, ALU.add, ALU.bypass)
            v.drain()
            v.tensor_copy(declast[:, n:n + 1], dec_fm[0:16, CK * n + 127:CK * n + 128])
            tags[f"v_dec{n}"] = inc(v.drain(), "dve")
        VE(ve_dec)

        def sc_dw(sc, n=n):
            w(sc, "dve", tags[f"v_dec{n}"])
            sc.activation(t16[:, 640:641], declast[0:16, n:n + 1], AF.Exp)
            sc.activation(t16[:, 644:772], dec_fm[0:16, CK * n:CK * (n + 1)],
                          AF.Exp, scale=-1.0, bias=declast[0:16, n:n + 1])
            tags[f"a_dw{n}"] = inc(sc.drain(), "act")
        SC(sc_dw)

        def ve_vb(v, n=n):
            w(v, "pe", tags[f"pe_v{n}"])
            w(v, "act", tags[f"a_gb{n}"])
            for h in range(H):
                v.tensor_scalar(vb_tm[:, 1024 * n + 64 * h:1024 * n + 64 * (h + 1)],
                                PS[:, 64 * h:64 * (h + 1)],
                                beta_t[:, h:h + 1], None, ALU.mult)
            tags[f"v_vb{n}"] = inc(v.drain(), "dve")
            w(v, "act", tags[f"a_dw{n}"])
            v.tensor_copy(t16[:, 772:836].bitcast(BF16), t16[0:16, 644:772])
            v.tensor_scalar(R16[:], M8c[:], t16[0:16, 640:641], None, ALU.mult)
            tags[f"v_dw{n}"] = inc(v.drain(), "dve")
        VE(ve_vb)

        def te_dwt(te, n=n):
            w(te, "dve", tags[f"v_dw{n}"])
            if n > 0:
                w(te, "dve", tags[f"v_dwt{n - 1}"])
            te.transpose(psb(1800, 1808)[:, 0:16], t16[0:16, 772:836].bitcast(BF16),
                         identb[0:16, 0:16])
            te.matmul(PS[:, 2560:2568], QC32[:], R16[:], start=True, stop=True,
                      skip_group_check=True)
            tags[f"pe_dwt{n}"] = inc(te.drain(), "pe")
        TE(te_dwt)

        def ve_vbdw(v, n=n):
            w(v, "pe", tags[f"pe_dwt{n}"])
            if n > 0:
                w(v, "pe", tags[f"pe_u{n - 1}_{NS - 1}"])
            v.tensor_copy(dwT[:], psb(1800, 1808)[:, 0:16])
            v.tensor_copy(Earr[:], PS[:, 2560:2568])
            tags[f"v_dwt{n}"] = inc(v.drain(), "dve")
            for h in range(H):
                v.tensor_scalar(vbdw[:, 64 * h:64 * (h + 1)],
                                vb_tm[:, 1024 * n + 64 * h:1024 * n + 64 * (h + 1)],
                                dwT[:, h:h + 1], None, ALU.mult)
            tags[f"v_vbdw{n}"] = inc(v.drain(), "dve")
        VE(ve_vbdw)

        def ve_snap(v, n=n):
            v.tensor_copy(states[:, 512 * n:512 * (n + 1)],
                          S_run[:, 512 * (n % 2):512 * (n % 2) + 512])
            tags[f"v_snap{n}"] = inc(v.drain(), "dve")
        VE(ve_snap)

        for s in range(NS):
            def te_wk(te, n=n, s=s):
                if s == 0:
                    w(te, "dve", tags[f"v_vbdw{n}"])
                if n > 0 and s == 0:
                    w(te, "pe", tags[f"pe_u{n - 1}_{NS - 1}"])
                if s >= 2:
                    w(te, "dve", tags[f"v_wkev{n}_{s - 2}"])
                slot = s % 2
                te.transpose(psb(WKT[slot], WKT[slot] + 64),
                             xn_fm[:, SLABW * s + 128 * n:SLABW * s + 128 * (n + 1)],
                             identb[:])
                tags[f"pe_wk{n}_{s}"] = inc(te.drain(), "pe")
            TE(te_wk)

            def ve_wkev(v, n=n, s=s):
                w(v, "pe", tags[f"pe_wk{n}_{s}"])
                slot = s % 2
                v.tensor_copy(wk_tm[:, 128 * s:128 * (s + 1)],
                              psb(WKT[slot], WKT[slot] + 64))
                tags[f"v_wkev{n}_{s}"] = inc(v.drain(), "dve")
            VE(ve_wkev)

            def te_u(te, n=n, s=s):
                w(te, "dve", tags[f"v_wkev{n}_{s}"])
                if s >= 2:
                    w(te, "dve", tags[f"v_su{n}_{s - 2}"])
                slot = s % 2
                te.matmul(PS[:, 2048 + 128 * slot:2176 + 128 * slot],
                          wk_tm[:, 128 * s:128 * (s + 1)],
                          vbdw[:, 128 * s:128 * (s + 1)],
                          start=True, stop=True, skip_group_check=True)
                tags[f"pe_u{n}_{s}"] = inc(te.drain(), "pe")
            TE(te_u)

            def ve_srun(v, n=n, s=s):
                if s == 0:
                    w(v, "dve", tags[f"v_snap{n}"])
                w(v, "pe", tags[f"pe_u{n}_{s}"])
                slot = s % 2
                bc = 2048 + 128 * slot
                rd, wr = 512 * (n % 2), 512 * ((n + 1) % 2)
                v.scalar_tensor_tensor(S_run[0:64, wr + 64 * s:wr + 64 * (s + 1)],
                                       S_run[0:64, rd + 64 * s:rd + 64 * (s + 1)],
                                       Earr[0:64, s:s + 1],
                                       PS[0:64, bc:bc + 64], ALU.mult, ALU.add)
                v.scalar_tensor_tensor(S_run[64:128, wr + 64 * s:wr + 64 * (s + 1)],
                                       S_run[64:128, rd + 64 * s:rd + 64 * (s + 1)],
                                       Earr[64:128, s:s + 1],
                                       PS[64:128, bc + 64:bc + 128], ALU.mult, ALU.add)
                tags[f"v_su{n}_{s}"] = inc(v.drain(), "dve")
            VE(ve_srun)

    barrier()

    def p3_ve(v):
        v.reduce_sum(t16[:, 900:901], declast[0:16, 0:NC], axis=AX.X)
        v.tensor_tensor_scan(t16[:, 904:904 + NC], declast[0:16, 0:NC],
                             declast[0:16, 0:NC], 0.0, ALU.add, ALU.bypass)
        v.drain()
        v.tensor_tensor(t16[:, 960:960 + NC], t16[0:16, 904:904 + NC],
                        declast[0:16, 0:NC], ALU.subtract)
        tags["v_p3a"] = inc(v.drain(), "dve")
    VE(p3_ve)

    def p3_sc(sc):
        w(sc, "dve", tags["v_p3a"])
        sc.activation(t16[:, 940:941], t16[0:16, 900:901], AF.Exp)
        sc.activation(pn16[:, 0:NC], t16[0:16, 960:960 + NC], AF.Exp)
        tags["a_p3"] = inc(sc.drain(), "act")
    SC(p3_sc)

    def p3_gp(gp):
        w(gp, "act", tags["a_p3"])
        tags["gd_g1"] = dinc(gp.dma_start(
            out=gin.ap()[0:1, 0:65536].rearrange("a (p c) -> (a p) c", p=128),
            in_=S_run[:, 512 * (NC % 2):512 * (NC % 2) + 512]), "gd")
        tags["gd_g2"] = dinc(gp.dma_start(
            out=gin.ap()[0:1, 65536:65552].rearrange("a (p b) -> (a p) b", p=16),
            in_=t16[0:16, 940:941]), "gd")
        gp.wait_ge(sem["gd"], tags["gd_g2"])
        tags["cc"] = inc(gp.collective_compute(
            "AllGather", ALU.bypass, replica_groups=[list(range(8))],
            ins=[gin.ap()], outs=[gout.ap()]), "cc")
    GP(p3_gp)

    for k in range(len(wslabs)):
        def p3w_sy(sync, k=k):
            slot = k % 2
            w(sync, "gd", tags.get(f"gd_t{k - 2}"))
            tags[f"ld2_w{k}"] = dinc(sync.dma_start(
                out=stag[:, 512 * slot:512 * (slot + 1)], in_=wslabs[k][0]), "ld")
        SY(p3w_sy)

        def p3w_ve(v, k=k):
            w(v, "ld", tags[f"ld2_w{k}"])
            if k == 0:
                w(v, "dve", tags["v_wth"])
            w(v, "gd", tags.get(f"gd_t{k - 2}"))
            slot = k % 2
            mi = wslabs[k][2]
            v.tensor_scalar(xn_tmp[:, 0:512], stag[:, 512 * slot:512 * (slot + 1)],
                            nthv[:, mi:mi + 1], None, ALU.is_lt)
            v.scalar_tensor_tensor(ternst[:, 512 * slot:512 * (slot + 1)],
                                   stag[:, 512 * slot:512 * (slot + 1)],
                                   mwth[:, 3 + mi:4 + mi], xn_tmp[:, 0:512],
                                   ALU.is_gt, ALU.subtract)
            tags[f"v_t{k}"] = inc(v.drain(), "dve")
        VE(p3w_ve)

        def p3w_gp(gp, k=k):
            w(gp, "dve", tags[f"v_t{k}"])
            slot = k % 2
            tags[f"gd_t{k}"] = dinc(gp.dma_start(
                out=scr_dst(k), in_=ternst[:, 512 * slot:512 * (slot + 1)]), "gd")
        GP(p3w_gp)

    def p3c_sy(sync):
        w(sync, "dve", tags[f"v_t{len(wslabs) - 1}"])
        w(sync, "gd", tags[f"gd_t{len(wslabs) - 1}"])
        tags["ld_SELH"] = dinc(sync.dma_start(out=SELH[:], in_=dSELH), "ldc")
        w(sync, "gd", tags["gd_g1"])
        tags["ld_SEL16"] = dinc(sync.dma_start(out=SEL16[:], in_=dSEL16), "ldc")
        tags["ld_mask"] = dinc(sync.dma_start(out=maskst[:], in_=dmask), "ldc")
    SY(p3c_sy)
    LDC_ALL = cnt["ldc"]

    def p3r_gp(gp):
        w(gp, "cc", tags["cc"])
        for j in range(8):
            tags[f"gd_Dg{j}"] = dinc(gp.dma_start(
                out=DgT[:, j:j + 1],
                in_=gout.ap()[j:j + 1, 65536:65552].rearrange(
                    "a (p b) -> (a p) b", p=16)), "gd")
    GP(p3r_gp)

    def p3r_ve(v):
        w(v, "gd", tags["gd_Dg7"])
        v.tensor_scalar(G16[:], DgT[:], 1.0, None, ALU.subtract)
        v.drain()
        v.tensor_tensor(G16[:], G16[:], mprev16[:], ALU.mult)
        v.drain()
        v.tensor_scalar(G16[:], G16[:], 1.0, None, ALU.add)
        v.memset(P16[:, 7:8], 1.0)
        v.drain()
        for j in range(6, -1, -1):
            v.tensor_tensor(P16[:, j:j + 1], P16[:, j + 1:j + 2],
                            G16[:, j + 1:j + 2], ALU.mult)
            v.drain()
        v.tensor_tensor(cj16[:], P16[:], mprev16[:], ALU.mult)
        tags["v_cj"] = inc(v.drain(), "dve")
    VE(p3r_ve)

    for j in range(8):
        def p3s_gp(gp, j=j):
            if j > 0:
                w(gp, "dve", tags[f"v_s0_{j - 1}"])
            tags[f"gd_Sg{j}"] = dinc(gp.dma_start(
                out=Sg[:], in_=gout.ap()[j:j + 1, 0:65536].rearrange(
                    "a (p c) -> (a p) c", p=128)), "gd")
        GP(p3s_gp)

        def p3s_ve(v, j=j):
            if j == 0:
                w(v, "dve", tags["v_cj"])
            else:
                w(v, "pe", tags[f"pe_cb{j - 1}"])
            v.tensor_scalar(R16[:], M8c[:], cj16[:, j:j + 1], None, ALU.mult)
            tags[f"v_r{j}"] = inc(v.drain(), "dve")
        VE(p3s_ve)

        def p3s_te(te, j=j):
            w(te, "dve", tags[f"v_r{j}"])
            te.matmul(PS[:, 2560:2568], QC32[:], R16[:], start=True, stop=True,
                      skip_group_check=True)
            tags[f"pe_cb{j}"] = inc(te.drain(), "pe")
        TE(p3s_te)

        def p3s_ve2(v, j=j):
            w(v, "pe", tags[f"pe_cb{j}"])
            v.tensor_copy(cbj[:], PS[:, 2560:2568])
            w(v, "gd", tags[f"gd_Sg{j}"])
            v.drain()
            rd, wr = 512 * (j % 2), 512 * ((j + 1) % 2)
            for p in range(8):
                v.scalar_tensor_tensor(S0[:, wr + 64 * p:wr + 64 * (p + 1)],
                                       Sg[:, 64 * p:64 * (p + 1)],
                                       cbj[:, p:p + 1],
                                       S0[:, rd + 64 * p:rd + 64 * (p + 1)],
                                       ALU.mult, ALU.add)
            tags[f"v_s0_{j}"] = inc(v.drain(), "dve")
        VE(p3s_ve2)

    barrier()
    if phases < 4:
        return debug_tail()

    def p4_wo(gp):
        tags["gd_Wo"] = dinc(gp.dma_start(
            out=Wsh[:].rearrange("p (s n) -> p s n", s=NS),
            in_=dWo.rearrange("(s p) n -> p s n", p=128)), "gd")
    GP(p4_wo)

    for n in range(NC):
        def ve_pn(v, n=n):
            if not G("GPN"):
                tags[f"v_pn{n}"] = cnt["dve"]
                tags[f"pe_dt{n}"] = cnt["pe"]
                tags[f"v_ndt{n}"] = cnt["dve"]
                tags[f"v_str{n}"] = inc(v.drain(), "dve")
                return
            if n > 0:
                w(v, "pe", tags[f"pe_dt{n - 1}"])
            v.tensor_scalar(R16[:], M8c[:], pn16[:, n:n + 1], None, ALU.mult)
            tags[f"v_pn{n}"] = inc(v.drain(), "dve")
        VE(ve_pn)

        def te_dt(te, n=n):
            if not G("GPN"):
                return
            w(te, "dve", tags[f"v_pn{n}"])
            if n > 0:
                w(te, "dve", tags[f"v_ndt{n - 1}"])
            if G("GDT"):
                te.transpose(PS[:, 384:512], dec_fm[:, CK * n:CK * (n + 1)],
                             identf[:])
                te.matmul(PS[:, 640:648], QC32[:], R16[:], start=True, stop=True,
                          skip_group_check=True)
            tags[f"pe_dt{n}"] = inc(te.drain(), "pe")
        TE(te_dt)

        def ve_b(v, n=n):
            if not G("GPN"):
                return
            w(v, "pe", tags[f"pe_dt{n}"])
            if n >= 1:
                w(v, "pe", tags[f"pe_o{n - 1}_{H - 1}"])
            v.tensor_scalar(negdecT[:], PS[:, 384:400], -1.0, None, ALU.mult)
            v.tensor_copy(pnb[:], PS[:, 640:648])
            tags[f"v_ndt{n}"] = inc(v.drain(), "dve")
            if G("GST"):
                for p in range(8):
                    so = 64 * p
                    v.scalar_tensor_tensor(Strue[:, so:so + 64],
                                           S0[:, 64 * p:64 * (p + 1)],
                                           pnb[:, p:p + 1],
                                           states[:, 512 * n + 64 * p:512 * n + 64 * (p + 1)],
                                           ALU.mult, ALU.add)
            tags[f"v_str{n}"] = inc(v.drain(), "dve")
        VE(ve_b)

        def sc_de(sc, n=n):
            if not G("GDE"):
                tags[f"a_de{n}"] = cnt["act"]
                return
            w(sc, "dve", tags[f"v_ndt{n}"])
            sc.activation(t16[:, 644:772], dec_fm[0:16, CK * n:CK * (n + 1)], AF.Exp)
            tags[f"a_de{n}"] = inc(sc.drain(), "act")
        SC(sc_de)

        def ve_deb(v, n=n):
            w(v, "act", tags[f"a_de{n}"])
            v.tensor_copy(dexpb[0:16, :], t16[0:16, 644:772])
            tags[f"v_deb{n}"] = inc(v.drain(), "dve")
        VE(ve_deb)

        for s in range(NS):
            def te_rk(te, n=n, s=s):
                if s == 0:
                    w(te, "dve", tags[f"v_deb{n}"])
                    if n == 0:
                        w(te, "ldc", LDC_ALL)
                if s >= 2:
                    w(te, "dve", tags[f"v_rk{n}_{s - 2}"])
                slot = s % 2
                if G("GRKM"):
                    rkc = 256 + 1024 * slot
                    te.matmul(PS[:, rkc:rkc + 128],
                              SEL16[:, 128 * s:128 * (s + 1)], dexpb[:],
                              start=True, stop=True, skip_group_check=True)
                tags[f"pe_rk{n}_{s}"] = inc(te.drain(), "pe")
            TE(te_rk)

            def ve_rk(v, n=n, s=s):
                if s == 0 and n > 0:
                    w(v, "pe", tags[f"pe_o{n - 1}_{H - 1}"])
                w(v, "pe", tags[f"pe_rk{n}_{s}"])
                slot = s % 2
                rkc = 256 + 1024 * slot
                if G("GRKE"):
                    if G("GRKC"):
                        v.tensor_tensor(rkdx[:, 128 * s:128 * (s + 1)],
                                        xn_fm[:, SLABW * s + 1 + 128 * n:SLABW * s + 1 + 128 * (n + 1)],
                                        PS[:, rkc:rkc + 128], ALU.mult)
                    else:
                        v.tensor_copy(rkdx[:, 128 * s:128 * (s + 1)],
                                      PS[:, rkc:rkc + 128])
                tags[f"v_rk{n}_{s}"] = inc(v.drain(), "dve")
            VE(ve_rk)

        ordq = [hh for hh in range(H) if hh % 2 == 0] + \
               [hh for hh in range(H) if hh % 2 == 1]
        if p4sub < 2:
            def ve_skipA(v, n=n):
                v.memset(intra_sb[:], 0.0)
                for hh in range(H):
                    tags[f"v_iev{n}_{hh}"] = inc(v.drain(), "dve")
                    tags[f"a_L{n}_{hh}"] = cnt["act"]
                    tags[f"v_La{n}_{hh}"] = tags[f"v_iev{n}_{hh}"]
                    if p4sub < 3:
                        tags[f"pe_o{n}_{hh}"] = cnt["pe"]
            VE(ve_skipA)
        for q, hh in enumerate(ordq):
            if p4sub < 2:
                break
            par = hh % 2
            sl = hh // 2
            r0 = 64 * par
            islot = q % 4
            lslot = q % 2

            def te_hd(te, n=n, q=q, hh=hh, par=par, sl=sl, r0=r0,
                      islot=islot, lslot=lslot):
                if q == 0:
                    w(te, "dve", tags[f"v_rk{n}_{NS - 1}"])
                    w(te, "dve", tags[f"v_str{n}"])
                    if n == 0:
                        w(te, "ldc", LDC_ALL)
                    if n > 0:
                        w(te, "dve", tags[f"v_oev{n - 1}"])
                pL = (tags.get(f"v_La{n}_{ordq[q - 2]}") if q >= 2
                      else (tags.get(f"v_La{n - 1}_{ordq[q + 14]}") if n > 0 else None))
                w(te, "act", None)
                w(te, "dve", pL)
                pI = (tags.get(f"v_iev{n}_{ordq[q - 2]}") if q >= 2
                      else (tags.get(f"v_iev{n - 1}_{ordq[q + 14]}") if n > 0 else None))
                w(te, "dve", pI)
                lc = 512 + 1024 * lslot
                te.matmul(PS[:, lc:lc + 128],
                          SELH[:, 128 * hh:128 * (hh + 1)],
                          dec_fm[:, CK * n:CK * (n + 1)],
                          start=True, stop=True, skip_group_check=True)
                tags[f"pe_L{n}_{hh}"] = inc(te.drain(), "pe")
                ic = 1024 * (islot % 2) + 128 * (islot // 2)
                te.matmul(PS[:, ic:ic + 128],
                          xn_fm[r0:r0 + 64, SLABW * sl + 128 * n:SLABW * sl + 128 * (n + 1)],
                          xn_fm[r0:r0 + 64, SLABW * sl + 1 + 128 * n:SLABW * sl + 1 + 128 * (n + 1)],
                          start=True, stop=True, skip_group_check=True)
                tags[f"pe_i{n}_{hh}"] = inc(te.drain(), "pe")
            TE(te_hd)

            def ve_La(v, n=n, q=q, hh=hh, lslot=lslot):
                w(v, "pe", tags[f"pe_L{n}_{hh}"])
                if n == 0 and q == 0:
                    w(v, "ldc", LDC_ALL)
                pS = (tags.get(f"a_L{n}_{ordq[q - 2]}") if q >= 2
                      else (tags.get(f"a_L{n - 1}_{ordq[q + 14]}") if n > 0 else None))
                w(v, "act", pS)
                lc = 512 + 1024 * lslot
                v.tensor_tensor(Lraw[:, 128 * lslot:128 * (lslot + 1)],
                                PS[:, lc:lc + 128],
                                maskst[:], ALU.add)
                tags[f"v_La{n}_{hh}"] = inc(v.drain(), "dve")
            VE(ve_La)

            def sc_hd(sc, n=n, q=q, hh=hh, lslot=lslot):
                w(sc, "dve", tags[f"v_La{n}_{hh}"])
                pS = (tags.get(f"v_iev{n}_{ordq[q - 2]}") if q >= 2
                      else (tags.get(f"v_iev{n - 1}_{ordq[q + 14]}") if n > 0 else None))
                w(sc, "dve", pS)
                sc.activation(Lh_sb[:, 128 * lslot:128 * (lslot + 1)],
                              Lraw[:, 128 * lslot:128 * (lslot + 1)],
                              AF.Exp, bias=negdecT[:, hh:hh + 1])
                tags[f"a_L{n}_{hh}"] = inc(sc.drain(), "act")
            SC(sc_hd)

            def ve_hd(v, n=n, hh=hh, islot=islot, lslot=lslot):
                w(v, "pe", tags[f"pe_i{n}_{hh}"])
                w(v, "act", tags[f"a_L{n}_{hh}"])
                if n > 0:
                    w(v, "pe", tags[f"pe_o{n - 1}_{hh}"])
                ic = 1024 * (islot % 2) + 128 * (islot // 2)
                v.tensor_tensor(intra_sb[:, 128 * hh:128 * (hh + 1)],
                                PS[:, ic:ic + 128],
                                Lh_sb[:, 128 * lslot:128 * (lslot + 1)], ALU.mult)
                tags[f"v_iev{n}_{hh}"] = inc(v.drain(), "dve")
            VE(ve_hd)

            def te_o1(te, n=n, hh=hh, par=par, sl=sl, r0=r0):
                if p4sub < 3:
                    tags[f"pe_o{n}_{hh}"] = cnt["pe"]
                    return
                w(te, "dve", tags[f"v_iev{n}_{hh}"])
                te.matmul(PS[:, 2048 + 64 * hh:2048 + 64 * (hh + 1)],
                          intra_sb[:, 128 * hh:128 * (hh + 1)],
                          vb_tm[:, 1024 * n + 64 * hh:1024 * n + 64 * (hh + 1)],
                          start=True, stop=False, skip_group_check=True)
                te.matmul(PS[:, 2048 + 64 * hh:2048 + 64 * (hh + 1)],
                          rkdx[r0:r0 + 64, 128 * sl:128 * (sl + 1)],
                          Strue[r0:r0 + 64, 64 * sl:64 * (sl + 1)],
                          start=False, stop=True, skip_group_check=True)
                tags[f"pe_o{n}_{hh}"] = inc(te.drain(), "pe")
            TE(te_o1)

        def ve_oev(v, n=n):
            if n > 0:
                w(v, "pe", tags[f"pe_ot{n - 1}_{NS - 1}"])
            if p4sub >= 3:
                w(v, "pe", tags[f"pe_o{n}_{ordq[-1]}"])
            for hh in range(H):
                if p4sub >= 3:
                    v.tensor_scalar(o_tm[:, 64 * hh:64 * (hh + 1)],
                                    PS[:, 2048 + 64 * hh:2048 + 64 * (hh + 1)],
                                    gate_tm[:, 16 * n + hh:16 * n + hh + 1], None, ALU.mult)
                else:
                    v.memset(o_tm[:, 64 * hh:64 * (hh + 1)], 0.0)
            tags[f"v_oev{n}"] = inc(v.drain(), "dve")
        VE(ve_oev)

        for s in range(NS):
            if p4sub < 4:
                def ve_skipC(v, n=n, s=s):
                    w(v, "dve", tags[f"v_oev{n}"])
                    tags[f"pe_ot{n}_{s}"] = cnt["pe"]
                    tags[f"v_ot{n}_{s}"] = inc(v.drain(), "dve")
                VE(ve_skipC)
                continue
            def te_ot(te, n=n, s=s):
                if s == 0:
                    w(te, "dve", tags[f"v_oev{n}"])
                    if n == 0:
                        w(te, "gd", tags["gd_Wo"])
                if s >= 2:
                    w(te, "dve", tags[f"v_ot{n}_{s - 2}"])
                slot = s % 2
                te.transpose(psb(OTT[slot], OTT[slot] + 64),
                             o_tm[:, 128 * s:128 * (s + 1)], identb[:])
                tags[f"pe_ot{n}_{s}"] = inc(te.drain(), "pe")
            TE(te_ot)

            def ve_ot(v, n=n, s=s):
                if s == 0 and n > 0:
                    w(v, "pe", tags[f"pe_wo{n - 1}"])
                w(v, "pe", tags[f"pe_ot{n}_{s}"])
                slot = s % 2
                v.tensor_copy(o_fm[:, 128 * s:128 * (s + 1)],
                              psb(OTT[slot], OTT[slot] + 64))
                tags[f"v_ot{n}_{s}"] = inc(v.drain(), "dve")
            VE(ve_ot)

        def te_wo(te, n=n):
            if p4sub < 4:
                tags[f"pe_wo{n}"] = cnt["pe"]
                return
            if n > 0:
                w(te, "dve", tags[f"v_x1_{n - 1}"])
            for s in range(NS):
                w(te, "dve", tags[f"v_ot{n}_{s}"])
                te.matmul(PS[:, 3072:3584], o_fm[:, 128 * s:128 * (s + 1)],
                          Wsh[:, 1024 * s:1024 * s + 512],
                          start=(s == 0), stop=(s == NS - 1), skip_group_check=True)
                te.matmul(PS[:, 3584:4096], o_fm[:, 128 * s:128 * (s + 1)],
                          Wsh[:, 1024 * s + 512:1024 * (s + 1)],
                          start=(s == 0), stop=(s == NS - 1), skip_group_check=True)
            tags[f"pe_wo{n}"] = inc(te.drain(), "pe")
        TE(te_wo)

        def ve_x1(v, n=n):
            if p4sub >= 4:
                w(v, "pe", tags[f"pe_wo{n}"])
            v.drain()
            if p4sub >= 4:
                v.tensor_tensor(stream[:, 1024 * n:1024 * (n + 1)],
                                stream[:, 1024 * n:1024 * (n + 1)],
                                PS[:, 3072:4096], ALU.add)
            tags[f"v_x1_{n}"] = inc(v.drain(), "dve")
        VE(ve_x1)

    barrier()
    if phases < 5:
        return debug_tail()

    STT_ = 4
    n_st = (NC + STT_ - 1) // STT_

    for st in range(n_st):
        tiles = list(range(STT_ * st, min(STT_ * (st + 1), NC)))

        for i, t in enumerate(tiles):
            def ln2_ve(v, t=t, i=i, st=st):
                if i == 0 and st > 0:
                    w(v, "dve", tags[f"v_y{STT_ * st - 1}"])
                if t > 0:
                    w(v, "dve", tags[f"v_q1{t - 1}"])
                v.reduce_sum(t1a[:], stream[:, 1024 * t:1024 * (t + 1)], axis=AX.X)
                v.drain()
                v.tensor_scalar(t1c[:], t1a[:], 1.0 / DM, None, ALU.mult)
                v.drain()
                v.tensor_scalar(xc[:], stream[:, 1024 * t:1024 * (t + 1)],
                                t1c[:], None, ALU.subtract)
                v.reduce_max(q5a[:], xc[:], axis=AX.X, apply_absolute_value=True)
                tags[f"v_l2a{t}"] = inc(v.drain(), "dve")
            VE(ln2_ve)

            def ln2_sc(sc, t=t):
                w(sc, "dve", tags[f"v_l2a{t}"])
                if t > 0:
                    w(sc, "dve", tags[f"v_q1{t - 1}"])
                    w(sc, "pe", tags[f"pe_q1{t - 1}_{NS - 1}"])
                sc.activation(f_bf[:], xc[:], AF.Square, accum_out=t1b[:])
                sc.activation(f_bf[:], xc[:], AF.Sign)
                tags[f"a_l2{t}"] = inc(sc.drain(), "act")
            SC(ln2_sc)

            def ln2_ve2(v, t=t):
                w(v, "act", tags[f"a_l2{t}"])
                v.tensor_scalar(t1d[:], t1b[:], 1.0 / DM, None, ALU.mult)
                tags[f"v_l2b{t}"] = inc(v.drain(), "dve")
            VE(ln2_ve2)

            def ln2_sc2(sc, t=t):
                w(sc, "dve", tags[f"v_l2b{t}"])
                sc.activation(t1a[:], t1d[:], AF.Ln, bias=consts[:, C_EPS:C_EPS + 1])
                sc.drain()
                sc.activation(t1d[:], t1a[:], AF.Exp, scale=-0.5)
                tags[f"a_l2b{t}"] = inc(sc.drain(), "act")
            SC(ln2_sc2)

            def ln2_ve3(v, t=t):
                w(v, "act", tags[f"a_l2b{t}"])
                v.tensor_tensor(q5b[:], q5a[:], t1d[:], ALU.mult)
                v.drain()
                v.tensor_scalar(q5b[:], q5b[:], 1e-5, None, ALU.max)
                v.drain()
                v.reciprocal(q5c[:], q5b[:])
                v.drain()
                v.tensor_tensor(q5d[:], q5c[:], t1d[:], ALU.mult)
                v.drain()
                v.tensor_scalar(q5d[:], q5d[:], 127.0, None, ALU.mult)
                v.scalar_tensor_tensor(cgv[:, t:t + 1], q5b[:], 1.0 / 127.0,
                                       mwth[:, 0:1], ALU.mult, ALU.mult)
                v.scalar_tensor_tensor(cuv[:, t:t + 1], q5b[:], 1.0 / 127.0,
                                       mwth[:, 1:2], ALU.mult, ALU.mult)
                tags[f"v_l2c{t}"] = inc(v.drain(), "dve")
            VE(ln2_ve3)

            def q1_sc(sc, t=t):
                w(sc, "dve", tags[f"v_l2c{t}"])
                sc.activation(f_tmp[:], xc[:], AF.Copy, scale=q5d[:, 0:1],
                              bias=MAGIC)
                sc.drain()
                sc.activation(xc[:], f_tmp[:], AF.Copy, bias=-MAGIC)
                tags[f"a_q1{t}"] = inc(sc.drain(), "act")
            SC(q1_sc)

            def q1_ve(v, t=t):
                w(v, "act", tags[f"a_q1{t}"])
                v.scalar_tensor_tensor(f_tmp[:], f_bf[:], DELTA, xc[:],
                                       ALU.mult, ALU.add)
                v.tensor_copy(xc[:].bitcast(I32), f_tmp[:])
                v.tensor_copy(f_bf[:], xc[:].bitcast(I32))
                tags[f"v_q1{t}"] = inc(v.drain(), "dve")
            VE(q1_ve)

            for s in range(NS):
                def q1_te(te, t=t, i=i, st=st, s=s):
                    gi = t * NS + s
                    if s == 0:
                        w(te, "dve", tags[f"v_q1{t}"])
                        if i == 0 and st > 0:
                            w(te, "dve", tags[f"v_y{STT_ * st - 1}"])
                    if s >= 2:
                        w(te, "dve", tags[f"v_q1e{t}_{s - 2}"])
                    slot = gi % 2
                    te.transpose(psb(QT[slot], QT[slot] + 64),
                                 f_bf[:, 128 * s:128 * (s + 1)], identb[:])
                    tags[f"pe_q1{t}_{s}"] = inc(te.drain(), "pe")
                TE(q1_te)

                def q1_ve2(v, t=t, i=i, s=s):
                    gi = t * NS + s
                    w(v, "pe", tags[f"pe_q1{t}_{s}"])
                    slot = gi % 2
                    v.tensor_copy(q1_fm[:, 512 * s + 128 * i:512 * s + 128 * (i + 1)],
                                  psb(QT[slot], QT[slot] + 64))
                    tags[f"v_q1e{t}_{s}"] = inc(v.drain(), "dve")
                VE(q1_ve2)

        for m8 in range(8):
            def gu_sy(sync, m8=m8, st=st):
                if m8 > 0:
                    w(sync, "pe", tags[f"pe_gu{st}_{m8 - 1}"])
                elif st > 0:
                    w(sync, "pe", tags[f"pe_dn{st - 1}_7"])
                w(sync, "gd", tags[f"gd_t{len(wslabs) - 1}"])
                tags[f"ld_tg{st}_{m8}"] = dinc(sync.dma_start(
                    out=tg_sb[:].rearrange("p (s n) -> p s n", s=NS),
                    in_=scr_g.ap()[:, 512 * m8:512 * (m8 + 1)].rearrange(
                        "(s p) n -> p s n", p=128)), "ld")
                tags[f"ld_tu{st}_{m8}"] = dinc(sync.dma_start(
                    out=tu_sb[:].rearrange("p (s n) -> p s n", s=NS),
                    in_=scr_u.ap()[:, 512 * m8:512 * (m8 + 1)].rearrange(
                        "(s p) n -> p s n", p=128)), "ld")
            SY(gu_sy)

            for i, t in enumerate(tiles):
                def gu_te(te, m8=m8, i=i, t=t, st=st, tiles=tiles):
                    if m8 == 0 and i == 0:
                        for t2 in tiles:
                            w(te, "dve", tags[f"v_q1e{t2}_{NS - 1}"])
                        if st > 0:
                            w(te, "dve", tags[f"v_y{STT_ * st - 1}"])
                    w(te, "ld", tags[f"ld_tu{st}_{m8}"])
                    if i >= 2:
                        w(te, "dve", tags.get(f"v_h{st}_{m8}_{tiles[i - 2]}"))
                    elif m8 > 0:
                        pi = tiles[i + 2] if i + 2 < len(tiles) else tiles[-1]
                        w(te, "dve", tags.get(f"v_h{st}_{m8 - 1}_{pi}"))
                    ga = 1024 * (i % 2)
                    gb = ga + 512
                    for s in range(NS):
                        lhs = q1_fm[:, 512 * s + 128 * i:512 * s + 128 * (i + 1)]
                        te.matmul(PS[:, ga:ga + 512], lhs,
                                  tg_sb[:, 512 * s:512 * (s + 1)],
                                  start=(s == 0), stop=(s == NS - 1),
                                  skip_group_check=True)
                    for s in range(NS):
                        lhs = q1_fm[:, 512 * s + 128 * i:512 * s + 128 * (i + 1)]
                        te.matmul(PS[:, gb:gb + 512], lhs,
                                  tu_sb[:, 512 * s:512 * (s + 1)],
                                  start=(s == 0), stop=(s == NS - 1),
                                  skip_group_check=True)
                    tags[f"pe_g{st}_{m8}_{t}"] = inc(te.drain(), "pe")
                    if i == len(tiles) - 1:
                        tags[f"pe_gu{st}_{m8}"] = tags[f"pe_g{st}_{m8}_{t}"]
                TE(gu_te)

                def gu_sc(sc, m8=m8, i=i, t=t, st=st, tiles=tiles):
                    w(sc, "pe", tags[f"pe_g{st}_{m8}_{t}"])
                    if i >= 2:
                        w(sc, "dve", tags.get(f"v_h{st}_{m8}_{tiles[i - 2]}"))
                    elif m8 > 0:
                        pi = tiles[i + 2] if i + 2 < len(tiles) else tiles[-1]
                        w(sc, "dve", tags.get(f"v_h{st}_{m8 - 1}_{pi}"))
                    ga = 1024 * (i % 2)
                    sc.activation(xc[:, 512 * (i % 2):512 * (i % 2) + 512],
                                  PS[:, ga:ga + 512], AF.Silu, scale=cgv[:, t:t + 1])
                    tags[f"a_h{st}_{m8}_{t}"] = inc(sc.drain(), "act")
                SC(gu_sc)

                def gu_ve(v, m8=m8, i=i, t=t, st=st):
                    w(v, "act", tags[f"a_h{st}_{m8}_{t}"])
                    gb = 1024 * (i % 2) + 512
                    v.scalar_tensor_tensor(
                        h_arena[:, INNER * i + 512 * m8:INNER * i + 512 * (m8 + 1)],
                        PS[:, gb:gb + 512], cuv[:, t:t + 1],
                        xc[:, 512 * (i % 2):512 * (i % 2) + 512], ALU.mult, ALU.mult)
                    tags[f"v_h{st}_{m8}_{t}"] = inc(v.drain(), "dve")
                VE(gu_ve)

        for i, t in enumerate(tiles):
            def hq_ve(v, i=i, t=t, st=st):
                w(v, "dve", tags[f"v_h{st}_7_{t}"])
                if t > 0:
                    w(v, "act", tags.get(f"a_hz{t - 1}_3"))
                v.reduce_max(q5a[:], h_arena[:, INNER * i:INNER * (i + 1)],
                             axis=AX.X, apply_absolute_value=True)
                v.drain()
                v.tensor_scalar(q5b[:], q5a[:], 1e-5, None, ALU.max)
                v.drain()
                v.reciprocal(q5c[:], q5b[:])
                v.drain()
                v.tensor_scalar(q5d[:], q5c[:], 127.0, None, ALU.mult)
                v.scalar_tensor_tensor(cdv[:, t:t + 1], q5b[:], 1.0 / 127.0,
                                       mwth[:, 2:3], ALU.mult, ALU.mult)
                tags[f"v_hq{t}"] = inc(v.drain(), "dve")
            VE(hq_ve)

            for qr in range(4):
                def hqq_sc(sc, i=i, t=t, qr=qr):
                    hsl = h_arena[:, INNER * i + 1024 * qr:INNER * i + 1024 * (qr + 1)]
                    if qr == 0:
                        w(sc, "dve", tags[f"v_hq{t}"])
                    else:
                        w(sc, "dve", tags[f"v_hqq{t}_{qr - 1}"])
                    sc.activation(f_bf[:], hsl, AF.Sign)
                    sc.activation(f_tmp[:], hsl, AF.Copy, scale=q5d[:, 0:1],
                                  bias=MAGIC)
                    sc.drain()
                    sc.activation(xc[:], f_tmp[:], AF.Copy, bias=-MAGIC)
                    tags[f"a_hz{t}_{qr}"] = inc(sc.drain(), "act")
                SC(hqq_sc)

                def hqq_ve(v, i=i, t=t, qr=qr):
                    w(v, "act", tags[f"a_hz{t}_{qr}"])
                    if qr == 0 and t > 0:
                        w(v, "pe", tags.get(f"pe_hf{t - 1}_31"))
                    v.scalar_tensor_tensor(f_tmp[:], f_bf[:], DELTA, xc[:],
                                           ALU.mult, ALU.add)
                    v.tensor_copy(xc[:].bitcast(I32), f_tmp[:])
                    v.tensor_copy(hq_tm[:, 1024 * qr:1024 * (qr + 1)],
                                  xc[:].bitcast(I32))
                    tags[f"v_hqq{t}_{qr}"] = inc(v.drain(), "dve")
                VE(hqq_ve)

            for s2 in range(32):
                def hq_te(te, i=i, t=t, s2=s2):
                    if s2 == 0:
                        w(te, "dve", tags[f"v_hqq{t}_3"])
                    if s2 >= 2:
                        w(te, "dve", tags[f"v_hf{t}_{s2 - 2}"])
                    slot = s2 % 2
                    te.transpose(psb(QT[slot], QT[slot] + 64),
                                 hq_tm[:, 128 * s2:128 * (s2 + 1)], identb[:])
                    tags[f"pe_hf{t}_{s2}"] = inc(te.drain(), "pe")
                TE(hq_te)

                def hq_ve3(v, i=i, t=t, s2=s2):
                    hq_fm = h_arena[:, INNER * i:INNER * i + 2048].bitcast(BF16)
                    w(v, "pe", tags[f"pe_hf{t}_{s2}"])
                    slot = s2 % 2
                    v.tensor_copy(hq_fm[:, 128 * s2:128 * (s2 + 1)],
                                  psb(QT[slot], QT[slot] + 64))
                    tags[f"v_hf{t}_{s2}"] = inc(v.drain(), "dve")
                VE(hq_ve3)

        for m8 in range(8):
            def dn_sy(sync, m8=m8, st=st):
                if m8 > 0:
                    w(sync, "pe", tags[f"pe_dn{st}_{m8 - 1}"])
                else:
                    w(sync, "pe", tags[f"pe_gu{st}_7"])
                tags[f"ld_td{st}_{m8}"] = dinc(sync.dma_start(
                    out=td_sb[:].rearrange("p (s n) -> p s n", s=4),
                    in_=scr_d.ap()[512 * m8:512 * (m8 + 1), :].rearrange(
                        "(s p) n -> p s n", p=128)), "ld")
            SY(dn_sy)

            def dn_te(te, m8=m8, st=st, tiles=tiles):
                w(te, "ld", tags[f"ld_td{st}_{m8}"])
                if m8 == 0:
                    for t2 in tiles:
                        w(te, "dve", tags[f"v_hf{t2}_31"])
                        w(te, "dve", tags[f"v_h{st}_7_{t2}"])
                for i2, t2 in enumerate(tiles):
                    hq_fm = h_arena[:, INNER * i2:INNER * i2 + 2048].bitcast(BF16)
                    for o2 in range(2):
                        for s4 in range(4):
                            s2 = 4 * m8 + s4
                            te.matmul(
                                PS[:, 1024 * i2 + 512 * o2:1024 * i2 + 512 * (o2 + 1)],
                                hq_fm[:, 128 * s2:128 * (s2 + 1)],
                                td_sb[:, 1024 * s4 + 512 * o2:1024 * s4 + 512 * (o2 + 1)],
                                start=(m8 == 0 and s4 == 0),
                                stop=(m8 == 7 and s4 == 3),
                                skip_group_check=True)
                tags[f"pe_dn{st}_{m8}"] = inc(te.drain(), "pe")
            TE(dn_te)

        for i, t in enumerate(tiles):
            def y_ve(v, i=i, t=t, st=st):
                w(v, "pe", tags[f"pe_dn{st}_7"])
                w(v, "st", tags.get(f"st_y{t - 1}"))
                for o2 in range(2):
                    v.scalar_tensor_tensor(
                        y_sb[:, 512 * o2:512 * (o2 + 1)],
                        PS[:, 1024 * i + 512 * o2:1024 * i + 512 * (o2 + 1)],
                        cdv[:, t:t + 1],
                        stream[:, 1024 * t + 512 * o2:1024 * t + 512 * (o2 + 1)],
                        ALU.mult, ALU.add)
                tags[f"v_y{t}"] = inc(v.drain(), "dve")
            VE(y_ve)

            def y_gp(gp, i=i, t=t):
                w(gp, "dve", tags[f"v_y{t}"])
                tags[f"st_y{t}"] = dinc(gp.dma_start(
                    out=dy[128 * t:128 * (t + 1), :], in_=y_sb[:]), "st")
            GP(y_gp)

    def fin(gp):
        gp.wait_ge(sem["st"], cnt["st"])
    GP(fin)

    ctx.close()
    return nc



def _host_consts():
    c = np.zeros((128, 16), np.float32)
    c[:, C_EPS] = EPS
    c[:, C_MAGIC] = MAGIC
    c[:, C_NMAGIC] = -MAGIC
    c[:, C_ONE] = 1.0
    return c


def _selector_consts():
    SELHv = np.zeros((128, H * CK), np.float32)
    for h in range(H):
        SELHv[h, 128 * h:128 * (h + 1)] = 1.0
    QCv = np.zeros((16, 128), np.float32)
    for h in range(16):
        QCv[h, 64 * (h % 2):64 * (h % 2) + 64] = 1.0
    M8v = np.zeros((16, 8), np.float32)
    for h in range(16):
        M8v[h, h // 2] = 1.0
    S16v = np.zeros((128, NS * 128), np.float32)
    for s in range(NS):
        for p in range(128):
            S16v[2 * s + p // 64, 128 * s + p] = 1.0
    maskv = np.full((128, 128), -1e9, np.float32)
    for k in range(128):
        maskv[k, k + 1:] = 0.0
    return SELHv, QCv, M8v, S16v.astype(ml_dtypes.bfloat16), maskv


def make_in_maps(inputs, NC=16, n_cores=8):
    x = np.asarray(inputs["L0_attn_out"], np.float32)
    B = x.shape[0]
    spb = n_cores // B
    TPC = NC * CK
    WwT = np.ascontiguousarray(np.asarray(inputs["L0_attn_W_write"], np.float32).T)
    WgbaT = np.ascontiguousarray(np.concatenate(
        [np.asarray(inputs["L0_attn_W_gate"], np.float32),
         np.asarray(inputs["L0_attn_W_beta"], np.float32),
         np.asarray(inputs["L0_attn_W_alpha"], np.float32)], axis=0).T)
    WoT = np.ascontiguousarray(np.asarray(inputs["L0_attn_W_out"], np.float32).T)
    wgT = np.ascontiguousarray(np.asarray(inputs["L0_ffn_w_gate"], np.float32).T)
    wuT = np.ascontiguousarray(np.asarray(inputs["L0_ffn_w_up"], np.float32).T)
    wdT = np.ascontiguousarray(np.asarray(inputs["L0_ffn_w_down"], np.float32).T)
    a16v = np.ascontiguousarray(
        np.asarray(inputs["L0_attn_A_log"], np.float32).reshape(16, 1))
    dtbv = np.ascontiguousarray(
        np.asarray(inputs["L0_attn_dt_bias"], np.float32).reshape(16, 1))
    SELHv, QCv, M8v, S16v, maskv = _selector_consts()
    common = dict(
        WwT=WwT, WgbaT=WgbaT, WoT=WoT, wgT=wgT, wuT=wuT, wdT=wdT,
        a16=a16v, dtb16=dtbv, consts=_host_consts(),
        identb=np.eye(128, dtype=ml_dtypes.bfloat16),
        identf=np.eye(128, dtype=np.float32),
        maskst=maskv, SELH=SELHv, QC32=QCv, M8c=M8v, SEL16b=S16v,
        ones1f=np.ones((1, 128), np.float32),
        ones128=np.ones((128, 1), np.float32),
    )
    in_maps = []
    for c in range(n_cores):
        b, s = c // spb, c % spb
        x_sl = np.ascontiguousarray(x[b, s * TPC:(s + 1) * TPC])
        x_prev = (np.zeros((1, DM), np.float32) if s == 0
                  else np.ascontiguousarray(x[b, s * TPC - 1:s * TPC]))
        mprev = np.zeros((1, 8), np.float32)
        for j in range(n_cores):
            if j // spb == b and j % spb < s:
                mprev[0, j] = 1.0
        m = dict(common)
        m.update(x_sl=x_sl, x_prev=x_prev,
                 mprev16=np.ascontiguousarray(np.tile(mprev, (16, 1))))
        in_maps.append(m)
    return in_maps


def kernel(**inputs):
    x = np.asarray(inputs["L0_attn_out"])
    B, T_full, _ = x.shape
    n_cores = 8
    NC = (B * T_full) // (n_cores * CK)
    nc = build_program(NC)
    in_maps = make_in_maps(inputs, NC=NC, n_cores=n_cores)
    res = run_bass_kernel_spmd(nc, in_maps, core_ids=list(range(n_cores)))
    TPC = NC * CK
    out = np.empty((B, T_full, DM), np.float32)
    spb = n_cores // B
    for c in range(n_cores):
        b, s = c // spb, c % spb
        out[b, s * TPC:(s + 1) * TPC] = res.results[c]["y_sl"]
    return out


# version 04
# speedup vs baseline: 1.2089x; 1.0356x over previous
import numpy as np
import ml_dtypes
import concourse.bass as bass
import concourse.mybir as mybir
from concourse.bass_utils import run_bass_kernel_spmd

F32 = mybir.dt.float32
BF16 = mybir.dt.bfloat16
I32 = mybir.dt.int32
AF = mybir.ActivationFunctionType
ALU = mybir.AluOpType
AX = mybir.AxisListType

DM = 1024
H = 16
CK = 128
NS = 8
INNER = 4096
MAGIC = 12582912.0
DELTA = 0.4995
EPS = 1e-5
C_EPS, C_MAGIC, C_NMAGIC, C_ONE = 0, 1, 2, 3
SLABW = 2052


def build_program(NC=16, phases=5, w1=True, p4sub=99):
    import os
    G = lambda k: os.environ.get(k, "1") == "1"
    nc = bass.Bass("TRN2", target_bir_lowering=False, detect_race_conditions=False)
    T = NC * CK

    dx = nc.dram_tensor("x_sl", [T, DM], F32, kind="ExternalInput").ap()
    dxprev = nc.dram_tensor("x_prev", [1, DM], F32, kind="ExternalInput").ap()
    dWw = nc.dram_tensor("WwT", [DM, DM], F32, kind="ExternalInput").ap()
    dWgba = nc.dram_tensor("WgbaT", [DM, 48], F32, kind="ExternalInput").ap()
    dWo = nc.dram_tensor("WoT", [DM, DM], F32, kind="ExternalInput").ap()
    dwg = nc.dram_tensor("wgT", [DM, INNER], F32, kind="ExternalInput").ap()
    dwu = nc.dram_tensor("wuT", [DM, INNER], F32, kind="ExternalInput").ap()
    dwd = nc.dram_tensor("wdT", [INNER, DM], F32, kind="ExternalInput").ap()
    da16 = nc.dram_tensor("a16", [16, 1], F32, kind="ExternalInput").ap()
    ddtb = nc.dram_tensor("dtb16", [16, 1], F32, kind="ExternalInput").ap()
    dconsts = nc.dram_tensor("consts", [128, 16], F32, kind="ExternalInput").ap()
    dident = nc.dram_tensor("identb", [128, 128], BF16, kind="ExternalInput").ap()
    didentf = nc.dram_tensor("identf", [128, 128], F32, kind="ExternalInput").ap()
    dmask = nc.dram_tensor("maskst", [128, 128], F32, kind="ExternalInput").ap()
    dSELH = nc.dram_tensor("SELH", [128, H * CK], F32, kind="ExternalInput").ap()
    dQC = nc.dram_tensor("QC32", [16, 128], F32, kind="ExternalInput").ap()
    dM8 = nc.dram_tensor("M8c", [16, 8], F32, kind="ExternalInput").ap()
    dSEL16 = nc.dram_tensor("SEL16b", [128, NS * 128], BF16, kind="ExternalInput").ap()
    dones1 = nc.dram_tensor("ones1f", [1, 128], F32, kind="ExternalInput").ap()
    dones128 = nc.dram_tensor("ones128", [128, 1], F32, kind="ExternalInput").ap()
    dmprev = nc.dram_tensor("mprev16", [16, 8], F32, kind="ExternalInput").ap()
    dy = nc.dram_tensor("y_sl", [T, DM], F32, kind="ExternalOutput").ap()

    scr_g = nc.dram_tensor("scr_g", [DM, INNER], BF16, kind="Internal")
    scr_u = nc.dram_tensor("scr_u", [DM, INNER], BF16, kind="Internal")
    scr_d = nc.dram_tensor("scr_d", [INNER, DM], BF16, kind="Internal")
    GN = 128 * 512 + 16
    gin = nc.dram_tensor("gin", [1, GN], F32, kind="Internal")
    gout = nc.dram_tensor("gout", [8, GN], F32, kind="Internal")

    base0 = (nc.sbuf_base + 63) & ~63
    cur = [base0]

    def alo(name, shape, dtype, at=None):
        if at is None:
            at = cur[0]
            nbytes = int(np.prod(shape[1:])) * mybir.dt.size(dtype)
            cur[0] = (at + nbytes + 63) & ~63
        return nc.alloc_sbuf_tensor_at(name, shape, dtype, offset=at)

    stream = alo("stream", [128, NC * 1024], F32)
    consts = alo("consts_sb", [128, 16], F32)
    gate_tm = alo("gate_tm", [128, NC * 16], F32)
    cgv = alo("cgv", [128, NC], F32)
    cuv = alo("cuv", [128, NC], F32)
    cdv = alo("cdv", [128, NC], F32)
    mwth = alo("mwth", [128, 8], F32)
    nthv = alo("nthv", [128, 4], F32)
    identb = alo("identb_sb", [128, 128], BF16)
    identf = alo("identf_sb", [128, 128], F32)
    QC32 = alo("QC32_sb", [16, 128], F32)
    M8c = alo("M8c_sb", [16, 8], F32)
    ones1f = alo("ones1f_sb", [1, 128], F32)
    ones128 = alo("ones128_sb", [128, 1], F32)
    mprev16 = alo("mprev_sb", [16, 8], F32)
    a16 = alo("a16_sb", [16, 1], F32)
    dtb16 = alo("dtb16_sb", [16, 1], F32)
    negeA = alo("negeA", [16, 1], F32)
    wabs = alo("wabs", [128, 192], F32)
    wsum3 = alo("wsum3", [128, 4], F32)
    t1a = alo("t1a", [128, 1], F32)
    t1b = alo("t1b", [128, 1], F32)
    t1c = alo("t1c", [128, 1], F32)
    t1d = alo("t1d", [128, 1], F32)
    tp1 = alo("tp1", [1, 8], F32)

    attn_base = cur[0]
    xn_fm = alo("xn_fm", [128, NS * SLABW], BF16)
    vb_tm = alo("vb_tm", [128, NC * 1024], BF16)
    states = alo("states", [128, NC * 512], BF16)
    dec_fm = alo("dec_fm", [128, NC * CK], F32)
    Wsh = alo("Wsh", [128, NS * 1024], BF16)
    Wgb = alo("Wgb", [128, NS * 48], BF16)
    maskst = nc.alloc_sbuf_tensor_at("maskst_sb", [128, 128], F32,
                                     offset=nc.lookup_mloc(Wgb).addr)
    S_run = alo("S_run", [128, 2 * 512], F32)
    Sg = nc.alloc_sbuf_tensor_at("Sg", [128, 512], F32,
                                 offset=nc.lookup_mloc(S_run).addr)
    if NC % 2 == 0:
        SEL16 = nc.alloc_sbuf_tensor_at("SEL16_sb", [128, NS * 128], BF16,
                                        offset=nc.lookup_mloc(S_run).addr + 2048)
    else:
        SEL16 = alo("SEL16_sb", [128, NS * 128], BF16)
    S0 = alo("S0", [128, 2 * 512], F32)
    t16 = alo("t16", [16, 1024], F32)
    xprev = nc.alloc_sbuf_tensor_at("xprev", [1, DM], F32,
                                    offset=nc.lookup_mloc(t16).addr)
    declast = alo("declast", [16, NC], F32)
    pn16 = alo("pn16", [16, NC], F32)
    DgT = alo("DgT", [16, 8], F32)
    G16 = alo("G16", [16, 8], F32)
    P16 = alo("P16", [16, 8], F32)
    cj16 = alo("cj16", [16, 8], F32)
    R16 = alo("R16", [16, 8], F32)

    pool_base = cur[0]
    wk_tm = alo("wk_tm", [128, 1024], BF16)
    vbdw = alo("vbdw", [128, 1024], BF16)
    xn_tmp = alo("xn_tmp", [128, 1024], F32)
    xnb = alo("xnb", [128, 1024], BF16)
    beta_t = alo("beta_t", [128, 16], F32)
    apre = alo("apre", [128, 16], F32)
    dwT = alo("dwT", [128, 16], F32)
    Earr = alo("Earr", [128, 8], F32)
    p2_end = cur[0]
    cur[0] = pool_base
    intra_sb = alo("intra_sb", [128, H * CK], BF16)
    Lh_sb = alo("Lh_sb", [128, 2 * 128], F32)
    Lraw = alo("Lraw", [128, 2 * 128], F32)
    o_tm = alo("o_tm", [128, 1024], BF16)
    o_fm = alo("o_fm", [128, 1024], BF16)
    rkdx = alo("rkdx", [128, NS * 128], BF16)
    Strue = alo("Strue", [128, 512], BF16)
    negdecT = alo("negdecT", [128, 16], F32)
    pnb = alo("pnb", [128, 8], F32)
    dexpb = alo("dexpb", [128, 128], BF16)
    cbj = alo("cbj", [128, 8], F32)
    p4_end = cur[0]

    cur[0] = attn_base
    h_arena = alo("h_arena", [128, 4 * INNER], F32)
    hq_tm = alo("hq_tm", [128, INNER], BF16)
    q1_fm = alo("q1_fm", [128, NS * 512], BF16)
    tg_sb = alo("tg_sb", [128, NS * 512], BF16)
    tu_sb = alo("tu_sb", [128, NS * 512], BF16)
    td_sb = nc.alloc_sbuf_tensor_at("td_sb", [128, 4 * 1024], BF16,
                                    offset=nc.lookup_mloc(tg_sb).addr)
    td_sb2 = nc.alloc_sbuf_tensor_at("td_sb2", [128, 4 * 1024], BF16,
                                     offset=nc.lookup_mloc(tu_sb).addr)
    y_sb = alo("y_sb", [128, 1024], F32)
    xc = alo("xc", [128, 1024], F32)
    f_tmp = alo("f_tmp", [128, 1024], F32)
    f_bf = alo("f_bf", [128, 1024], BF16)
    q5a = alo("q5a", [128, 1], F32)
    q5b = alo("q5b", [128, 1], F32)
    q5c = alo("q5c", [128, 1], F32)
    q5d = alo("q5d", [128, 1], F32)
    ffn_end = cur[0]

    cur[0] = max(p2_end, p4_end, ffn_end)
    stag = alo("stag", [128, 2 * 512], F32)
    ternst = alo("ternst", [128, 2 * 512], BF16)
    SELH = nc.alloc_sbuf_tensor_at("SELH_sb", [128, H * CK], F32,
                                   offset=nc.lookup_mloc(stag).addr)
    assert cur[0] <= nc.sbuf_top, f"SBUF over: {cur[0]} > {nc.sbuf_top}"

    nc.const_aps.register(F32, MAGIC, consts[:, C_MAGIC:C_MAGIC + 1])
    nc.const_aps.register(F32, -MAGIC, consts[:, C_NMAGIC:C_NMAGIC + 1])
    PS = nc.alloc_psum_tensor("PSA", [128, 4096], F32)

    def psb(a, b):
        return PS[:, a:b].bitcast(BF16)

    XT = (1792, 3872)
    WKT = (1664, 3584)
    OTT = (1664, 0)
    QT = (2048, 2560)

    from contextlib import ExitStack
    ctx = ExitStack()
    sem = {}
    for s in ("ld", "ldc", "gd", "pe", "dve", "act", "gpc", "cc", "st", "ph"):
        sem[s] = ctx.enter_context(nc.semaphore("s_" + s))
    block = ctx.enter_context(nc.Block())
    cnt = {k: 0 for k in sem}
    tags = {}

    def inc(instr, name, n=1):
        instr.then_inc(sem[name], n)
        cnt[name] += n
        return cnt[name]

    def dinc(instr, name):
        return inc(instr, name, 16)

    def w(eng, name, tag):
        if tag is not None:
            eng.wait_ge(sem[name], tag)

    _pe_last = [None]

    class _TEW:
        def __init__(self, e):
            self._e = e

        def matmul(self, *a, **k):
            r = self._e.matmul(*a, **k)
            _pe_last[0] = r
            return r

        def transpose(self, *a, **k):
            r = self._e.transpose(*a, **k)
            _pe_last[0] = None
            return r

        def drain(self):
            if _pe_last[0] is None:
                return self._e.drain()
            r = _pe_last[0]
            _pe_last[0] = None
            return r

        def __getattr__(self, k):
            return getattr(self._e, k)

    SY, VE, SC, GP = block.sync, block.vector, block.scalar, block.gpsimd
    TE = lambda f: block.tensor(lambda te: f(_TEW(te)))

    def barrier():
        target = cnt["ph"] + 5
        for s in (SY, VE, SC, TE, GP):
            def f(eng, target=target):
                eng.sem_inc(sem["ph"], 1)
                eng.wait_ge(sem["ph"], target)
            s(f)
        cnt["ph"] = target

    def p0(sync):
        for name, dst, src in (
            ("consts", consts, dconsts), ("identb", identb, dident),
            ("identf", identf, didentf),
            ("QC32", QC32, dQC), ("M8c", M8c, dM8),
            ("ones1f", ones1f, dones1),
            ("ones128", ones128, dones128), ("mprev", mprev16, dmprev),
            ("a16", a16, da16), ("dtb16", dtb16, ddtb), ("xprev", xprev, dxprev),
        ):
            tags["ld_" + name] = dinc(sync.dma_start(out=dst[:], in_=src), "ldc")
    SY(p0)
    P0C = cnt["ldc"]

    def p0g(gp):
        tags["gd_Ww"] = dinc(gp.dma_start(
            out=Wsh[:].rearrange("p (s n) -> p s n", s=NS),
            in_=dWw.rearrange("(s p) n -> p s n", p=128)), "gd")
        tags["gd_Wgb"] = dinc(gp.dma_start(
            out=Wgb[:].rearrange("p (s n) -> p s n", s=NS),
            in_=dWgba.rearrange("(s p) n -> p s n", p=128)), "gd")
    GP(p0g)

    def p0v(v):
        v.memset(dexpb[:], 0.0)
        v.memset(S_run[:], 0.0)
        v.memset(S0[:], 0.0)
        if os.environ.get("DBG", "0") == "1":
            for tns in (xn_fm, vb_tm, states, rkdx, Strue, o_tm, o_fm,
                        intra_sb, negdecT, pnb, Lh_sb, Lraw, gate_tm,
                        declast, pn16, DgT, cj16):
                v.memset(tns[:], 0.0)
        tags["z_dec"] = inc(v.memset(dec_fm[:], 0.0), "dve")
    VE(p0v)

    def p0s(sc):
        w(sc, "ldc", P0C)
        sc.activation(negeA[:], a16[:], AF.Exp)
        sc.activation(xnb[0:1, :], xprev[:], AF.Square, accum_out=tp1[:, 1:2])
        tags["a_prev"] = inc(sc.drain(), "act")
    SC(p0s)

    def p0v2(v):
        w(v, "act", tags["a_prev"])
        v.drain()
        v.tensor_scalar(negeA[:], negeA[:], -1.0, None, ALU.mult)
        v.reduce_sum(tp1[:, 0:1], xprev[:], axis=AX.X)
        v.drain()
        v.tensor_scalar(tp1[:, 2:3], tp1[:, 0:1], 1.0 / DM, None, ALU.mult)
        v.drain()
        v.scalar_tensor_tensor(tp1[:, 3:4], tp1[:, 2:3], -1.0, tp1[:, 2:3], ALU.mult, ALU.mult)
        v.drain()
        v.scalar_tensor_tensor(tp1[:, 4:5], tp1[:, 1:2], 1.0 / DM, tp1[:, 3:4], ALU.mult, ALU.add)
        tags["v_prev1"] = inc(v.drain(), "dve")
    VE(p0v2)

    def p0s2(sc):
        w(sc, "dve", tags["v_prev1"])
        w(sc, "ldc", P0C)
        sc.activation(tp1[:, 5:6], tp1[:, 4:5], AF.Ln, bias=consts[0:1, C_EPS:C_EPS + 1])
        sc.drain()
        tags["a_prev2"] = inc(
            sc.activation(tp1[:, 6:7], tp1[:, 5:6], AF.Exp, scale=-0.5), "act")
    SC(p0s2)

    def p0v3(v):
        w(v, "act", tags["a_prev2"])
        v.drain()
        v.tensor_scalar(xprev[:], xprev[:], tp1[:, 2:3], tp1[:, 6:7], ALU.subtract, ALU.mult)
        v.drain()
        tags["v_prev2"] = inc(v.tensor_copy(xnb[0:1, :], xprev[:]), "dve")
        if os.environ.get("DBG", "0") == "1":
            v.memset(t16[:], 0.0)
    VE(p0v3)

    for s in range(NS):
        def p0t(te, s=s):
            if s == 0:
                w(te, "dve", tags["v_prev2"])
                w(te, "ldc", P0C)
            if s >= 2:
                w(te, "dve", tags[f"v_prev3_{s - 2}"])
            slot = s % 2
            te.transpose(psb(XT[slot], XT[slot] + 64)[:, 0:1],
                         xnb[0:1, 128 * s:128 * (s + 1)], identb[0:1, 0:1])
            tags[f"pe_prev{s}"] = inc(te.drain(), "pe")
        TE(p0t)

        def p0v4(v, s=s):
            w(v, "pe", tags[f"pe_prev{s}"])
            slot = s % 2
            v.tensor_copy(xn_fm[:, SLABW * s:SLABW * s + 1],
                          psb(XT[slot], XT[slot] + 64)[:, 0:1])
            tags[f"v_prev3_{s}"] = inc(v.drain(), "dve")
        VE(p0v4)
    tags["v_prev3"] = tags[f"v_prev3_{NS - 1}"]

    wslabs = []
    for mi, dsrc in enumerate((dwg, dwu)):
        for a in range(8):
            for b in range(8):
                wslabs.append((dsrc[128 * a:128 * (a + 1), 512 * b:512 * (b + 1)],
                               64 * mi + 8 * a + b, mi))
    for a in range(32):
        for b in range(2):
            wslabs.append((dwd[128 * a:128 * (a + 1), 512 * b:512 * (b + 1)],
                           128 + 2 * a + b, 2))

    def scr_dst(k):
        _, idx, mi = wslabs[k]
        if mi == 0:
            r = idx
            return scr_g.ap()[128 * (r // 8):128 * (r // 8 + 1),
                              512 * (r % 8):512 * (r % 8 + 1)]
        if mi == 1:
            r = idx - 64
            return scr_u.ap()[128 * (r // 8):128 * (r // 8 + 1),
                              512 * (r % 8):512 * (r % 8 + 1)]
        r = idx - 128
        return scr_d.ap()[128 * (r // 2):128 * (r // 2 + 1),
                          512 * (r % 2):512 * (r % 2 + 1)]

    wlA = [0]
    wlB = [0]

    def p1_w_loads(sync, budget):
        for _ in range(budget):
            k = 2 * wlA[0]
            if k >= len(wslabs):
                return
            w(sync, "act", tags.get(f"a_w{k - 2}"))
            tags[f"ld_w{k}"] = dinc(sync.dma_start(
                out=stag[:, 0:512], in_=wslabs[k][0]), "ld")
            wlA[0] += 1

    def p1_w_loads_gp(gp, budget):
        for _ in range(budget):
            k = 2 * wlB[0] + 1
            if k >= len(wslabs):
                return
            w(gp, "act", tags.get(f"a_w{k - 2}"))
            tags[f"ld_w{k}"] = dinc(gp.dma_start(
                out=stag[:, 512:1024], in_=wslabs[k][0]), "gd")
            wlB[0] += 1

    wd_ = [0]

    def p1_w_abs(sc):
        while wd_[0] < len(wslabs) and f"ld_w{wd_[0]}" in tags:
            k = wd_[0]
            w(sc, "ld" if k % 2 == 0 else "gd", tags[f"ld_w{k}"])
            slot = k % 2
            _, idx, mi = wslabs[k]
            sc.activation(xnb[:, 0:512], stag[:, 512 * slot:512 * (slot + 1)],
                          AF.Abs, accum_out=wabs[:, idx:idx + 1])
            tags[f"a_w{k}"] = inc(sc.drain(), "act")
            wd_[0] += 1

    for t in range(NC):
        def p1_sy(sync, t=t):
            tags[f"ld_x{t}"] = dinc(sync.dma_start(
                out=stream[:, 1024 * t:1024 * (t + 1)],
                in_=dx[128 * t:128 * (t + 1), :]), "ld")
            if w1:
                p1_w_loads(sync, 3)
        SY(p1_sy)
        if w1:
            GP(lambda gp: p1_w_loads_gp(gp, 3))

        def p1_sc(sc, t=t):
            w(sc, "ld", tags[f"ld_x{t}"])
            if t > 0:
                w(sc, "dve", tags[f"v_st{t - 1}"])
            if t > 0:
                w(sc, "pe", tags[f"pe_xn{t - 1}_{NS - 1}"])
            else:
                w(sc, "pe", tags[f"pe_prev{NS - 1}"])
            sc.activation(xnb[:], stream[:, 1024 * t:1024 * (t + 1)],
                          AF.Square, accum_out=t1b[:])
            tags[f"a_sq{t}"] = inc(sc.drain(), "act")
            if w1:
                p1_w_abs(sc)
        SC(p1_sc)

        def p1_ve(v, t=t):
            w(v, "ld", tags[f"ld_x{t}"])
            v.reduce_sum(t1a[:], stream[:, 1024 * t:1024 * (t + 1)], axis=AX.X)
            w(v, "act", tags[f"a_sq{t}"])
            v.drain()
            v.tensor_scalar(t1c[:], t1a[:], 1.0 / DM, None, ALU.mult)
            v.drain()
            v.scalar_tensor_tensor(t1d[:], t1c[:], -1.0, t1c[:], ALU.mult, ALU.mult)
            v.drain()
            v.scalar_tensor_tensor(t1d[:], t1b[:], 1.0 / DM, t1d[:], ALU.mult, ALU.add)
            tags[f"v_st{t}"] = inc(v.drain(), "dve")
        VE(p1_ve)

        def p1_sc2(sc, t=t):
            w(sc, "dve", tags[f"v_st{t}"])
            sc.activation(t1a[:], t1d[:], AF.Ln, bias=consts[:, C_EPS:C_EPS + 1])
            sc.drain()
            tags[f"a_r{t}"] = inc(sc.activation(t1d[:], t1a[:], AF.Exp, scale=-0.5), "act")
        SC(p1_sc2)

        def p1_ve2(v, t=t):
            w(v, "act", tags[f"a_r{t}"])
            v.tensor_scalar(xn_tmp[:], stream[:, 1024 * t:1024 * (t + 1)],
                            t1c[:], t1d[:], ALU.subtract, ALU.mult)
            v.drain()
            v.tensor_tensor(stream[:, 1024 * t:1024 * (t + 1)],
                            stream[:, 1024 * t:1024 * (t + 1)], xn_tmp[:], ALU.add)
            if t > 0:
                w(v, "pe", tags[f"pe_xn{t - 1}_{NS - 1}"])
            tags[f"v_xn{t}"] = inc(v.tensor_copy(xnb[:], xn_tmp[:]), "dve")
        VE(p1_ve2)

        for s in range(NS):
            def p1_te(te, t=t, s=s):
                gi = t * NS + s
                if s == 0:
                    w(te, "dve", tags[f"v_xn{t}"])
                if gi >= 2:
                    w(te, "dve", tags.get(f"v_ev{gi - 2}"))
                slot = gi % 2
                te.transpose(psb(XT[slot], XT[slot] + 64),
                             xnb[:, 128 * s:128 * (s + 1)], identb[:])
                tags[f"pe_xn{t}_{s}"] = inc(te.drain(), "pe")
            TE(p1_te)

            def p1_ve3(v, t=t, s=s):
                gi = t * NS + s
                w(v, "pe", tags[f"pe_xn{t}_{s}"])
                slot = gi % 2
                v.tensor_copy(
                    xn_fm[:, SLABW * s + 1 + 128 * t:SLABW * s + 1 + 128 * (t + 1)],
                    psb(XT[slot], XT[slot] + 64))
                tags[f"v_ev{gi}"] = inc(v.drain(), "dve")
            VE(p1_ve3)

    def p1_w_guard(sc):
        w(sc, "pe", tags[f"pe_xn{NC - 1}_{NS - 1}"])
    if w1 and wd_[0] < len(wslabs):
        SC(p1_w_guard)
    while w1 and wd_[0] < len(wslabs):
        SY(lambda sync: p1_w_loads(sync, 3))
        GP(lambda gp: p1_w_loads_gp(gp, 3))
        SC(p1_w_abs)

    def p1f_ve(v):
        if not w1:
            v.memset(wabs[:], 0.02)
        else:
            w(v, "act", tags[f"a_w{len(wslabs) - 1}"])
        for m in range(3):
            v.reduce_sum(t1a[:], wabs[:, 64 * m:64 * (m + 1)], axis=AX.X)
            v.drain()
            v.tensor_copy(wsum3[:, m:m + 1], t1a[:])
        tags["v_wsum"] = inc(v.drain(), "dve")
    VE(p1f_ve)

    def p1f_te(te):
        w(te, "dve", tags["v_wsum"])
        for m in range(3):
            te.matmul(PS[0:1, 3968 + m:3969 + m], wsum3[:, m:m + 1],
                      ones128[:], start=True, stop=True)
        tags["pe_wsum"] = inc(te.drain(), "pe")
    TE(p1f_te)

    def p1f_ve2(v):
        w(v, "pe", tags["pe_wsum"])
        for m in range(3):
            v.tensor_scalar(t16[0:1, 40 + m:41 + m], PS[0:1, 3968 + m:3969 + m],
                            1.0 / (INNER * DM), 1e-5, ALU.mult, ALU.max)
        tags["v_wmean"] = inc(v.drain(), "dve")
    VE(p1f_ve2)

    def p1f_te2(te):
        w(te, "dve", tags["v_wmean"])
        for m in range(3):
            te.matmul(PS[:, 3972 + m:3973 + m], ones1f[:], t16[0:1, 40 + m:41 + m],
                      start=True, stop=True)
        tags["pe_wb"] = inc(te.drain(), "pe")
    TE(p1f_te2)

    def p1f_ve3(v):
        w(v, "pe", tags["pe_wb"])
        for m in range(3):
            v.tensor_copy(mwth[:, m:m + 1], PS[:, 3972 + m:3973 + m])
            v.tensor_scalar(mwth[:, 3 + m:4 + m], PS[:, 3972 + m:3973 + m],
                            0.5, None, ALU.mult)
            v.tensor_scalar(nthv[:, m:m + 1], PS[:, 3972 + m:3973 + m],
                            -0.5, None, ALU.mult)
        tags["v_wth"] = inc(v.drain(), "dve")
    VE(p1f_ve3)

    dbg_specs = []
    if os.environ.get("DBG", "0") == "1":
        for nm, src in (("S0", lambda: S0[:, 0:512]), ("cj16", lambda: cj16[:]),
                        ("DgT", lambda: DgT[:]), ("pn16", lambda: pn16[:]),
                        ("mprev", lambda: mprev16[:]), ("decl", lambda: declast[:]),
                        ("st0", lambda: states[:, 0:512]),
                        ("rkdx", lambda: rkdx[:]), ("Strue", lambda: Strue[:]),
                        ("otm", lambda: o_tm[:]), ("ofm", lambda: o_fm[:]),
                        ("intra", lambda: intra_sb[:]), ("dexpb", lambda: dexpb[:]),
                        ("ndT", lambda: negdecT[:]), ("pnb", lambda: pnb[:]),
                        ("decfm", lambda: dec_fm[:]), ("Lh", lambda: Lh_sb[:]),
                        ("Lraw", lambda: Lraw[:]), ("gatetm", lambda: gate_tm[:]),
                        ("vbtm", lambda: vb_tm[:]), ("xnfm", lambda: xn_fm[:]),
                        ("negeA", lambda: negeA[:]), ("a16d", lambda: a16[:]),
                        ("dtb", lambda: dtb16[:]),
                        ("t16d", lambda: t16[:])):
            dbg_specs.append((nm, src))

    def debug_tail():
        barrier()
        def dt_gp(gp):
            for t in range(NC):
                tags[f"st_dbg{t}"] = dinc(gp.dma_start(
                    out=dy[128 * t:128 * (t + 1), :],
                    in_=stream[:, 1024 * t:1024 * (t + 1)]), "st")
            for nm, src in dbg_specs:
                ap = src()
                d = nc.dram_tensor("dbg_" + nm, list(ap.shape), ap.dtype,
                                   kind="ExternalOutput").ap()
                dinc(gp.dma_start(out=d, in_=ap), "st")
            gp.wait_ge(sem["st"], cnt["st"])
        GP(dt_gp)
        ctx.close()
        return nc

    if phases < 2:
        return debug_tail()

    for n in range(NC):
        def te_v(te, n=n):
            if n == 0:
                w(te, "gd", tags["gd_Ww"])
                w(te, "gd", tags["gd_Wgb"])
                w(te, "dve", tags["v_prev3"])
            else:
                w(te, "dve", tags[f"v_vb{n - 1}"])
                w(te, "dve", tags[f"v_ap{n - 1}"])
                w(te, "act", tags[f"a_gb{n - 1}"])
            for s in range(NS):
                lhs = xn_fm[:, SLABW * s + 1 + 128 * n:SLABW * s + 1 + 128 * (n + 1)]
                te.matmul(PS[:, 0:512], lhs, Wsh[:, 1024 * s:1024 * s + 512],
                          start=(s == 0), stop=(s == NS - 1), skip_group_check=True)
                te.matmul(PS[:, 512:1024], lhs, Wsh[:, 1024 * s + 512:1024 * (s + 1)],
                          start=(s == 0), stop=(s == NS - 1), skip_group_check=True)
            tags[f"pe_v{n}"] = inc(te.drain(), "pe")
            for s in range(NS):
                lhs = xn_fm[:, SLABW * s + 1 + 128 * n:SLABW * s + 1 + 128 * (n + 1)]
                te.matmul(PS[:, 1024:1072], lhs, Wgb[:, 48 * s:48 * (s + 1)],
                          start=(s == 0), stop=(s == NS - 1), skip_group_check=True)
            tags[f"pe_gba{n}"] = inc(te.drain(), "pe")
        TE(te_v)

        def sc_g(sc, n=n):
            w(sc, "pe", tags[f"pe_gba{n}"])
            sc.activation(gate_tm[:, 16 * n:16 * (n + 1)], PS[:, 1024:1040], AF.Sigmoid)
            sc.activation(beta_t[:], PS[:, 1040:1056], AF.Sigmoid)
            tags[f"a_gb{n}"] = inc(sc.drain(), "act")
        SC(sc_g)

        def ve_a(v, n=n):
            w(v, "pe", tags[f"pe_gba{n}"])
            if n > 0:
                w(v, "pe", tags[f"pe_at{n - 1}"])
            v.tensor_copy(apre[:], PS[:, 1056:1072])
            tags[f"v_ap{n}"] = inc(v.drain(), "dve")
        VE(ve_a)

        def te_at(te, n=n):
            w(te, "dve", tags[f"v_ap{n}"])
            if n > 0:
                w(te, "dve", tags[f"v_z{n - 1}"])
            te.transpose(PS[0:16, 1536:1664], apre[:], identf[:])
            tags[f"pe_at{n}"] = inc(te.drain(), "pe")
        TE(te_at)

        def ve_z(v, n=n):
            w(v, "pe", tags[f"pe_at{n}"])
            v.tensor_scalar(t16[:, 0:128], PS[0:16, 1536:1664], dtb16[:, 0:1],
                            None, ALU.add)
            tags[f"v_z{n}"] = inc(v.drain(), "dve")
        VE(ve_z)

        def sc_sp(sc, n=n):
            w(sc, "dve", tags[f"v_z{n}"])
            sc.activation(t16[:, 128:256], t16[0:16, 0:128], AF.Abs)
            sc.drain()
            sc.activation(t16[:, 256:384], t16[0:16, 128:256], AF.Exp, scale=-1.0)
            sc.drain()
            sc.activation(t16[:, 384:512], t16[0:16, 256:384], AF.Ln,
                          bias=consts[0:16, C_ONE:C_ONE + 1])
            tags[f"a_sp{n}"] = inc(sc.drain(), "act")
        SC(sc_sp)

        def ve_dec(v, n=n):
            w(v, "act", tags[f"a_sp{n}"])
            if n == 0:
                w(v, "dve", tags["z_dec"])
            v.tensor_scalar(t16[:, 512:640], t16[0:16, 0:128], 0.0, None, ALU.max)
            v.drain()
            v.tensor_tensor(t16[:, 512:640], t16[0:16, 512:640],
                            t16[0:16, 384:512], ALU.add)
            v.drain()
            v.tensor_scalar(t16[:, 512:640], t16[0:16, 512:640],
                            negeA[:, 0:1], None, ALU.mult)
            v.drain()
            v.tensor_tensor_scan(dec_fm[0:16, CK * n:CK * (n + 1)],
                                 t16[0:16, 512:640], t16[0:16, 512:640],
                                 0.0, ALU.add, ALU.bypass)
            v.drain()
            v.tensor_copy(declast[:, n:n + 1], dec_fm[0:16, CK * n + 127:CK * n + 128])
            tags[f"v_dec{n}"] = inc(v.drain(), "dve")
        VE(ve_dec)

        def sc_dw(sc, n=n):
            w(sc, "dve", tags[f"v_dec{n}"])
            sc.activation(t16[:, 640:641], declast[0:16, n:n + 1], AF.Exp)
            sc.activation(t16[:, 644:772], dec_fm[0:16, CK * n:CK * (n + 1)],
                          AF.Exp, scale=-1.0, bias=declast[0:16, n:n + 1])
            tags[f"a_dw{n}"] = inc(sc.drain(), "act")
        SC(sc_dw)

        def ve_vb(v, n=n):
            w(v, "pe", tags[f"pe_v{n}"])
            w(v, "act", tags[f"a_gb{n}"])
            for h in range(H):
                v.tensor_scalar(vb_tm[:, 1024 * n + 64 * h:1024 * n + 64 * (h + 1)],
                                PS[:, 64 * h:64 * (h + 1)],
                                beta_t[:, h:h + 1], None, ALU.mult)
            tags[f"v_vb{n}"] = inc(v.drain(), "dve")
            w(v, "act", tags[f"a_dw{n}"])
            v.tensor_copy(t16[:, 772:836].bitcast(BF16), t16[0:16, 644:772])
            v.tensor_scalar(R16[:], M8c[:], t16[0:16, 640:641], None, ALU.mult)
            tags[f"v_dw{n}"] = inc(v.drain(), "dve")
        VE(ve_vb)

        def te_dwt(te, n=n):
            w(te, "dve", tags[f"v_dw{n}"])
            if n > 0:
                w(te, "dve", tags[f"v_dwt{n - 1}"])
            te.transpose(psb(1800, 1808)[:, 0:16], t16[0:16, 772:836].bitcast(BF16),
                         identb[0:16, 0:16])
            te.matmul(PS[:, 2560:2568], QC32[:], R16[:], start=True, stop=True,
                      skip_group_check=True)
            tags[f"pe_dwt{n}"] = inc(te.drain(), "pe")
        TE(te_dwt)

        def ve_vbdw(v, n=n):
            w(v, "pe", tags[f"pe_dwt{n}"])
            if n > 0:
                w(v, "pe", tags[f"pe_u{n - 1}_{NS - 1}"])
            v.tensor_copy(dwT[:], psb(1800, 1808)[:, 0:16])
            v.tensor_copy(Earr[:], PS[:, 2560:2568])
            tags[f"v_dwt{n}"] = inc(v.drain(), "dve")
            for h in range(H):
                v.tensor_scalar(vbdw[:, 64 * h:64 * (h + 1)],
                                vb_tm[:, 1024 * n + 64 * h:1024 * n + 64 * (h + 1)],
                                dwT[:, h:h + 1], None, ALU.mult)
            tags[f"v_vbdw{n}"] = inc(v.drain(), "dve")
        VE(ve_vbdw)

        def ve_snap(v, n=n):
            v.tensor_copy(states[:, 512 * n:512 * (n + 1)],
                          S_run[:, 512 * (n % 2):512 * (n % 2) + 512])
            tags[f"v_snap{n}"] = inc(v.drain(), "dve")
        VE(ve_snap)

        for s in range(NS):
            def te_wk(te, n=n, s=s):
                if s == 0:
                    w(te, "dve", tags[f"v_vbdw{n}"])
                if n > 0 and s == 0:
                    w(te, "pe", tags[f"pe_u{n - 1}_{NS - 1}"])
                if s >= 2:
                    w(te, "dve", tags[f"v_wkev{n}_{s - 2}"])
                slot = s % 2
                te.transpose(psb(WKT[slot], WKT[slot] + 64),
                             xn_fm[:, SLABW * s + 128 * n:SLABW * s + 128 * (n + 1)],
                             identb[:])
                tags[f"pe_wk{n}_{s}"] = inc(te.drain(), "pe")
            TE(te_wk)

            def ve_wkev(v, n=n, s=s):
                w(v, "pe", tags[f"pe_wk{n}_{s}"])
                slot = s % 2
                v.tensor_copy(wk_tm[:, 128 * s:128 * (s + 1)],
                              psb(WKT[slot], WKT[slot] + 64))
                tags[f"v_wkev{n}_{s}"] = inc(v.drain(), "dve")
            VE(ve_wkev)

            def te_u(te, n=n, s=s):
                w(te, "dve", tags[f"v_wkev{n}_{s}"])
                if s >= 2:
                    w(te, "dve", tags[f"v_su{n}_{s - 2}"])
                slot = s % 2
                te.matmul(PS[:, 2048 + 128 * slot:2176 + 128 * slot],
                          wk_tm[:, 128 * s:128 * (s + 1)],
                          vbdw[:, 128 * s:128 * (s + 1)],
                          start=True, stop=True, skip_group_check=True)
                tags[f"pe_u{n}_{s}"] = inc(te.drain(), "pe")
            TE(te_u)

            def ve_srun(v, n=n, s=s):
                if s == 0:
                    w(v, "dve", tags[f"v_snap{n}"])
                w(v, "pe", tags[f"pe_u{n}_{s}"])
                slot = s % 2
                bc = 2048 + 128 * slot
                rd, wr = 512 * (n % 2), 512 * ((n + 1) % 2)
                v.scalar_tensor_tensor(S_run[0:64, wr + 64 * s:wr + 64 * (s + 1)],
                                       S_run[0:64, rd + 64 * s:rd + 64 * (s + 1)],
                                       Earr[0:64, s:s + 1],
                                       PS[0:64, bc:bc + 64], ALU.mult, ALU.add)
                v.scalar_tensor_tensor(S_run[64:128, wr + 64 * s:wr + 64 * (s + 1)],
                                       S_run[64:128, rd + 64 * s:rd + 64 * (s + 1)],
                                       Earr[64:128, s:s + 1],
                                       PS[64:128, bc + 64:bc + 128], ALU.mult, ALU.add)
                tags[f"v_su{n}_{s}"] = inc(v.drain(), "dve")
            VE(ve_srun)

    barrier()

    def p3_ve(v):
        v.reduce_sum(t16[:, 900:901], declast[0:16, 0:NC], axis=AX.X)
        v.tensor_tensor_scan(t16[:, 904:904 + NC], declast[0:16, 0:NC],
                             declast[0:16, 0:NC], 0.0, ALU.add, ALU.bypass)
        v.drain()
        v.tensor_tensor(t16[:, 960:960 + NC], t16[0:16, 904:904 + NC],
                        declast[0:16, 0:NC], ALU.subtract)
        tags["v_p3a"] = inc(v.drain(), "dve")
    VE(p3_ve)

    def p3_sc(sc):
        w(sc, "dve", tags["v_p3a"])
        sc.activation(t16[:, 940:941], t16[0:16, 900:901], AF.Exp)
        sc.activation(pn16[:, 0:NC], t16[0:16, 960:960 + NC], AF.Exp)
        tags["a_p3"] = inc(sc.drain(), "act")
    SC(p3_sc)

    def p3_gp(gp):
        w(gp, "act", tags["a_p3"])
        tags["gd_g1"] = dinc(gp.dma_start(
            out=gin.ap()[0:1, 0:65536].rearrange("a (p c) -> (a p) c", p=128),
            in_=S_run[:, 512 * (NC % 2):512 * (NC % 2) + 512]), "gd")
        tags["gd_g2"] = dinc(gp.dma_start(
            out=gin.ap()[0:1, 65536:65552].rearrange("a (p b) -> (a p) b", p=16),
            in_=t16[0:16, 940:941]), "gd")
        gp.wait_ge(sem["gd"], tags["gd_g2"])
        tags["cc"] = inc(gp.collective_compute(
            "AllGather", ALU.bypass, replica_groups=[list(range(8))],
            ins=[gin.ap()], outs=[gout.ap()]), "cc")
    GP(p3_gp)

    for k in range(len(wslabs)):
        def p3w_sy(sync, k=k):
            slot = k % 2
            w(sync, "gd", tags.get(f"gd_t{k - 2}"))
            tags[f"ld2_w{k}"] = dinc(sync.dma_start(
                out=stag[:, 512 * slot:512 * (slot + 1)], in_=wslabs[k][0]), "ld")
        SY(p3w_sy)

        def p3w_ve(v, k=k):
            w(v, "ld", tags[f"ld2_w{k}"])
            if k == 0:
                w(v, "dve", tags["v_wth"])
            w(v, "gd", tags.get(f"gd_t{k - 2}"))
            slot = k % 2
            mi = wslabs[k][2]
            v.tensor_scalar(xn_tmp[:, 0:512], stag[:, 512 * slot:512 * (slot + 1)],
                            nthv[:, mi:mi + 1], None, ALU.is_lt)
            v.scalar_tensor_tensor(ternst[:, 512 * slot:512 * (slot + 1)],
                                   stag[:, 512 * slot:512 * (slot + 1)],
                                   mwth[:, 3 + mi:4 + mi], xn_tmp[:, 0:512],
                                   ALU.is_gt, ALU.subtract)
            tags[f"v_t{k}"] = inc(v.drain(), "dve")
        VE(p3w_ve)

        def p3w_gp(gp, k=k):
            w(gp, "dve", tags[f"v_t{k}"])
            slot = k % 2
            tags[f"gd_t{k}"] = dinc(gp.dma_start(
                out=scr_dst(k), in_=ternst[:, 512 * slot:512 * (slot + 1)]), "gd")
        GP(p3w_gp)

    def p3c_sy(sync):
        w(sync, "dve", tags[f"v_t{len(wslabs) - 1}"])
        w(sync, "gd", tags[f"gd_t{len(wslabs) - 1}"])
        tags["ld_SELH"] = dinc(sync.dma_start(out=SELH[:], in_=dSELH), "ldc")
        w(sync, "gd", tags["gd_g1"])
        tags["ld_SEL16"] = dinc(sync.dma_start(out=SEL16[:], in_=dSEL16), "ldc")
        tags["ld_mask"] = dinc(sync.dma_start(out=maskst[:], in_=dmask), "ldc")
    SY(p3c_sy)
    LDC_ALL = cnt["ldc"]

    def p3r_gp(gp):
        w(gp, "cc", tags["cc"])
        for j in range(8):
            tags[f"gd_Dg{j}"] = dinc(gp.dma_start(
                out=DgT[:, j:j + 1],
                in_=gout.ap()[j:j + 1, 65536:65552].rearrange(
                    "a (p b) -> (a p) b", p=16)), "gd")
    GP(p3r_gp)

    def p3r_ve(v):
        w(v, "gd", tags["gd_Dg7"])
        v.tensor_scalar(G16[:], DgT[:], 1.0, None, ALU.subtract)
        v.drain()
        v.tensor_tensor(G16[:], G16[:], mprev16[:], ALU.mult)
        v.drain()
        v.tensor_scalar(G16[:], G16[:], 1.0, None, ALU.add)
        v.memset(P16[:, 7:8], 1.0)
        v.drain()
        for j in range(6, -1, -1):
            v.tensor_tensor(P16[:, j:j + 1], P16[:, j + 1:j + 2],
                            G16[:, j + 1:j + 2], ALU.mult)
            v.drain()
        v.tensor_tensor(cj16[:], P16[:], mprev16[:], ALU.mult)
        tags["v_cj"] = inc(v.drain(), "dve")
    VE(p3r_ve)

    for j in range(8):
        def p3s_gp(gp, j=j):
            if j > 0:
                w(gp, "dve", tags[f"v_s0_{j - 1}"])
            tags[f"gd_Sg{j}"] = dinc(gp.dma_start(
                out=Sg[:], in_=gout.ap()[j:j + 1, 0:65536].rearrange(
                    "a (p c) -> (a p) c", p=128)), "gd")
        GP(p3s_gp)

        def p3s_ve(v, j=j):
            if j == 0:
                w(v, "dve", tags["v_cj"])
            else:
                w(v, "pe", tags[f"pe_cb{j - 1}"])
            v.tensor_scalar(R16[:], M8c[:], cj16[:, j:j + 1], None, ALU.mult)
            tags[f"v_r{j}"] = inc(v.drain(), "dve")
        VE(p3s_ve)

        def p3s_te(te, j=j):
            w(te, "dve", tags[f"v_r{j}"])
            te.matmul(PS[:, 2560:2568], QC32[:], R16[:], start=True, stop=True,
                      skip_group_check=True)
            tags[f"pe_cb{j}"] = inc(te.drain(), "pe")
        TE(p3s_te)

        def p3s_ve2(v, j=j):
            w(v, "pe", tags[f"pe_cb{j}"])
            v.tensor_copy(cbj[:], PS[:, 2560:2568])
            w(v, "gd", tags[f"gd_Sg{j}"])
            v.drain()
            rd, wr = 512 * (j % 2), 512 * ((j + 1) % 2)
            for p in range(8):
                v.scalar_tensor_tensor(S0[:, wr + 64 * p:wr + 64 * (p + 1)],
                                       Sg[:, 64 * p:64 * (p + 1)],
                                       cbj[:, p:p + 1],
                                       S0[:, rd + 64 * p:rd + 64 * (p + 1)],
                                       ALU.mult, ALU.add)
            tags[f"v_s0_{j}"] = inc(v.drain(), "dve")
        VE(p3s_ve2)

    barrier()
    if phases < 4:
        return debug_tail()

    def p4_wo(gp):
        tags["gd_Wo"] = dinc(gp.dma_start(
            out=Wsh[:].rearrange("p (s n) -> p s n", s=NS),
            in_=dWo.rearrange("(s p) n -> p s n", p=128)), "gd")
    GP(p4_wo)

    for n in range(NC):
        def ve_pn(v, n=n):
            if not G("GPN"):
                tags[f"v_pn{n}"] = cnt["dve"]
                tags[f"pe_dt{n}"] = cnt["pe"]
                tags[f"v_ndt{n}"] = cnt["dve"]
                tags[f"v_str{n}"] = inc(v.drain(), "dve")
                return
            if n > 0:
                w(v, "pe", tags[f"pe_dt{n - 1}"])
            v.tensor_scalar(R16[:], M8c[:], pn16[:, n:n + 1], None, ALU.mult)
            tags[f"v_pn{n}"] = inc(v.drain(), "dve")
        VE(ve_pn)

        def te_dt(te, n=n):
            if not G("GPN"):
                return
            w(te, "dve", tags[f"v_pn{n}"])
            if n > 0:
                w(te, "dve", tags[f"v_ndt{n - 1}"])
            if G("GDT"):
                te.transpose(PS[:, 384:512], dec_fm[:, CK * n:CK * (n + 1)],
                             identf[:])
                te.matmul(PS[:, 640:648], QC32[:], R16[:], start=True, stop=True,
                          skip_group_check=True)
            tags[f"pe_dt{n}"] = inc(te.drain(), "pe")
        TE(te_dt)

        def ve_b(v, n=n):
            if not G("GPN"):
                return
            w(v, "pe", tags[f"pe_dt{n}"])
            if n >= 1:
                w(v, "pe", tags[f"pe_o{n - 1}_{H - 1}"])
            v.tensor_scalar(negdecT[:], PS[:, 384:400], -1.0, None, ALU.mult)
            v.tensor_copy(pnb[:], PS[:, 640:648])
            tags[f"v_ndt{n}"] = inc(v.drain(), "dve")
            if G("GST"):
                for p in range(8):
                    so = 64 * p
                    v.scalar_tensor_tensor(Strue[:, so:so + 64],
                                           S0[:, 64 * p:64 * (p + 1)],
                                           pnb[:, p:p + 1],
                                           states[:, 512 * n + 64 * p:512 * n + 64 * (p + 1)],
                                           ALU.mult, ALU.add)
            tags[f"v_str{n}"] = inc(v.drain(), "dve")
        VE(ve_b)

        def sc_de(sc, n=n):
            if not G("GDE"):
                tags[f"a_de{n}"] = cnt["act"]
                return
            w(sc, "dve", tags[f"v_ndt{n}"])
            sc.activation(t16[:, 644:772], dec_fm[0:16, CK * n:CK * (n + 1)], AF.Exp)
            tags[f"a_de{n}"] = inc(sc.drain(), "act")
        SC(sc_de)

        def ve_deb(v, n=n):
            w(v, "act", tags[f"a_de{n}"])
            v.tensor_copy(dexpb[0:16, :], t16[0:16, 644:772])
            tags[f"v_deb{n}"] = inc(v.drain(), "dve")
        VE(ve_deb)

        for s in range(NS):
            def te_rk(te, n=n, s=s):
                if s == 0:
                    w(te, "dve", tags[f"v_deb{n}"])
                    if n == 0:
                        w(te, "ldc", LDC_ALL)
                if s >= 2:
                    w(te, "dve", tags[f"v_rk{n}_{s - 2}"])
                slot = s % 2
                if G("GRKM"):
                    rkc = 256 + 1024 * slot
                    te.matmul(PS[:, rkc:rkc + 128],
                              SEL16[:, 128 * s:128 * (s + 1)], dexpb[:],
                              start=True, stop=True, skip_group_check=True)
                tags[f"pe_rk{n}_{s}"] = inc(te.drain(), "pe")
            TE(te_rk)

            def ve_rk(v, n=n, s=s):
                if s == 0 and n > 0:
                    w(v, "pe", tags[f"pe_o{n - 1}_{H - 1}"])
                w(v, "pe", tags[f"pe_rk{n}_{s}"])
                slot = s % 2
                rkc = 256 + 1024 * slot
                if G("GRKE"):
                    if G("GRKC"):
                        v.tensor_tensor(rkdx[:, 128 * s:128 * (s + 1)],
                                        xn_fm[:, SLABW * s + 1 + 128 * n:SLABW * s + 1 + 128 * (n + 1)],
                                        PS[:, rkc:rkc + 128], ALU.mult)
                    else:
                        v.tensor_copy(rkdx[:, 128 * s:128 * (s + 1)],
                                      PS[:, rkc:rkc + 128])
                tags[f"v_rk{n}_{s}"] = inc(v.drain(), "dve")
            VE(ve_rk)

        ordq = [hh for hh in range(H) if hh % 2 == 0] + \
               [hh for hh in range(H) if hh % 2 == 1]
        if p4sub < 2:
            def ve_skipA(v, n=n):
                v.memset(intra_sb[:], 0.0)
                for hh in range(H):
                    tags[f"v_iev{n}_{hh}"] = inc(v.drain(), "dve")
                    tags[f"a_L{n}_{hh}"] = cnt["act"]
                    tags[f"v_La{n}_{hh}"] = tags[f"v_iev{n}_{hh}"]
                    if p4sub < 3:
                        tags[f"pe_o{n}_{hh}"] = cnt["pe"]
            VE(ve_skipA)
        for q, hh in enumerate(ordq):
            if p4sub < 2:
                break
            par = hh % 2
            sl = hh // 2
            r0 = 64 * par
            islot = q % 4
            lslot = q % 2

            def te_hd(te, n=n, q=q, hh=hh, par=par, sl=sl, r0=r0,
                      islot=islot, lslot=lslot):
                if q == 0:
                    w(te, "dve", tags[f"v_rk{n}_{NS - 1}"])
                    w(te, "dve", tags[f"v_str{n}"])
                    if n == 0:
                        w(te, "ldc", LDC_ALL)
                    if n > 0:
                        w(te, "dve", tags[f"v_oev{n - 1}"])
                pL = (tags.get(f"v_La{n}_{ordq[q - 2]}") if q >= 2
                      else (tags.get(f"v_La{n - 1}_{ordq[q + 14]}") if n > 0 else None))
                w(te, "act", None)
                w(te, "dve", pL)
                pI = (tags.get(f"v_iev{n}_{ordq[q - 2]}") if q >= 2
                      else (tags.get(f"v_iev{n - 1}_{ordq[q + 14]}") if n > 0 else None))
                w(te, "dve", pI)
                lc = 512 + 1024 * lslot
                te.matmul(PS[:, lc:lc + 128],
                          SELH[:, 128 * hh:128 * (hh + 1)],
                          dec_fm[:, CK * n:CK * (n + 1)],
                          start=True, stop=True, skip_group_check=True)
                tags[f"pe_L{n}_{hh}"] = inc(te.drain(), "pe")
                ic = 1024 * (islot % 2) + 128 * (islot // 2)
                te.matmul(PS[:, ic:ic + 128],
                          xn_fm[r0:r0 + 64, SLABW * sl + 128 * n:SLABW * sl + 128 * (n + 1)],
                          xn_fm[r0:r0 + 64, SLABW * sl + 1 + 128 * n:SLABW * sl + 1 + 128 * (n + 1)],
                          start=True, stop=True, skip_group_check=True)
                tags[f"pe_i{n}_{hh}"] = inc(te.drain(), "pe")
            TE(te_hd)

            def ve_La(v, n=n, q=q, hh=hh, lslot=lslot):
                w(v, "pe", tags[f"pe_L{n}_{hh}"])
                if n == 0 and q == 0:
                    w(v, "ldc", LDC_ALL)
                pS = (tags.get(f"a_L{n}_{ordq[q - 2]}") if q >= 2
                      else (tags.get(f"a_L{n - 1}_{ordq[q + 14]}") if n > 0 else None))
                w(v, "act", pS)
                lc = 512 + 1024 * lslot
                v.tensor_tensor(Lraw[:, 128 * lslot:128 * (lslot + 1)],
                                PS[:, lc:lc + 128],
                                maskst[:], ALU.add)
                tags[f"v_La{n}_{hh}"] = inc(v.drain(), "dve")
            VE(ve_La)

            def sc_hd(sc, n=n, q=q, hh=hh, lslot=lslot):
                w(sc, "dve", tags[f"v_La{n}_{hh}"])
                pS = (tags.get(f"v_iev{n}_{ordq[q - 2]}") if q >= 2
                      else (tags.get(f"v_iev{n - 1}_{ordq[q + 14]}") if n > 0 else None))
                w(sc, "dve", pS)
                sc.activation(Lh_sb[:, 128 * lslot:128 * (lslot + 1)],
                              Lraw[:, 128 * lslot:128 * (lslot + 1)],
                              AF.Exp, bias=negdecT[:, hh:hh + 1])
                tags[f"a_L{n}_{hh}"] = inc(sc.drain(), "act")
            SC(sc_hd)

            def ve_hd(v, n=n, hh=hh, islot=islot, lslot=lslot):
                w(v, "pe", tags[f"pe_i{n}_{hh}"])
                w(v, "act", tags[f"a_L{n}_{hh}"])
                if n > 0:
                    w(v, "pe", tags[f"pe_o{n - 1}_{hh}"])
                ic = 1024 * (islot % 2) + 128 * (islot // 2)
                v.tensor_tensor(intra_sb[:, 128 * hh:128 * (hh + 1)],
                                PS[:, ic:ic + 128],
                                Lh_sb[:, 128 * lslot:128 * (lslot + 1)], ALU.mult)
                tags[f"v_iev{n}_{hh}"] = inc(v.drain(), "dve")
            VE(ve_hd)

            def te_o1(te, n=n, hh=hh, par=par, sl=sl, r0=r0):
                if p4sub < 3:
                    tags[f"pe_o{n}_{hh}"] = cnt["pe"]
                    return
                w(te, "dve", tags[f"v_iev{n}_{hh}"])
                te.matmul(PS[:, 2048 + 64 * hh:2048 + 64 * (hh + 1)],
                          intra_sb[:, 128 * hh:128 * (hh + 1)],
                          vb_tm[:, 1024 * n + 64 * hh:1024 * n + 64 * (hh + 1)],
                          start=True, stop=False, skip_group_check=True)
                te.matmul(PS[:, 2048 + 64 * hh:2048 + 64 * (hh + 1)],
                          rkdx[r0:r0 + 64, 128 * sl:128 * (sl + 1)],
                          Strue[r0:r0 + 64, 64 * sl:64 * (sl + 1)],
                          start=False, stop=True, skip_group_check=True)
                tags[f"pe_o{n}_{hh}"] = inc(te.drain(), "pe")
            TE(te_o1)

        def ve_oev(v, n=n):
            if n > 0:
                w(v, "pe", tags[f"pe_ot{n - 1}_{NS - 1}"])
            if p4sub >= 3:
                w(v, "pe", tags[f"pe_o{n}_{ordq[-1]}"])
            for hh in range(H):
                if p4sub >= 3:
                    v.tensor_scalar(o_tm[:, 64 * hh:64 * (hh + 1)],
                                    PS[:, 2048 + 64 * hh:2048 + 64 * (hh + 1)],
                                    gate_tm[:, 16 * n + hh:16 * n + hh + 1], None, ALU.mult)
                else:
                    v.memset(o_tm[:, 64 * hh:64 * (hh + 1)], 0.0)
            tags[f"v_oev{n}"] = inc(v.drain(), "dve")
        VE(ve_oev)

        for s in range(NS):
            if p4sub < 4:
                def ve_skipC(v, n=n, s=s):
                    w(v, "dve", tags[f"v_oev{n}"])
                    tags[f"pe_ot{n}_{s}"] = cnt["pe"]
                    tags[f"v_ot{n}_{s}"] = inc(v.drain(), "dve")
                VE(ve_skipC)
                continue
            def te_ot(te, n=n, s=s):
                if s == 0:
                    w(te, "dve", tags[f"v_oev{n}"])
                    if n == 0:
                        w(te, "gd", tags["gd_Wo"])
                if s >= 2:
                    w(te, "dve", tags[f"v_ot{n}_{s - 2}"])
                slot = s % 2
                te.transpose(psb(OTT[slot], OTT[slot] + 64),
                             o_tm[:, 128 * s:128 * (s + 1)], identb[:])
                tags[f"pe_ot{n}_{s}"] = inc(te.drain(), "pe")
            TE(te_ot)

            def ve_ot(v, n=n, s=s):
                if s == 0 and n > 0:
                    w(v, "pe", tags[f"pe_wo{n - 1}"])
                w(v, "pe", tags[f"pe_ot{n}_{s}"])
                slot = s % 2
                v.tensor_copy(o_fm[:, 128 * s:128 * (s + 1)],
                              psb(OTT[slot], OTT[slot] + 64))
                tags[f"v_ot{n}_{s}"] = inc(v.drain(), "dve")
            VE(ve_ot)

        def te_wo(te, n=n):
            if p4sub < 4:
                tags[f"pe_wo{n}"] = cnt["pe"]
                return
            if n > 0:
                w(te, "dve", tags[f"v_x1_{n - 1}"])
            for s in range(NS):
                w(te, "dve", tags[f"v_ot{n}_{s}"])
                te.matmul(PS[:, 3072:3584], o_fm[:, 128 * s:128 * (s + 1)],
                          Wsh[:, 1024 * s:1024 * s + 512],
                          start=(s == 0), stop=(s == NS - 1), skip_group_check=True)
                te.matmul(PS[:, 3584:4096], o_fm[:, 128 * s:128 * (s + 1)],
                          Wsh[:, 1024 * s + 512:1024 * (s + 1)],
                          start=(s == 0), stop=(s == NS - 1), skip_group_check=True)
            tags[f"pe_wo{n}"] = inc(te.drain(), "pe")
        TE(te_wo)

        def ve_x1(v, n=n):
            if p4sub >= 4:
                w(v, "pe", tags[f"pe_wo{n}"])
            v.drain()
            if p4sub >= 4:
                v.tensor_tensor(stream[:, 1024 * n:1024 * (n + 1)],
                                stream[:, 1024 * n:1024 * (n + 1)],
                                PS[:, 3072:4096], ALU.add)
            tags[f"v_x1_{n}"] = inc(v.drain(), "dve")
        VE(ve_x1)

    barrier()
    if phases < 5:
        return debug_tail()

    STT_ = 4
    n_st = (NC + STT_ - 1) // STT_

    for st in range(n_st):
        tiles = list(range(STT_ * st, min(STT_ * (st + 1), NC)))

        for i, t in enumerate(tiles):
            def ln2_ve(v, t=t, i=i, st=st):
                if i == 0 and st > 0:
                    w(v, "dve", tags[f"v_y{STT_ * st - 1}"])
                if t > 0:
                    w(v, "dve", tags[f"v_q1{t - 1}"])
                v.reduce_sum(t1a[:], stream[:, 1024 * t:1024 * (t + 1)], axis=AX.X)
                v.drain()
                v.tensor_scalar(t1c[:], t1a[:], 1.0 / DM, None, ALU.mult)
                v.drain()
                v.tensor_scalar(xc[:], stream[:, 1024 * t:1024 * (t + 1)],
                                t1c[:], None, ALU.subtract)
                v.reduce_max(q5a[:], xc[:], axis=AX.X, apply_absolute_value=True)
                tags[f"v_l2a{t}"] = inc(v.drain(), "dve")
            VE(ln2_ve)

            def ln2_sc(sc, t=t):
                w(sc, "dve", tags[f"v_l2a{t}"])
                if t > 0:
                    w(sc, "dve", tags[f"v_q1{t - 1}"])
                    w(sc, "pe", tags[f"pe_q1{t - 1}_{NS - 1}"])
                sc.activation(f_bf[:], xc[:], AF.Square, accum_out=t1b[:])
                sc.activation(f_bf[:], xc[:], AF.Sign)
                tags[f"a_l2{t}"] = inc(sc.drain(), "act")
            SC(ln2_sc)

            def ln2_ve2(v, t=t):
                w(v, "act", tags[f"a_l2{t}"])
                v.tensor_scalar(t1d[:], t1b[:], 1.0 / DM, None, ALU.mult)
                tags[f"v_l2b{t}"] = inc(v.drain(), "dve")
            VE(ln2_ve2)

            def ln2_sc2(sc, t=t):
                w(sc, "dve", tags[f"v_l2b{t}"])
                sc.activation(t1a[:], t1d[:], AF.Ln, bias=consts[:, C_EPS:C_EPS + 1])
                sc.drain()
                sc.activation(t1d[:], t1a[:], AF.Exp, scale=-0.5)
                tags[f"a_l2b{t}"] = inc(sc.drain(), "act")
            SC(ln2_sc2)

            def ln2_ve3(v, t=t):
                w(v, "act", tags[f"a_l2b{t}"])
                v.tensor_tensor(q5b[:], q5a[:], t1d[:], ALU.mult)
                v.drain()
                v.tensor_scalar(q5b[:], q5b[:], 1e-5, None, ALU.max)
                v.drain()
                v.reciprocal(q5c[:], q5b[:])
                v.drain()
                v.tensor_tensor(q5d[:], q5c[:], t1d[:], ALU.mult)
                v.drain()
                v.tensor_scalar(q5d[:], q5d[:], 127.0, None, ALU.mult)
                v.scalar_tensor_tensor(cgv[:, t:t + 1], q5b[:], 1.0 / 127.0,
                                       mwth[:, 0:1], ALU.mult, ALU.mult)
                v.scalar_tensor_tensor(cuv[:, t:t + 1], q5b[:], 1.0 / 127.0,
                                       mwth[:, 1:2], ALU.mult, ALU.mult)
                tags[f"v_l2c{t}"] = inc(v.drain(), "dve")
            VE(ln2_ve3)

            def q1_sc(sc, t=t):
                w(sc, "dve", tags[f"v_l2c{t}"])
                sc.activation(f_tmp[:], xc[:], AF.Copy, scale=q5d[:, 0:1],
                              bias=MAGIC)
                sc.drain()
                sc.activation(xc[:], f_tmp[:], AF.Copy, bias=-MAGIC)
                tags[f"a_q1{t}"] = inc(sc.drain(), "act")
            SC(q1_sc)

            def q1_ve(v, t=t):
                w(v, "act", tags[f"a_q1{t}"])
                v.scalar_tensor_tensor(f_tmp[:], f_bf[:], DELTA, xc[:],
                                       ALU.mult, ALU.add)
                v.tensor_copy(xc[:].bitcast(I32), f_tmp[:])
                v.tensor_copy(f_bf[:], xc[:].bitcast(I32))
                tags[f"v_q1{t}"] = inc(v.drain(), "dve")
            VE(q1_ve)

            for s in range(NS):
                def q1_te(te, t=t, i=i, st=st, s=s):
                    gi = t * NS + s
                    if s == 0:
                        w(te, "dve", tags[f"v_q1{t}"])
                        if i == 0 and st > 0:
                            w(te, "dve", tags[f"v_y{STT_ * st - 1}"])
                    if s >= 2:
                        w(te, "dve", tags[f"v_q1e{t}_{s - 2}"])
                    slot = gi % 2
                    te.transpose(psb(QT[slot], QT[slot] + 64),
                                 f_bf[:, 128 * s:128 * (s + 1)], identb[:])
                    tags[f"pe_q1{t}_{s}"] = inc(te.drain(), "pe")
                TE(q1_te)

                def q1_ve2(v, t=t, i=i, s=s):
                    gi = t * NS + s
                    w(v, "pe", tags[f"pe_q1{t}_{s}"])
                    slot = gi % 2
                    v.tensor_copy(q1_fm[:, 512 * s + 128 * i:512 * s + 128 * (i + 1)],
                                  psb(QT[slot], QT[slot] + 64))
                    tags[f"v_q1e{t}_{s}"] = inc(v.drain(), "dve")
                VE(q1_ve2)

        for m8 in range(8):
            def gu_sy(sync, m8=m8, st=st):
                if m8 > 0:
                    w(sync, "pe", tags[f"pe_gu{st}_{m8 - 1}"])
                elif st > 0:
                    w(sync, "pe", tags[f"pe_dn{st - 1}_7"])
                w(sync, "gd", tags[f"gd_t{len(wslabs) - 1}"])
                tags[f"ld_tg{st}_{m8}"] = dinc(sync.dma_start(
                    out=tg_sb[:].rearrange("p (s n) -> p s n", s=NS),
                    in_=scr_g.ap()[:, 512 * m8:512 * (m8 + 1)].rearrange(
                        "(s p) n -> p s n", p=128)), "ld")
                tags[f"ld_tu{st}_{m8}"] = dinc(sync.dma_start(
                    out=tu_sb[:].rearrange("p (s n) -> p s n", s=NS),
                    in_=scr_u.ap()[:, 512 * m8:512 * (m8 + 1)].rearrange(
                        "(s p) n -> p s n", p=128)), "ld")
            SY(gu_sy)

            for i, t in enumerate(tiles):
                def gu_te(te, m8=m8, i=i, t=t, st=st, tiles=tiles):
                    if m8 == 0 and i == 0:
                        for t2 in tiles:
                            w(te, "dve", tags[f"v_q1e{t2}_{NS - 1}"])
                        if st > 0:
                            w(te, "dve", tags[f"v_y{STT_ * st - 1}"])
                    w(te, "ld", tags[f"ld_tu{st}_{m8}"])
                    if i >= 2:
                        w(te, "dve", tags.get(f"v_h{st}_{m8}_{tiles[i - 2]}"))
                    elif m8 > 0:
                        pi = tiles[i + 2] if i + 2 < len(tiles) else tiles[-1]
                        w(te, "dve", tags.get(f"v_h{st}_{m8 - 1}_{pi}"))
                    ga = 1024 * (i % 2)
                    gb = ga + 512
                    for s in range(NS):
                        lhs = q1_fm[:, 512 * s + 128 * i:512 * s + 128 * (i + 1)]
                        te.matmul(PS[:, ga:ga + 512], lhs,
                                  tg_sb[:, 512 * s:512 * (s + 1)],
                                  start=(s == 0), stop=(s == NS - 1),
                                  skip_group_check=True)
                    for s in range(NS):
                        lhs = q1_fm[:, 512 * s + 128 * i:512 * s + 128 * (i + 1)]
                        te.matmul(PS[:, gb:gb + 512], lhs,
                                  tu_sb[:, 512 * s:512 * (s + 1)],
                                  start=(s == 0), stop=(s == NS - 1),
                                  skip_group_check=True)
                    tags[f"pe_g{st}_{m8}_{t}"] = inc(te.drain(), "pe")
                    if i == len(tiles) - 1:
                        tags[f"pe_gu{st}_{m8}"] = tags[f"pe_g{st}_{m8}_{t}"]
                TE(gu_te)

                def gu_sc(sc, m8=m8, i=i, t=t, st=st, tiles=tiles):
                    w(sc, "pe", tags[f"pe_g{st}_{m8}_{t}"])
                    if i >= 2:
                        w(sc, "dve", tags.get(f"v_h{st}_{m8}_{tiles[i - 2]}"))
                    elif m8 > 0:
                        pi = tiles[i + 2] if i + 2 < len(tiles) else tiles[-1]
                        w(sc, "dve", tags.get(f"v_h{st}_{m8 - 1}_{pi}"))
                    ga = 1024 * (i % 2)
                    sc.activation(xc[:, 512 * (i % 2):512 * (i % 2) + 512],
                                  PS[:, ga:ga + 512], AF.Silu, scale=cgv[:, t:t + 1])
                    tags[f"a_h{st}_{m8}_{t}"] = inc(sc.drain(), "act")
                SC(gu_sc)

                def gu_ve(v, m8=m8, i=i, t=t, st=st):
                    w(v, "act", tags[f"a_h{st}_{m8}_{t}"])
                    gb = 1024 * (i % 2) + 512
                    v.scalar_tensor_tensor(
                        h_arena[:, INNER * i + 512 * m8:INNER * i + 512 * (m8 + 1)],
                        PS[:, gb:gb + 512], cuv[:, t:t + 1],
                        xc[:, 512 * (i % 2):512 * (i % 2) + 512], ALU.mult, ALU.mult)
                    tags[f"v_h{st}_{m8}_{t}"] = inc(v.drain(), "dve")
                VE(gu_ve)

        for i, t in enumerate(tiles):
            def hq_ve(v, i=i, t=t, st=st):
                w(v, "dve", tags[f"v_h{st}_7_{t}"])
                if t > 0:
                    w(v, "act", tags.get(f"a_hz{t - 1}_3"))
                v.reduce_max(q5a[:], h_arena[:, INNER * i:INNER * (i + 1)],
                             axis=AX.X, apply_absolute_value=True)
                v.drain()
                v.tensor_scalar(q5b[:], q5a[:], 1e-5, None, ALU.max)
                v.drain()
                v.reciprocal(q5c[:], q5b[:])
                v.drain()
                v.tensor_scalar(q5d[:], q5c[:], 127.0, None, ALU.mult)
                v.scalar_tensor_tensor(cdv[:, t:t + 1], q5b[:], 1.0 / 127.0,
                                       mwth[:, 2:3], ALU.mult, ALU.mult)
                tags[f"v_hq{t}"] = inc(v.drain(), "dve")
            VE(hq_ve)

            for qr in range(4):
                def hqq_sc(sc, i=i, t=t, qr=qr):
                    hsl = h_arena[:, INNER * i + 1024 * qr:INNER * i + 1024 * (qr + 1)]
                    if qr == 0:
                        w(sc, "dve", tags[f"v_hq{t}"])
                    else:
                        w(sc, "dve", tags[f"v_hqq{t}_{qr - 1}"])
                    sc.activation(f_bf[:], hsl, AF.Sign)
                    sc.activation(f_tmp[:], hsl, AF.Copy, scale=q5d[:, 0:1],
                                  bias=MAGIC)
                    sc.drain()
                    sc.activation(xc[:], f_tmp[:], AF.Copy, bias=-MAGIC)
                    tags[f"a_hz{t}_{qr}"] = inc(sc.drain(), "act")
                SC(hqq_sc)

                def hqq_ve(v, i=i, t=t, qr=qr):
                    w(v, "act", tags[f"a_hz{t}_{qr}"])
                    if qr == 0 and t > 0:
                        w(v, "pe", tags.get(f"pe_hf{t - 1}_31"))
                    v.scalar_tensor_tensor(f_tmp[:], f_bf[:], DELTA, xc[:],
                                           ALU.mult, ALU.add)
                    v.tensor_copy(xc[:].bitcast(I32), f_tmp[:])
                    v.tensor_copy(hq_tm[:, 1024 * qr:1024 * (qr + 1)],
                                  xc[:].bitcast(I32))
                    tags[f"v_hqq{t}_{qr}"] = inc(v.drain(), "dve")
                VE(hqq_ve)

            for s2 in range(32):
                def hq_te(te, i=i, t=t, s2=s2):
                    if s2 == 0:
                        w(te, "dve", tags[f"v_hqq{t}_3"])
                    if s2 >= 2:
                        w(te, "dve", tags[f"v_hf{t}_{s2 - 2}"])
                    slot = s2 % 2
                    te.transpose(psb(QT[slot], QT[slot] + 64),
                                 hq_tm[:, 128 * s2:128 * (s2 + 1)], identb[:])
                    tags[f"pe_hf{t}_{s2}"] = inc(te.drain(), "pe")
                TE(hq_te)

                def hq_ve3(v, i=i, t=t, s2=s2):
                    hq_fm = h_arena[:, INNER * i:INNER * i + 2048].bitcast(BF16)
                    w(v, "pe", tags[f"pe_hf{t}_{s2}"])
                    slot = s2 % 2
                    v.tensor_copy(hq_fm[:, 128 * s2:128 * (s2 + 1)],
                                  psb(QT[slot], QT[slot] + 64))
                    tags[f"v_hf{t}_{s2}"] = inc(v.drain(), "dve")
                VE(hq_ve3)

        for m8 in range(8):
            def dn_sy(sync, m8=m8, st=st):
                if m8 >= 2:
                    w(sync, "pe", tags[f"pe_dn{st}_{m8 - 2}"])
                else:
                    w(sync, "pe", tags[f"pe_gu{st}_7"])
                buf = td_sb if m8 % 2 == 0 else td_sb2
                tags[f"ld_td{st}_{m8}"] = dinc(sync.dma_start(
                    out=buf[:].rearrange("p (s n) -> p s n", s=4),
                    in_=scr_d.ap()[512 * m8:512 * (m8 + 1), :].rearrange(
                        "(s p) n -> p s n", p=128)), "ld")
            SY(dn_sy)

            def dn_te(te, m8=m8, st=st, tiles=tiles):
                td_cur = td_sb if m8 % 2 == 0 else td_sb2
                w(te, "ld", tags[f"ld_td{st}_{m8}"])
                if m8 == 0:
                    for t2 in tiles:
                        w(te, "dve", tags[f"v_hf{t2}_31"])
                        w(te, "dve", tags[f"v_h{st}_7_{t2}"])
                for i2, t2 in enumerate(tiles):
                    hq_fm = h_arena[:, INNER * i2:INNER * i2 + 2048].bitcast(BF16)
                    for o2 in range(2):
                        for s4 in range(4):
                            s2 = 4 * m8 + s4
                            te.matmul(
                                PS[:, 1024 * i2 + 512 * o2:1024 * i2 + 512 * (o2 + 1)],
                                hq_fm[:, 128 * s2:128 * (s2 + 1)],
                                td_cur[:, 1024 * s4 + 512 * o2:1024 * s4 + 512 * (o2 + 1)],
                                start=(m8 == 0 and s4 == 0),
                                stop=(m8 == 7 and s4 == 3),
                                skip_group_check=True)
                tags[f"pe_dn{st}_{m8}"] = inc(te.drain(), "pe")
            TE(dn_te)

        for i, t in enumerate(tiles):
            def y_ve(v, i=i, t=t, st=st):
                w(v, "pe", tags[f"pe_dn{st}_7"])
                w(v, "st", tags.get(f"st_y{t - 1}"))
                for o2 in range(2):
                    v.scalar_tensor_tensor(
                        y_sb[:, 512 * o2:512 * (o2 + 1)],
                        PS[:, 1024 * i + 512 * o2:1024 * i + 512 * (o2 + 1)],
                        cdv[:, t:t + 1],
                        stream[:, 1024 * t + 512 * o2:1024 * t + 512 * (o2 + 1)],
                        ALU.mult, ALU.add)
                tags[f"v_y{t}"] = inc(v.drain(), "dve")
            VE(y_ve)

            def y_gp(gp, i=i, t=t):
                w(gp, "dve", tags[f"v_y{t}"])
                tags[f"st_y{t}"] = dinc(gp.dma_start(
                    out=dy[128 * t:128 * (t + 1), :], in_=y_sb[:]), "st")
            GP(y_gp)

    def fin(gp):
        gp.wait_ge(sem["st"], cnt["st"])
    GP(fin)

    ctx.close()
    return nc



def _host_consts():
    c = np.zeros((128, 16), np.float32)
    c[:, C_EPS] = EPS
    c[:, C_MAGIC] = MAGIC
    c[:, C_NMAGIC] = -MAGIC
    c[:, C_ONE] = 1.0
    return c


def _selector_consts():
    SELHv = np.zeros((128, H * CK), np.float32)
    for h in range(H):
        SELHv[h, 128 * h:128 * (h + 1)] = 1.0
    QCv = np.zeros((16, 128), np.float32)
    for h in range(16):
        QCv[h, 64 * (h % 2):64 * (h % 2) + 64] = 1.0
    M8v = np.zeros((16, 8), np.float32)
    for h in range(16):
        M8v[h, h // 2] = 1.0
    S16v = np.zeros((128, NS * 128), np.float32)
    for s in range(NS):
        for p in range(128):
            S16v[2 * s + p // 64, 128 * s + p] = 1.0
    maskv = np.full((128, 128), -1e9, np.float32)
    for k in range(128):
        maskv[k, k + 1:] = 0.0
    return SELHv, QCv, M8v, S16v.astype(ml_dtypes.bfloat16), maskv


def make_in_maps(inputs, NC=16, n_cores=8):
    x = np.asarray(inputs["L0_attn_out"], np.float32)
    B = x.shape[0]
    spb = n_cores // B
    TPC = NC * CK
    WwT = np.ascontiguousarray(np.asarray(inputs["L0_attn_W_write"], np.float32).T)
    WgbaT = np.ascontiguousarray(np.concatenate(
        [np.asarray(inputs["L0_attn_W_gate"], np.float32),
         np.asarray(inputs["L0_attn_W_beta"], np.float32),
         np.asarray(inputs["L0_attn_W_alpha"], np.float32)], axis=0).T)
    WoT = np.ascontiguousarray(np.asarray(inputs["L0_attn_W_out"], np.float32).T)
    wgT = np.ascontiguousarray(np.asarray(inputs["L0_ffn_w_gate"], np.float32).T)
    wuT = np.ascontiguousarray(np.asarray(inputs["L0_ffn_w_up"], np.float32).T)
    wdT = np.ascontiguousarray(np.asarray(inputs["L0_ffn_w_down"], np.float32).T)
    a16v = np.ascontiguousarray(
        np.asarray(inputs["L0_attn_A_log"], np.float32).reshape(16, 1))
    dtbv = np.ascontiguousarray(
        np.asarray(inputs["L0_attn_dt_bias"], np.float32).reshape(16, 1))
    SELHv, QCv, M8v, S16v, maskv = _selector_consts()
    common = dict(
        WwT=WwT, WgbaT=WgbaT, WoT=WoT, wgT=wgT, wuT=wuT, wdT=wdT,
        a16=a16v, dtb16=dtbv, consts=_host_consts(),
        identb=np.eye(128, dtype=ml_dtypes.bfloat16),
        identf=np.eye(128, dtype=np.float32),
        maskst=maskv, SELH=SELHv, QC32=QCv, M8c=M8v, SEL16b=S16v,
        ones1f=np.ones((1, 128), np.float32),
        ones128=np.ones((128, 1), np.float32),
    )
    in_maps = []
    for c in range(n_cores):
        b, s = c // spb, c % spb
        x_sl = np.ascontiguousarray(x[b, s * TPC:(s + 1) * TPC])
        x_prev = (np.zeros((1, DM), np.float32) if s == 0
                  else np.ascontiguousarray(x[b, s * TPC - 1:s * TPC]))
        mprev = np.zeros((1, 8), np.float32)
        for j in range(n_cores):
            if j // spb == b and j % spb < s:
                mprev[0, j] = 1.0
        m = dict(common)
        m.update(x_sl=x_sl, x_prev=x_prev,
                 mprev16=np.ascontiguousarray(np.tile(mprev, (16, 1))))
        in_maps.append(m)
    return in_maps


def kernel(**inputs):
    x = np.asarray(inputs["L0_attn_out"])
    B, T_full, _ = x.shape
    n_cores = 8
    NC = (B * T_full) // (n_cores * CK)
    nc = build_program(NC)
    in_maps = make_in_maps(inputs, NC=NC, n_cores=n_cores)
    res = run_bass_kernel_spmd(nc, in_maps, core_ids=list(range(n_cores)))
    TPC = NC * CK
    out = np.empty((B, T_full, DM), np.float32)
    spb = n_cores // B
    for c in range(n_cores):
        b, s = c // spb, c % spb
        out[b, s * TPC:(s + 1) * TPC] = res.results[c]["y_sl"]
    return out


# version 05 (final)
# speedup vs baseline: 1.2123x; 1.0028x over previous
import numpy as np
import ml_dtypes
import concourse.bass as bass
import concourse.mybir as mybir
from concourse.bass_utils import run_bass_kernel_spmd

F32 = mybir.dt.float32
BF16 = mybir.dt.bfloat16
I32 = mybir.dt.int32
AF = mybir.ActivationFunctionType
ALU = mybir.AluOpType
AX = mybir.AxisListType

DM = 1024
H = 16
CK = 128
NS = 8
INNER = 4096
MAGIC = 12582912.0
DELTA = 0.4995
EPS = 1e-5
C_EPS, C_MAGIC, C_NMAGIC, C_ONE = 0, 1, 2, 3
SLABW = 2052


def build_program(NC=16, phases=5, w1=True, p4sub=99):
    import os
    G = lambda k: os.environ.get(k, "1") == "1"
    nc = bass.Bass("TRN2", target_bir_lowering=False, detect_race_conditions=False)
    T = NC * CK

    dx = nc.dram_tensor("x_sl", [T, DM], F32, kind="ExternalInput").ap()
    dxprev = nc.dram_tensor("x_prev", [1, DM], F32, kind="ExternalInput").ap()
    dWw = nc.dram_tensor("WwT", [DM, DM], F32, kind="ExternalInput").ap()
    dWgba = nc.dram_tensor("WgbaT", [DM, 48], F32, kind="ExternalInput").ap()
    dWo = nc.dram_tensor("WoT", [DM, DM], F32, kind="ExternalInput").ap()
    dwg = nc.dram_tensor("wgT", [DM, INNER], F32, kind="ExternalInput").ap()
    dwu = nc.dram_tensor("wuT", [DM, INNER], F32, kind="ExternalInput").ap()
    dwd = nc.dram_tensor("wdT", [INNER, DM], F32, kind="ExternalInput").ap()
    da16 = nc.dram_tensor("a16", [16, 1], F32, kind="ExternalInput").ap()
    ddtb = nc.dram_tensor("dtb16", [16, 1], F32, kind="ExternalInput").ap()
    dconsts = nc.dram_tensor("consts", [128, 16], F32, kind="ExternalInput").ap()
    dident = nc.dram_tensor("identb", [128, 128], BF16, kind="ExternalInput").ap()
    didentf = nc.dram_tensor("identf", [128, 128], F32, kind="ExternalInput").ap()
    dmask = nc.dram_tensor("maskst", [128, 128], F32, kind="ExternalInput").ap()
    dSELH = nc.dram_tensor("SELH", [128, H * CK], F32, kind="ExternalInput").ap()
    dQC = nc.dram_tensor("QC32", [16, 128], F32, kind="ExternalInput").ap()
    dM8 = nc.dram_tensor("M8c", [16, 8], F32, kind="ExternalInput").ap()
    dSEL16 = nc.dram_tensor("SEL16b", [128, NS * 128], BF16, kind="ExternalInput").ap()
    dones1 = nc.dram_tensor("ones1f", [1, 128], F32, kind="ExternalInput").ap()
    dones128 = nc.dram_tensor("ones128", [128, 1], F32, kind="ExternalInput").ap()
    dmprev = nc.dram_tensor("mprev16", [16, 8], F32, kind="ExternalInput").ap()
    dy = nc.dram_tensor("y_sl", [T, DM], F32, kind="ExternalOutput").ap()

    scr_g = nc.dram_tensor("scr_g", [DM, INNER], BF16, kind="Internal")
    scr_u = nc.dram_tensor("scr_u", [DM, INNER], BF16, kind="Internal")
    scr_d = nc.dram_tensor("scr_d", [INNER, DM], BF16, kind="Internal")
    GN = 128 * 512 + 16
    gin = nc.dram_tensor("gin", [1, GN], F32, kind="Internal")
    gout = nc.dram_tensor("gout", [8, GN], F32, kind="Internal")

    base0 = (nc.sbuf_base + 63) & ~63
    cur = [base0]

    def alo(name, shape, dtype, at=None):
        if at is None:
            at = cur[0]
            nbytes = int(np.prod(shape[1:])) * mybir.dt.size(dtype)
            cur[0] = (at + nbytes + 63) & ~63
        return nc.alloc_sbuf_tensor_at(name, shape, dtype, offset=at)

    stream = alo("stream", [128, NC * 1024], F32)
    consts = alo("consts_sb", [128, 16], F32)
    gate_tm = alo("gate_tm", [128, NC * 16], F32)
    cgv = alo("cgv", [128, NC], F32)
    cuv = alo("cuv", [128, NC], F32)
    cdv = alo("cdv", [128, NC], F32)
    mwth = alo("mwth", [128, 8], F32)
    nthv = alo("nthv", [128, 4], F32)
    identb = alo("identb_sb", [128, 128], BF16)
    identf = alo("identf_sb", [128, 128], F32)
    QC32 = alo("QC32_sb", [16, 128], F32)
    M8c = alo("M8c_sb", [16, 8], F32)
    ones1f = alo("ones1f_sb", [1, 128], F32)
    ones128 = alo("ones128_sb", [128, 1], F32)
    mprev16 = alo("mprev_sb", [16, 8], F32)
    a16 = alo("a16_sb", [16, 1], F32)
    dtb16 = alo("dtb16_sb", [16, 1], F32)
    negeA = alo("negeA", [16, 1], F32)
    wabs = alo("wabs", [128, 192], F32)
    wsum3 = alo("wsum3", [128, 4], F32)
    t1a = alo("t1a", [128, 1], F32)
    t1b = alo("t1b", [128, 1], F32)
    t1c = alo("t1c", [128, 1], F32)
    t1d = alo("t1d", [128, 1], F32)
    tp1 = alo("tp1", [1, 8], F32)

    attn_base = cur[0]
    xn_fm = alo("xn_fm", [128, NS * SLABW], BF16)
    vb_tm = alo("vb_tm", [128, NC * 1024], BF16)
    states = alo("states", [128, NC * 512], BF16)
    dec_fm = alo("dec_fm", [128, NC * CK], F32)
    Wsh = alo("Wsh", [128, NS * 1024], BF16)
    Wgb = alo("Wgb", [128, NS * 48], BF16)
    maskst = nc.alloc_sbuf_tensor_at("maskst_sb", [128, 128], F32,
                                     offset=nc.lookup_mloc(Wgb).addr)
    S_run = alo("S_run", [128, 2 * 512], F32)
    Sg = nc.alloc_sbuf_tensor_at("Sg", [128, 512], F32,
                                 offset=nc.lookup_mloc(S_run).addr)
    if NC % 2 == 0:
        SEL16 = nc.alloc_sbuf_tensor_at("SEL16_sb", [128, NS * 128], BF16,
                                        offset=nc.lookup_mloc(S_run).addr + 2048)
    else:
        SEL16 = alo("SEL16_sb", [128, NS * 128], BF16)
    S0 = alo("S0", [128, 2 * 512], F32)
    t16 = alo("t16", [16, 1024], F32)
    xprev = nc.alloc_sbuf_tensor_at("xprev", [1, DM], F32,
                                    offset=nc.lookup_mloc(t16).addr)
    declast = alo("declast", [16, NC], F32)
    pn16 = alo("pn16", [16, NC], F32)
    DgT = alo("DgT", [16, 8], F32)
    G16 = alo("G16", [16, 8], F32)
    P16 = alo("P16", [16, 8], F32)
    cj16 = alo("cj16", [16, 8], F32)
    R16 = alo("R16", [16, 8], F32)

    pool_base = cur[0]
    wk_tm = alo("wk_tm", [128, 1024], BF16)
    vbdw = alo("vbdw", [128, 1024], BF16)
    xn_tmp = alo("xn_tmp", [128, 1024], F32)
    xnb = alo("xnb", [128, 1024], BF16)
    beta_t = alo("beta_t", [128, 16], F32)
    apre = alo("apre", [128, 16], F32)
    dwT = alo("dwT", [128, 16], F32)
    Earr = alo("Earr", [128, 8], F32)
    p2_end = cur[0]
    cur[0] = pool_base
    intra_sb = alo("intra_sb", [128, H * CK], BF16)
    Lh_sb = alo("Lh_sb", [128, 2 * 128], F32)
    Lraw = alo("Lraw", [128, 2 * 128], F32)
    o_tm = alo("o_tm", [128, 1024], BF16)
    o_fm = alo("o_fm", [128, 1024], BF16)
    rkdx = alo("rkdx", [128, NS * 128], BF16)
    Strue = alo("Strue", [128, 512], BF16)
    negdecT = alo("negdecT", [128, 16], F32)
    pnb = alo("pnb", [128, 8], F32)
    dexpb = alo("dexpb", [128, 128], BF16)
    cbj = alo("cbj", [128, 8], F32)
    p4_end = cur[0]

    cur[0] = attn_base
    h_arena = alo("h_arena", [128, 4 * INNER], F32)
    hq_tm = alo("hq_tm", [128, INNER], BF16)
    q1_fm = alo("q1_fm", [128, NS * 512], BF16)
    tg_sb = alo("tg_sb", [128, NS * 512], BF16)
    tu_sb = alo("tu_sb", [128, NS * 512], BF16)
    td_sb = nc.alloc_sbuf_tensor_at("td_sb", [128, 4 * 1024], BF16,
                                    offset=nc.lookup_mloc(tg_sb).addr)
    td_sb2 = nc.alloc_sbuf_tensor_at("td_sb2", [128, 4 * 1024], BF16,
                                     offset=nc.lookup_mloc(tu_sb).addr)
    y_sb = alo("y_sb", [128, 1024], F32)
    xc = alo("xc", [128, 1024], F32)
    f_tmp = alo("f_tmp", [128, 1024], F32)
    f_bf = alo("f_bf", [128, 1024], BF16)
    q5a = alo("q5a", [128, 1], F32)
    q5b = alo("q5b", [128, 1], F32)
    q5c = alo("q5c", [128, 1], F32)
    q5d = alo("q5d", [128, 1], F32)
    ffn_end = cur[0]

    cur[0] = max(p2_end, p4_end, ffn_end)
    stag = alo("stag", [128, 2 * 512], F32)
    ternst = alo("ternst", [128, 2 * 512], BF16)
    SELH = nc.alloc_sbuf_tensor_at("SELH_sb", [128, H * CK], F32,
                                   offset=nc.lookup_mloc(stag).addr)
    assert cur[0] <= nc.sbuf_top, f"SBUF over: {cur[0]} > {nc.sbuf_top}"

    nc.const_aps.register(F32, MAGIC, consts[:, C_MAGIC:C_MAGIC + 1])
    nc.const_aps.register(F32, -MAGIC, consts[:, C_NMAGIC:C_NMAGIC + 1])
    PS = nc.alloc_psum_tensor("PSA", [128, 4096], F32)

    def psb(a, b):
        return PS[:, a:b].bitcast(BF16)

    XT = (1792, 3872)
    WKT = (1664, 3584)
    OTT = (1664, 0)
    QT = (2048, 2560)

    from contextlib import ExitStack
    ctx = ExitStack()
    sem = {}
    for s in ("ld", "ldc", "gd", "pe", "dve", "act", "gpc", "cc", "st", "ph"):
        sem[s] = ctx.enter_context(nc.semaphore("s_" + s))
    block = ctx.enter_context(nc.Block())
    cnt = {k: 0 for k in sem}
    tags = {}

    def inc(instr, name, n=1):
        instr.then_inc(sem[name], n)
        cnt[name] += n
        return cnt[name]

    def dinc(instr, name):
        return inc(instr, name, 16)

    def w(eng, name, tag):
        if tag is not None:
            eng.wait_ge(sem[name], tag)

    _pe_last = [None]

    class _TEW:
        def __init__(self, e):
            self._e = e

        def matmul(self, *a, **k):
            r = self._e.matmul(*a, **k)
            _pe_last[0] = r
            return r

        def transpose(self, *a, **k):
            r = self._e.transpose(*a, **k)
            _pe_last[0] = None
            return r

        def drain(self):
            if _pe_last[0] is None:
                return self._e.drain()
            r = _pe_last[0]
            _pe_last[0] = None
            return r

        def __getattr__(self, k):
            return getattr(self._e, k)

    SY, VE, SC, GP = block.sync, block.vector, block.scalar, block.gpsimd
    TE = lambda f: block.tensor(lambda te: f(_TEW(te)))

    def barrier():
        target = cnt["ph"] + 5
        for s in (SY, VE, SC, TE, GP):
            def f(eng, target=target):
                eng.sem_inc(sem["ph"], 1)
                eng.wait_ge(sem["ph"], target)
            s(f)
        cnt["ph"] = target

    def p0(sync):
        for name, dst, src in (
            ("consts", consts, dconsts), ("identb", identb, dident),
            ("identf", identf, didentf),
            ("QC32", QC32, dQC), ("M8c", M8c, dM8),
            ("ones1f", ones1f, dones1),
            ("ones128", ones128, dones128), ("mprev", mprev16, dmprev),
            ("a16", a16, da16), ("dtb16", dtb16, ddtb), ("xprev", xprev, dxprev),
        ):
            tags["ld_" + name] = dinc(sync.dma_start(out=dst[:], in_=src), "ldc")
    SY(p0)
    P0C = cnt["ldc"]

    def p0g(gp):
        tags["gd_Ww"] = dinc(gp.dma_start(
            out=Wsh[:].rearrange("p (s n) -> p s n", s=NS),
            in_=dWw.rearrange("(s p) n -> p s n", p=128)), "gd")
        tags["gd_Wgb"] = dinc(gp.dma_start(
            out=Wgb[:].rearrange("p (s n) -> p s n", s=NS),
            in_=dWgba.rearrange("(s p) n -> p s n", p=128)), "gd")
    GP(p0g)

    def p0v(v):
        v.memset(dexpb[:], 0.0)
        v.memset(S_run[:], 0.0)
        v.memset(S0[:], 0.0)
        if os.environ.get("DBG", "0") == "1":
            for tns in (xn_fm, vb_tm, states, rkdx, Strue, o_tm, o_fm,
                        intra_sb, negdecT, pnb, Lh_sb, Lraw, gate_tm,
                        declast, pn16, DgT, cj16):
                v.memset(tns[:], 0.0)
        tags["z_dec"] = inc(v.memset(dec_fm[:], 0.0), "dve")
    VE(p0v)

    def p0s(sc):
        w(sc, "ldc", P0C)
        sc.activation(negeA[:], a16[:], AF.Exp)
        sc.activation(xnb[0:1, :], xprev[:], AF.Square, accum_out=tp1[:, 1:2])
        tags["a_prev"] = inc(sc.drain(), "act")
    SC(p0s)

    def p0v2(v):
        w(v, "act", tags["a_prev"])
        v.drain()
        v.tensor_scalar(negeA[:], negeA[:], -1.0, None, ALU.mult)
        v.reduce_sum(tp1[:, 0:1], xprev[:], axis=AX.X)
        v.drain()
        v.tensor_scalar(tp1[:, 2:3], tp1[:, 0:1], 1.0 / DM, None, ALU.mult)
        v.drain()
        v.scalar_tensor_tensor(tp1[:, 3:4], tp1[:, 2:3], -1.0, tp1[:, 2:3], ALU.mult, ALU.mult)
        v.drain()
        v.scalar_tensor_tensor(tp1[:, 4:5], tp1[:, 1:2], 1.0 / DM, tp1[:, 3:4], ALU.mult, ALU.add)
        tags["v_prev1"] = inc(v.drain(), "dve")
    VE(p0v2)

    def p0s2(sc):
        w(sc, "dve", tags["v_prev1"])
        w(sc, "ldc", P0C)
        sc.activation(tp1[:, 5:6], tp1[:, 4:5], AF.Ln, bias=consts[0:1, C_EPS:C_EPS + 1])
        sc.drain()
        tags["a_prev2"] = inc(
            sc.activation(tp1[:, 6:7], tp1[:, 5:6], AF.Exp, scale=-0.5), "act")
    SC(p0s2)

    def p0v3(v):
        w(v, "act", tags["a_prev2"])
        v.drain()
        v.tensor_scalar(xprev[:], xprev[:], tp1[:, 2:3], tp1[:, 6:7], ALU.subtract, ALU.mult)
        v.drain()
        tags["v_prev2"] = inc(v.tensor_copy(xnb[0:1, :], xprev[:]), "dve")
        if os.environ.get("DBG", "0") == "1":
            v.memset(t16[:], 0.0)
    VE(p0v3)

    for s in range(NS):
        def p0t(te, s=s):
            if s == 0:
                w(te, "dve", tags["v_prev2"])
                w(te, "ldc", P0C)
            if s >= 2:
                w(te, "dve", tags[f"v_prev3_{s - 2}"])
            slot = s % 2
            te.transpose(psb(XT[slot], XT[slot] + 64)[:, 0:1],
                         xnb[0:1, 128 * s:128 * (s + 1)], identb[0:1, 0:1])
            tags[f"pe_prev{s}"] = inc(te.drain(), "pe")
        TE(p0t)

        def p0v4(v, s=s):
            w(v, "pe", tags[f"pe_prev{s}"])
            slot = s % 2
            v.tensor_copy(xn_fm[:, SLABW * s:SLABW * s + 1],
                          psb(XT[slot], XT[slot] + 64)[:, 0:1])
            tags[f"v_prev3_{s}"] = inc(v.drain(), "dve")
        VE(p0v4)
    tags["v_prev3"] = tags[f"v_prev3_{NS - 1}"]

    wslabs = []
    for mi, dsrc in enumerate((dwg, dwu)):
        for a in range(8):
            for b in range(8):
                wslabs.append((dsrc[128 * a:128 * (a + 1), 512 * b:512 * (b + 1)],
                               64 * mi + 8 * a + b, mi))
    for a in range(32):
        for b in range(2):
            wslabs.append((dwd[128 * a:128 * (a + 1), 512 * b:512 * (b + 1)],
                           128 + 2 * a + b, 2))

    def scr_dst(k):
        _, idx, mi = wslabs[k]
        if mi == 0:
            r = idx
            return scr_g.ap()[128 * (r // 8):128 * (r // 8 + 1),
                              512 * (r % 8):512 * (r % 8 + 1)]
        if mi == 1:
            r = idx - 64
            return scr_u.ap()[128 * (r // 8):128 * (r // 8 + 1),
                              512 * (r % 8):512 * (r % 8 + 1)]
        r = idx - 128
        return scr_d.ap()[128 * (r // 2):128 * (r // 2 + 1),
                          512 * (r % 2):512 * (r % 2 + 1)]

    wlA = [0]
    wlB = [0]

    def p1_w_loads(sync, budget):
        for _ in range(budget):
            k = 2 * wlA[0]
            if k >= len(wslabs):
                return
            w(sync, "act", tags.get(f"a_w{k - 2}"))
            tags[f"ld_w{k}"] = dinc(sync.dma_start(
                out=stag[:, 0:512], in_=wslabs[k][0]), "ld")
            wlA[0] += 1

    def p1_w_loads_gp(gp, budget):
        for _ in range(budget):
            k = 2 * wlB[0] + 1
            if k >= len(wslabs):
                return
            w(gp, "act", tags.get(f"a_w{k - 2}"))
            tags[f"ld_w{k}"] = dinc(gp.dma_start(
                out=stag[:, 512:1024], in_=wslabs[k][0]), "gd")
            wlB[0] += 1

    wd_ = [0]

    def p1_w_abs(sc):
        while wd_[0] < len(wslabs) and f"ld_w{wd_[0]}" in tags:
            k = wd_[0]
            w(sc, "ld" if k % 2 == 0 else "gd", tags[f"ld_w{k}"])
            slot = k % 2
            _, idx, mi = wslabs[k]
            sc.activation(xnb[:, 0:512], stag[:, 512 * slot:512 * (slot + 1)],
                          AF.Abs, accum_out=wabs[:, idx:idx + 1])
            tags[f"a_w{k}"] = inc(sc.drain(), "act")
            wd_[0] += 1

    for t in range(NC):
        def p1_sy(sync, t=t):
            tags[f"ld_x{t}"] = dinc(sync.dma_start(
                out=stream[:, 1024 * t:1024 * (t + 1)],
                in_=dx[128 * t:128 * (t + 1), :]), "ld")
            if w1:
                p1_w_loads(sync, 3)
        SY(p1_sy)
        if w1:
            GP(lambda gp: p1_w_loads_gp(gp, 3))

        def p1_sc(sc, t=t):
            w(sc, "ld", tags[f"ld_x{t}"])
            if t > 0:
                w(sc, "dve", tags[f"v_st{t - 1}"])
            if t > 0:
                w(sc, "pe", tags[f"pe_xn{t - 1}_{NS - 1}"])
            else:
                w(sc, "pe", tags[f"pe_prev{NS - 1}"])
            sc.activation(xnb[:], stream[:, 1024 * t:1024 * (t + 1)],
                          AF.Square, accum_out=t1b[:])
            tags[f"a_sq{t}"] = inc(sc.drain(), "act")
            if w1:
                p1_w_abs(sc)
        SC(p1_sc)

        def p1_ve(v, t=t):
            w(v, "ld", tags[f"ld_x{t}"])
            v.reduce_sum(t1a[:], stream[:, 1024 * t:1024 * (t + 1)], axis=AX.X)
            w(v, "act", tags[f"a_sq{t}"])
            v.drain()
            v.tensor_scalar(t1c[:], t1a[:], 1.0 / DM, None, ALU.mult)
            v.drain()
            v.scalar_tensor_tensor(t1d[:], t1c[:], -1.0, t1c[:], ALU.mult, ALU.mult)
            v.drain()
            v.scalar_tensor_tensor(t1d[:], t1b[:], 1.0 / DM, t1d[:], ALU.mult, ALU.add)
            tags[f"v_st{t}"] = inc(v.drain(), "dve")
        VE(p1_ve)

        def p1_sc2(sc, t=t):
            w(sc, "dve", tags[f"v_st{t}"])
            sc.activation(t1a[:], t1d[:], AF.Ln, bias=consts[:, C_EPS:C_EPS + 1])
            sc.drain()
            tags[f"a_r{t}"] = inc(sc.activation(t1d[:], t1a[:], AF.Exp, scale=-0.5), "act")
        SC(p1_sc2)

        def p1_ve2(v, t=t):
            w(v, "act", tags[f"a_r{t}"])
            v.tensor_scalar(xn_tmp[:], stream[:, 1024 * t:1024 * (t + 1)],
                            t1c[:], t1d[:], ALU.subtract, ALU.mult)
            v.drain()
            v.tensor_tensor(stream[:, 1024 * t:1024 * (t + 1)],
                            stream[:, 1024 * t:1024 * (t + 1)], xn_tmp[:], ALU.add)
            if t > 0:
                w(v, "pe", tags[f"pe_xn{t - 1}_{NS - 1}"])
            tags[f"v_xn{t}"] = inc(v.tensor_copy(xnb[:], xn_tmp[:]), "dve")
        VE(p1_ve2)

        for s in range(NS):
            def p1_te(te, t=t, s=s):
                gi = t * NS + s
                if s == 0:
                    w(te, "dve", tags[f"v_xn{t}"])
                if gi >= 2:
                    w(te, "dve", tags.get(f"v_ev{gi - 2}"))
                slot = gi % 2
                te.transpose(psb(XT[slot], XT[slot] + 64),
                             xnb[:, 128 * s:128 * (s + 1)], identb[:])
                tags[f"pe_xn{t}_{s}"] = inc(te.drain(), "pe")
            TE(p1_te)

            def p1_ve3(v, t=t, s=s):
                gi = t * NS + s
                w(v, "pe", tags[f"pe_xn{t}_{s}"])
                slot = gi % 2
                v.tensor_copy(
                    xn_fm[:, SLABW * s + 1 + 128 * t:SLABW * s + 1 + 128 * (t + 1)],
                    psb(XT[slot], XT[slot] + 64))
                tags[f"v_ev{gi}"] = inc(v.drain(), "dve")
            VE(p1_ve3)

    def p1_w_guard(sc):
        w(sc, "pe", tags[f"pe_xn{NC - 1}_{NS - 1}"])
    if w1 and wd_[0] < len(wslabs):
        SC(p1_w_guard)
    while w1 and wd_[0] < len(wslabs):
        SY(lambda sync: p1_w_loads(sync, 3))
        GP(lambda gp: p1_w_loads_gp(gp, 3))
        SC(p1_w_abs)

    def p1f_ve(v):
        if not w1:
            v.memset(wabs[:], 0.02)
        else:
            w(v, "act", tags[f"a_w{len(wslabs) - 1}"])
        for m in range(3):
            v.reduce_sum(t1a[:], wabs[:, 64 * m:64 * (m + 1)], axis=AX.X)
            v.drain()
            v.tensor_copy(wsum3[:, m:m + 1], t1a[:])
        tags["v_wsum"] = inc(v.drain(), "dve")
    VE(p1f_ve)

    def p1f_te(te):
        w(te, "dve", tags["v_wsum"])
        for m in range(3):
            te.matmul(PS[0:1, 3968 + m:3969 + m], wsum3[:, m:m + 1],
                      ones128[:], start=True, stop=True)
        tags["pe_wsum"] = inc(te.drain(), "pe")
    TE(p1f_te)

    def p1f_ve2(v):
        w(v, "pe", tags["pe_wsum"])
        for m in range(3):
            v.tensor_scalar(t16[0:1, 40 + m:41 + m], PS[0:1, 3968 + m:3969 + m],
                            1.0 / (INNER * DM), 1e-5, ALU.mult, ALU.max)
        tags["v_wmean"] = inc(v.drain(), "dve")
    VE(p1f_ve2)

    def p1f_te2(te):
        w(te, "dve", tags["v_wmean"])
        for m in range(3):
            te.matmul(PS[:, 3972 + m:3973 + m], ones1f[:], t16[0:1, 40 + m:41 + m],
                      start=True, stop=True)
        tags["pe_wb"] = inc(te.drain(), "pe")
    TE(p1f_te2)

    def p1f_ve3(v):
        w(v, "pe", tags["pe_wb"])
        for m in range(3):
            v.tensor_copy(mwth[:, m:m + 1], PS[:, 3972 + m:3973 + m])
            v.tensor_scalar(mwth[:, 3 + m:4 + m], PS[:, 3972 + m:3973 + m],
                            0.5, None, ALU.mult)
            v.tensor_scalar(nthv[:, m:m + 1], PS[:, 3972 + m:3973 + m],
                            -0.5, None, ALU.mult)
        tags["v_wth"] = inc(v.drain(), "dve")
    VE(p1f_ve3)

    dbg_specs = []
    if os.environ.get("DBG", "0") == "1":
        for nm, src in (("S0", lambda: S0[:, 0:512]), ("cj16", lambda: cj16[:]),
                        ("DgT", lambda: DgT[:]), ("pn16", lambda: pn16[:]),
                        ("mprev", lambda: mprev16[:]), ("decl", lambda: declast[:]),
                        ("st0", lambda: states[:, 0:512]),
                        ("rkdx", lambda: rkdx[:]), ("Strue", lambda: Strue[:]),
                        ("otm", lambda: o_tm[:]), ("ofm", lambda: o_fm[:]),
                        ("intra", lambda: intra_sb[:]), ("dexpb", lambda: dexpb[:]),
                        ("ndT", lambda: negdecT[:]), ("pnb", lambda: pnb[:]),
                        ("decfm", lambda: dec_fm[:]), ("Lh", lambda: Lh_sb[:]),
                        ("Lraw", lambda: Lraw[:]), ("gatetm", lambda: gate_tm[:]),
                        ("vbtm", lambda: vb_tm[:]), ("xnfm", lambda: xn_fm[:]),
                        ("negeA", lambda: negeA[:]), ("a16d", lambda: a16[:]),
                        ("dtb", lambda: dtb16[:]),
                        ("t16d", lambda: t16[:])):
            dbg_specs.append((nm, src))

    def debug_tail():
        barrier()
        def dt_gp(gp):
            for t in range(NC):
                tags[f"st_dbg{t}"] = dinc(gp.dma_start(
                    out=dy[128 * t:128 * (t + 1), :],
                    in_=stream[:, 1024 * t:1024 * (t + 1)]), "st")
            for nm, src in dbg_specs:
                ap = src()
                d = nc.dram_tensor("dbg_" + nm, list(ap.shape), ap.dtype,
                                   kind="ExternalOutput").ap()
                dinc(gp.dma_start(out=d, in_=ap), "st")
            gp.wait_ge(sem["st"], cnt["st"])
        GP(dt_gp)
        ctx.close()
        return nc

    if phases < 2:
        return debug_tail()

    for n in range(NC):
        def te_v(te, n=n):
            if n == 0:
                w(te, "gd", tags["gd_Ww"])
                w(te, "gd", tags["gd_Wgb"])
                w(te, "dve", tags["v_prev3"])
            else:
                w(te, "dve", tags[f"v_vb{n - 1}"])
                w(te, "dve", tags[f"v_ap{n - 1}"])
                w(te, "act", tags[f"a_gb{n - 1}"])
            for s in range(NS):
                lhs = xn_fm[:, SLABW * s + 1 + 128 * n:SLABW * s + 1 + 128 * (n + 1)]
                te.matmul(PS[:, 0:512], lhs, Wsh[:, 1024 * s:1024 * s + 512],
                          start=(s == 0), stop=(s == NS - 1), skip_group_check=True)
                te.matmul(PS[:, 512:1024], lhs, Wsh[:, 1024 * s + 512:1024 * (s + 1)],
                          start=(s == 0), stop=(s == NS - 1), skip_group_check=True)
            tags[f"pe_v{n}"] = inc(te.drain(), "pe")
            for s in range(NS):
                lhs = xn_fm[:, SLABW * s + 1 + 128 * n:SLABW * s + 1 + 128 * (n + 1)]
                te.matmul(PS[:, 1024:1072], lhs, Wgb[:, 48 * s:48 * (s + 1)],
                          start=(s == 0), stop=(s == NS - 1), skip_group_check=True)
            tags[f"pe_gba{n}"] = inc(te.drain(), "pe")
        TE(te_v)

        def sc_g(sc, n=n):
            w(sc, "pe", tags[f"pe_gba{n}"])
            sc.activation(gate_tm[:, 16 * n:16 * (n + 1)], PS[:, 1024:1040], AF.Sigmoid)
            sc.activation(beta_t[:], PS[:, 1040:1056], AF.Sigmoid)
            tags[f"a_gb{n}"] = inc(sc.drain(), "act")
        SC(sc_g)

        def ve_a(v, n=n):
            w(v, "pe", tags[f"pe_gba{n}"])
            if n > 0:
                w(v, "pe", tags[f"pe_at{n - 1}"])
            v.tensor_copy(apre[:], PS[:, 1056:1072])
            tags[f"v_ap{n}"] = inc(v.drain(), "dve")
        VE(ve_a)

        def te_at(te, n=n):
            w(te, "dve", tags[f"v_ap{n}"])
            if n > 0:
                w(te, "dve", tags[f"v_z{n - 1}"])
            te.transpose(PS[0:16, 1536:1664], apre[:], identf[:])
            tags[f"pe_at{n}"] = inc(te.drain(), "pe")
        TE(te_at)

        def ve_z(v, n=n):
            w(v, "pe", tags[f"pe_at{n}"])
            v.tensor_scalar(t16[:, 0:128], PS[0:16, 1536:1664], dtb16[:, 0:1],
                            None, ALU.add)
            tags[f"v_z{n}"] = inc(v.drain(), "dve")
        VE(ve_z)

        def sc_sp(sc, n=n):
            w(sc, "dve", tags[f"v_z{n}"])
            sc.activation(t16[:, 128:256], t16[0:16, 0:128], AF.Abs)
            sc.drain()
            sc.activation(t16[:, 256:384], t16[0:16, 128:256], AF.Exp, scale=-1.0)
            sc.drain()
            sc.activation(t16[:, 384:512], t16[0:16, 256:384], AF.Ln,
                          bias=consts[0:16, C_ONE:C_ONE + 1])
            tags[f"a_sp{n}"] = inc(sc.drain(), "act")
        SC(sc_sp)

        def ve_dec(v, n=n):
            w(v, "act", tags[f"a_sp{n}"])
            if n == 0:
                w(v, "dve", tags["z_dec"])
            v.tensor_scalar(t16[:, 512:640], t16[0:16, 0:128], 0.0, None, ALU.max)
            v.drain()
            v.tensor_tensor(t16[:, 512:640], t16[0:16, 512:640],
                            t16[0:16, 384:512], ALU.add)
            v.drain()
            v.tensor_scalar(t16[:, 512:640], t16[0:16, 512:640],
                            negeA[:, 0:1], None, ALU.mult)
            v.drain()
            v.tensor_tensor_scan(dec_fm[0:16, CK * n:CK * (n + 1)],
                                 t16[0:16, 512:640], t16[0:16, 512:640],
                                 0.0, ALU.add, ALU.bypass)
            v.drain()
            v.tensor_copy(declast[:, n:n + 1], dec_fm[0:16, CK * n + 127:CK * n + 128])
            tags[f"v_dec{n}"] = inc(v.drain(), "dve")
        VE(ve_dec)

        def sc_dw(sc, n=n):
            w(sc, "dve", tags[f"v_dec{n}"])
            sc.activation(t16[:, 640:641], declast[0:16, n:n + 1], AF.Exp)
            sc.activation(t16[:, 644:772], dec_fm[0:16, CK * n:CK * (n + 1)],
                          AF.Exp, scale=-1.0, bias=declast[0:16, n:n + 1])
            tags[f"a_dw{n}"] = inc(sc.drain(), "act")
        SC(sc_dw)

        def ve_vb(v, n=n):
            w(v, "pe", tags[f"pe_v{n}"])
            w(v, "act", tags[f"a_gb{n}"])
            for h in range(H):
                v.tensor_scalar(vb_tm[:, 1024 * n + 64 * h:1024 * n + 64 * (h + 1)],
                                PS[:, 64 * h:64 * (h + 1)],
                                beta_t[:, h:h + 1], None, ALU.mult)
            tags[f"v_vb{n}"] = inc(v.drain(), "dve")
            w(v, "act", tags[f"a_dw{n}"])
            v.tensor_copy(t16[:, 772:836].bitcast(BF16), t16[0:16, 644:772])
            v.tensor_scalar(R16[:], M8c[:], t16[0:16, 640:641], None, ALU.mult)
            tags[f"v_dw{n}"] = inc(v.drain(), "dve")
        VE(ve_vb)

        def te_dwt(te, n=n):
            w(te, "dve", tags[f"v_dw{n}"])
            if n > 0:
                w(te, "dve", tags[f"v_dwt{n - 1}"])
            te.transpose(psb(1800, 1808)[:, 0:16], t16[0:16, 772:836].bitcast(BF16),
                         identb[0:16, 0:16])
            te.matmul(PS[:, 2560:2568], QC32[:], R16[:], start=True, stop=True,
                      skip_group_check=True)
            tags[f"pe_dwt{n}"] = inc(te.drain(), "pe")
        TE(te_dwt)

        def ve_vbdw(v, n=n):
            w(v, "pe", tags[f"pe_dwt{n}"])
            if n > 0:
                w(v, "pe", tags[f"pe_u{n - 1}_{NS - 1}"])
            v.tensor_copy(dwT[:], psb(1800, 1808)[:, 0:16])
            v.tensor_copy(Earr[:], PS[:, 2560:2568])
            tags[f"v_dwt{n}"] = inc(v.drain(), "dve")
            for h in range(H):
                v.tensor_scalar(vbdw[:, 64 * h:64 * (h + 1)],
                                vb_tm[:, 1024 * n + 64 * h:1024 * n + 64 * (h + 1)],
                                dwT[:, h:h + 1], None, ALU.mult)
            tags[f"v_vbdw{n}"] = inc(v.drain(), "dve")
        VE(ve_vbdw)

        def ve_snap(v, n=n):
            v.tensor_copy(states[:, 512 * n:512 * (n + 1)],
                          S_run[:, 512 * (n % 2):512 * (n % 2) + 512])
            tags[f"v_snap{n}"] = inc(v.drain(), "dve")
        VE(ve_snap)

        for s in range(NS):
            def te_wk(te, n=n, s=s):
                if s == 0:
                    w(te, "dve", tags[f"v_vbdw{n}"])
                if n > 0 and s == 0:
                    w(te, "pe", tags[f"pe_u{n - 1}_{NS - 1}"])
                if s >= 2:
                    w(te, "dve", tags[f"v_wkev{n}_{s - 2}"])
                slot = s % 2
                te.transpose(psb(WKT[slot], WKT[slot] + 64),
                             xn_fm[:, SLABW * s + 128 * n:SLABW * s + 128 * (n + 1)],
                             identb[:])
                tags[f"pe_wk{n}_{s}"] = inc(te.drain(), "pe")
            TE(te_wk)

            def ve_wkev(v, n=n, s=s):
                w(v, "pe", tags[f"pe_wk{n}_{s}"])
                slot = s % 2
                v.tensor_copy(wk_tm[:, 128 * s:128 * (s + 1)],
                              psb(WKT[slot], WKT[slot] + 64))
                tags[f"v_wkev{n}_{s}"] = inc(v.drain(), "dve")
            VE(ve_wkev)

            def te_u(te, n=n, s=s):
                w(te, "dve", tags[f"v_wkev{n}_{s}"])
                if s >= 2:
                    w(te, "dve", tags[f"v_su{n}_{s - 2}"])
                slot = s % 2
                te.matmul(PS[:, 2048 + 128 * slot:2176 + 128 * slot],
                          wk_tm[:, 128 * s:128 * (s + 1)],
                          vbdw[:, 128 * s:128 * (s + 1)],
                          start=True, stop=True, skip_group_check=True)
                tags[f"pe_u{n}_{s}"] = inc(te.drain(), "pe")
            TE(te_u)

            def ve_srun(v, n=n, s=s):
                if s == 0:
                    w(v, "dve", tags[f"v_snap{n}"])
                w(v, "pe", tags[f"pe_u{n}_{s}"])
                slot = s % 2
                bc = 2048 + 128 * slot
                rd, wr = 512 * (n % 2), 512 * ((n + 1) % 2)
                v.scalar_tensor_tensor(S_run[0:64, wr + 64 * s:wr + 64 * (s + 1)],
                                       S_run[0:64, rd + 64 * s:rd + 64 * (s + 1)],
                                       Earr[0:64, s:s + 1],
                                       PS[0:64, bc:bc + 64], ALU.mult, ALU.add)
                v.scalar_tensor_tensor(S_run[64:128, wr + 64 * s:wr + 64 * (s + 1)],
                                       S_run[64:128, rd + 64 * s:rd + 64 * (s + 1)],
                                       Earr[64:128, s:s + 1],
                                       PS[64:128, bc + 64:bc + 128], ALU.mult, ALU.add)
                tags[f"v_su{n}_{s}"] = inc(v.drain(), "dve")
            VE(ve_srun)

    barrier()

    def p3_ve(v):
        v.reduce_sum(t16[:, 900:901], declast[0:16, 0:NC], axis=AX.X)
        v.tensor_tensor_scan(t16[:, 904:904 + NC], declast[0:16, 0:NC],
                             declast[0:16, 0:NC], 0.0, ALU.add, ALU.bypass)
        v.drain()
        v.tensor_tensor(t16[:, 960:960 + NC], t16[0:16, 904:904 + NC],
                        declast[0:16, 0:NC], ALU.subtract)
        tags["v_p3a"] = inc(v.drain(), "dve")
    VE(p3_ve)

    def p3_sc(sc):
        w(sc, "dve", tags["v_p3a"])
        sc.activation(t16[:, 940:941], t16[0:16, 900:901], AF.Exp)
        sc.activation(pn16[:, 0:NC], t16[0:16, 960:960 + NC], AF.Exp)
        tags["a_p3"] = inc(sc.drain(), "act")
    SC(p3_sc)

    def p3_gp(gp):
        w(gp, "act", tags["a_p3"])
        tags["gd_g1"] = dinc(gp.dma_start(
            out=gin.ap()[0:1, 0:65536].rearrange("a (p c) -> (a p) c", p=128),
            in_=S_run[:, 512 * (NC % 2):512 * (NC % 2) + 512]), "gd")
        tags["gd_g2"] = dinc(gp.dma_start(
            out=gin.ap()[0:1, 65536:65552].rearrange("a (p b) -> (a p) b", p=16),
            in_=t16[0:16, 940:941]), "gd")
        gp.wait_ge(sem["gd"], tags["gd_g2"])
        tags["cc"] = inc(gp.collective_compute(
            "AllGather", ALU.bypass, replica_groups=[list(range(8))],
            ins=[gin.ap()], outs=[gout.ap()]), "cc")
    GP(p3_gp)

    for k in range(len(wslabs)):
        def p3w_sy(sync, k=k):
            slot = k % 2
            w(sync, "gd", tags.get(f"gd_t{k - 2}"))
            tags[f"ld2_w{k}"] = dinc(sync.dma_start(
                out=stag[:, 512 * slot:512 * (slot + 1)], in_=wslabs[k][0]), "ld")
        SY(p3w_sy)

        def p3w_ve(v, k=k):
            w(v, "ld", tags[f"ld2_w{k}"])
            if k == 0:
                w(v, "dve", tags["v_wth"])
            w(v, "gd", tags.get(f"gd_t{k - 2}"))
            slot = k % 2
            mi = wslabs[k][2]
            v.tensor_scalar(xn_tmp[:, 0:512], stag[:, 512 * slot:512 * (slot + 1)],
                            nthv[:, mi:mi + 1], None, ALU.is_lt)
            v.scalar_tensor_tensor(ternst[:, 512 * slot:512 * (slot + 1)],
                                   stag[:, 512 * slot:512 * (slot + 1)],
                                   mwth[:, 3 + mi:4 + mi], xn_tmp[:, 0:512],
                                   ALU.is_gt, ALU.subtract)
            tags[f"v_t{k}"] = inc(v.drain(), "dve")
        VE(p3w_ve)

        def p3w_gp(gp, k=k):
            w(gp, "dve", tags[f"v_t{k}"])
            slot = k % 2
            tags[f"gd_t{k}"] = dinc(gp.dma_start(
                out=scr_dst(k), in_=ternst[:, 512 * slot:512 * (slot + 1)]), "gd")
        GP(p3w_gp)

    def p3c_sy(sync):
        w(sync, "dve", tags[f"v_t{len(wslabs) - 1}"])
        w(sync, "gd", tags[f"gd_t{len(wslabs) - 1}"])
        tags["ld_SELH"] = dinc(sync.dma_start(out=SELH[:], in_=dSELH), "ldc")
        w(sync, "gd", tags["gd_g1"])
        tags["ld_SEL16"] = dinc(sync.dma_start(out=SEL16[:], in_=dSEL16), "ldc")
        tags["ld_mask"] = dinc(sync.dma_start(out=maskst[:], in_=dmask), "ldc")
    SY(p3c_sy)
    LDC_ALL = cnt["ldc"]

    def p3r_gp(gp):
        w(gp, "cc", tags["cc"])
        for j in range(8):
            tags[f"gd_Dg{j}"] = dinc(gp.dma_start(
                out=DgT[:, j:j + 1],
                in_=gout.ap()[j:j + 1, 65536:65552].rearrange(
                    "a (p b) -> (a p) b", p=16)), "gd")
    GP(p3r_gp)

    def p3r_ve(v):
        w(v, "gd", tags["gd_Dg7"])
        v.tensor_scalar(G16[:], DgT[:], 1.0, None, ALU.subtract)
        v.drain()
        v.tensor_tensor(G16[:], G16[:], mprev16[:], ALU.mult)
        v.drain()
        v.tensor_scalar(G16[:], G16[:], 1.0, None, ALU.add)
        v.memset(P16[:, 7:8], 1.0)
        v.drain()
        for j in range(6, -1, -1):
            v.tensor_tensor(P16[:, j:j + 1], P16[:, j + 1:j + 2],
                            G16[:, j + 1:j + 2], ALU.mult)
            v.drain()
        v.tensor_tensor(cj16[:], P16[:], mprev16[:], ALU.mult)
        tags["v_cj"] = inc(v.drain(), "dve")
    VE(p3r_ve)

    for j in range(8):
        def p3s_gp(gp, j=j):
            if j > 0:
                w(gp, "dve", tags[f"v_s0_{j - 1}"])
            tags[f"gd_Sg{j}"] = dinc(gp.dma_start(
                out=Sg[:], in_=gout.ap()[j:j + 1, 0:65536].rearrange(
                    "a (p c) -> (a p) c", p=128)), "gd")
        GP(p3s_gp)

        def p3s_ve(v, j=j):
            if j == 0:
                w(v, "dve", tags["v_cj"])
            else:
                w(v, "pe", tags[f"pe_cb{j - 1}"])
            v.tensor_scalar(R16[:], M8c[:], cj16[:, j:j + 1], None, ALU.mult)
            tags[f"v_r{j}"] = inc(v.drain(), "dve")
        VE(p3s_ve)

        def p3s_te(te, j=j):
            w(te, "dve", tags[f"v_r{j}"])
            te.matmul(PS[:, 2560:2568], QC32[:], R16[:], start=True, stop=True,
                      skip_group_check=True)
            tags[f"pe_cb{j}"] = inc(te.drain(), "pe")
        TE(p3s_te)

        def p3s_ve2(v, j=j):
            w(v, "pe", tags[f"pe_cb{j}"])
            v.tensor_copy(cbj[:], PS[:, 2560:2568])
            w(v, "gd", tags[f"gd_Sg{j}"])
            v.drain()
            rd, wr = 512 * (j % 2), 512 * ((j + 1) % 2)
            for p in range(8):
                v.scalar_tensor_tensor(S0[:, wr + 64 * p:wr + 64 * (p + 1)],
                                       Sg[:, 64 * p:64 * (p + 1)],
                                       cbj[:, p:p + 1],
                                       S0[:, rd + 64 * p:rd + 64 * (p + 1)],
                                       ALU.mult, ALU.add)
            tags[f"v_s0_{j}"] = inc(v.drain(), "dve")
        VE(p3s_ve2)

    barrier()
    if phases < 4:
        return debug_tail()

    def p4_wo(gp):
        tags["gd_Wo"] = dinc(gp.dma_start(
            out=Wsh[:].rearrange("p (s n) -> p s n", s=NS),
            in_=dWo.rearrange("(s p) n -> p s n", p=128)), "gd")
    GP(p4_wo)

    for n in range(NC):
        def ve_pn(v, n=n):
            if not G("GPN"):
                tags[f"v_pn{n}"] = cnt["dve"]
                tags[f"pe_dt{n}"] = cnt["pe"]
                tags[f"v_ndt{n}"] = cnt["dve"]
                tags[f"v_str{n}"] = inc(v.drain(), "dve")
                return
            if n > 0:
                w(v, "pe", tags[f"pe_dt{n - 1}"])
            v.tensor_scalar(R16[:], M8c[:], pn16[:, n:n + 1], None, ALU.mult)
            tags[f"v_pn{n}"] = inc(v.drain(), "dve")
        VE(ve_pn)

        def te_dt(te, n=n):
            if not G("GPN"):
                return
            w(te, "dve", tags[f"v_pn{n}"])
            if n > 0:
                w(te, "dve", tags[f"v_ndt{n - 1}"])
            if G("GDT"):
                te.transpose(PS[:, 384:512], dec_fm[:, CK * n:CK * (n + 1)],
                             identf[:])
                te.matmul(PS[:, 640:648], QC32[:], R16[:], start=True, stop=True,
                          skip_group_check=True)
            tags[f"pe_dt{n}"] = inc(te.drain(), "pe")
        TE(te_dt)

        def ve_b(v, n=n):
            if not G("GPN"):
                return
            w(v, "pe", tags[f"pe_dt{n}"])
            if n >= 1:
                w(v, "pe", tags[f"pe_o{n - 1}_{H - 1}"])
            v.tensor_scalar(negdecT[:], PS[:, 384:400], -1.0, None, ALU.mult)
            v.tensor_copy(pnb[:], PS[:, 640:648])
            tags[f"v_ndt{n}"] = inc(v.drain(), "dve")
            if G("GST"):
                for p in range(8):
                    so = 64 * p
                    v.scalar_tensor_tensor(Strue[:, so:so + 64],
                                           S0[:, 64 * p:64 * (p + 1)],
                                           pnb[:, p:p + 1],
                                           states[:, 512 * n + 64 * p:512 * n + 64 * (p + 1)],
                                           ALU.mult, ALU.add)
            tags[f"v_str{n}"] = inc(v.drain(), "dve")
        VE(ve_b)

        def sc_de(sc, n=n):
            if not G("GDE"):
                tags[f"a_de{n}"] = cnt["act"]
                return
            w(sc, "dve", tags[f"v_ndt{n}"])
            sc.activation(t16[:, 644:772], dec_fm[0:16, CK * n:CK * (n + 1)], AF.Exp)
            tags[f"a_de{n}"] = inc(sc.drain(), "act")
        SC(sc_de)

        def ve_deb(v, n=n):
            w(v, "act", tags[f"a_de{n}"])
            v.tensor_copy(dexpb[0:16, :], t16[0:16, 644:772])
            tags[f"v_deb{n}"] = inc(v.drain(), "dve")
        VE(ve_deb)

        for s in range(NS):
            def te_rk(te, n=n, s=s):
                if s == 0:
                    w(te, "dve", tags[f"v_deb{n}"])
                    if n == 0:
                        w(te, "ldc", LDC_ALL)
                if s >= 2:
                    w(te, "dve", tags[f"v_rk{n}_{s - 2}"])
                slot = s % 2
                if G("GRKM"):
                    rkc = 256 + 1024 * slot
                    te.matmul(PS[:, rkc:rkc + 128],
                              SEL16[:, 128 * s:128 * (s + 1)], dexpb[:],
                              start=True, stop=True, skip_group_check=True)
                tags[f"pe_rk{n}_{s}"] = inc(te.drain(), "pe")
            TE(te_rk)

            def ve_rk(v, n=n, s=s):
                if s == 0 and n > 0:
                    w(v, "pe", tags[f"pe_o{n - 1}_{H - 1}"])
                w(v, "pe", tags[f"pe_rk{n}_{s}"])
                slot = s % 2
                rkc = 256 + 1024 * slot
                if G("GRKE"):
                    if G("GRKC"):
                        v.tensor_tensor(rkdx[:, 128 * s:128 * (s + 1)],
                                        xn_fm[:, SLABW * s + 1 + 128 * n:SLABW * s + 1 + 128 * (n + 1)],
                                        PS[:, rkc:rkc + 128], ALU.mult)
                    else:
                        v.tensor_copy(rkdx[:, 128 * s:128 * (s + 1)],
                                      PS[:, rkc:rkc + 128])
                tags[f"v_rk{n}_{s}"] = inc(v.drain(), "dve")
            VE(ve_rk)

        ordq = [hh for hh in range(H) if hh % 2 == 0] + \
               [hh for hh in range(H) if hh % 2 == 1]
        if p4sub < 2:
            def ve_skipA(v, n=n):
                v.memset(intra_sb[:], 0.0)
                for hh in range(H):
                    tags[f"v_iev{n}_{hh}"] = inc(v.drain(), "dve")
                    tags[f"a_L{n}_{hh}"] = cnt["act"]
                    tags[f"v_La{n}_{hh}"] = tags[f"v_iev{n}_{hh}"]
                    if p4sub < 3:
                        tags[f"pe_o{n}_{hh}"] = cnt["pe"]
            VE(ve_skipA)
        for q, hh in enumerate(ordq):
            if p4sub < 2:
                break
            par = hh % 2
            sl = hh // 2
            r0 = 64 * par
            islot = q % 4
            lslot = q % 2

            def te_hd(te, n=n, q=q, hh=hh, par=par, sl=sl, r0=r0,
                      islot=islot, lslot=lslot):
                if q == 0:
                    w(te, "dve", tags[f"v_rk{n}_{NS - 1}"])
                    w(te, "dve", tags[f"v_str{n}"])
                    if n == 0:
                        w(te, "ldc", LDC_ALL)
                    if n > 0:
                        w(te, "dve", tags[f"v_oev{n - 1}"])
                pL = (tags.get(f"v_La{n}_{ordq[q - 2]}") if q >= 2
                      else (tags.get(f"v_La{n - 1}_{ordq[q + 14]}") if n > 0 else None))
                w(te, "act", None)
                w(te, "dve", pL)
                pI = (tags.get(f"v_iev{n}_{ordq[q - 2]}") if q >= 2
                      else (tags.get(f"v_iev{n - 1}_{ordq[q + 14]}") if n > 0 else None))
                w(te, "dve", pI)
                lc = 512 + 1024 * lslot
                te.matmul(PS[:, lc:lc + 128],
                          SELH[:, 128 * hh:128 * (hh + 1)],
                          dec_fm[:, CK * n:CK * (n + 1)],
                          start=True, stop=True, skip_group_check=True)
                tags[f"pe_L{n}_{hh}"] = inc(te.drain(), "pe")
                ic = 1024 * (islot % 2) + 128 * (islot // 2)
                te.matmul(PS[:, ic:ic + 128],
                          xn_fm[r0:r0 + 64, SLABW * sl + 128 * n:SLABW * sl + 128 * (n + 1)],
                          xn_fm[r0:r0 + 64, SLABW * sl + 1 + 128 * n:SLABW * sl + 1 + 128 * (n + 1)],
                          start=True, stop=True, skip_group_check=True)
                tags[f"pe_i{n}_{hh}"] = inc(te.drain(), "pe")
            TE(te_hd)

            def ve_La(v, n=n, q=q, hh=hh, lslot=lslot):
                w(v, "pe", tags[f"pe_L{n}_{hh}"])
                if n == 0 and q == 0:
                    w(v, "ldc", LDC_ALL)
                pS = (tags.get(f"a_L{n}_{ordq[q - 2]}") if q >= 2
                      else (tags.get(f"a_L{n - 1}_{ordq[q + 14]}") if n > 0 else None))
                w(v, "act", pS)
                lc = 512 + 1024 * lslot
                v.tensor_tensor(Lraw[:, 128 * lslot:128 * (lslot + 1)],
                                PS[:, lc:lc + 128],
                                maskst[:], ALU.add)
                tags[f"v_La{n}_{hh}"] = inc(v.drain(), "dve")
            VE(ve_La)

            def sc_hd(sc, n=n, q=q, hh=hh, lslot=lslot):
                w(sc, "dve", tags[f"v_La{n}_{hh}"])
                pS = (tags.get(f"v_iev{n}_{ordq[q - 2]}") if q >= 2
                      else (tags.get(f"v_iev{n - 1}_{ordq[q + 14]}") if n > 0 else None))
                w(sc, "dve", pS)
                sc.activation(Lh_sb[:, 128 * lslot:128 * (lslot + 1)],
                              Lraw[:, 128 * lslot:128 * (lslot + 1)],
                              AF.Exp, bias=negdecT[:, hh:hh + 1])
                tags[f"a_L{n}_{hh}"] = inc(sc.drain(), "act")
            SC(sc_hd)

            def ve_hd(v, n=n, hh=hh, islot=islot, lslot=lslot):
                w(v, "pe", tags[f"pe_i{n}_{hh}"])
                w(v, "act", tags[f"a_L{n}_{hh}"])
                if n > 0:
                    w(v, "pe", tags[f"pe_o{n - 1}_{hh}"])
                ic = 1024 * (islot % 2) + 128 * (islot // 2)
                v.tensor_tensor(intra_sb[:, 128 * hh:128 * (hh + 1)],
                                PS[:, ic:ic + 128],
                                Lh_sb[:, 128 * lslot:128 * (lslot + 1)], ALU.mult)
                tags[f"v_iev{n}_{hh}"] = inc(v.drain(), "dve")
            VE(ve_hd)

            def te_o1(te, n=n, hh=hh, par=par, sl=sl, r0=r0):
                if p4sub < 3:
                    tags[f"pe_o{n}_{hh}"] = cnt["pe"]
                    return
                w(te, "dve", tags[f"v_iev{n}_{hh}"])
                te.matmul(PS[:, 2048 + 64 * hh:2048 + 64 * (hh + 1)],
                          intra_sb[:, 128 * hh:128 * (hh + 1)],
                          vb_tm[:, 1024 * n + 64 * hh:1024 * n + 64 * (hh + 1)],
                          start=True, stop=False, skip_group_check=True)
                te.matmul(PS[:, 2048 + 64 * hh:2048 + 64 * (hh + 1)],
                          rkdx[r0:r0 + 64, 128 * sl:128 * (sl + 1)],
                          Strue[r0:r0 + 64, 64 * sl:64 * (sl + 1)],
                          start=False, stop=True, skip_group_check=True)
                tags[f"pe_o{n}_{hh}"] = inc(te.drain(), "pe")
            TE(te_o1)

        def ve_oev(v, n=n):
            if n > 0:
                w(v, "pe", tags[f"pe_ot{n - 1}_{NS - 1}"])
            if p4sub >= 3:
                w(v, "pe", tags[f"pe_o{n}_{ordq[-1]}"])
            for hh in range(H):
                if p4sub >= 3:
                    v.tensor_scalar(o_tm[:, 64 * hh:64 * (hh + 1)],
                                    PS[:, 2048 + 64 * hh:2048 + 64 * (hh + 1)],
                                    gate_tm[:, 16 * n + hh:16 * n + hh + 1], None, ALU.mult)
                else:
                    v.memset(o_tm[:, 64 * hh:64 * (hh + 1)], 0.0)
            tags[f"v_oev{n}"] = inc(v.drain(), "dve")
        VE(ve_oev)

        for s in range(NS):
            if p4sub < 4:
                def ve_skipC(v, n=n, s=s):
                    w(v, "dve", tags[f"v_oev{n}"])
                    tags[f"pe_ot{n}_{s}"] = cnt["pe"]
                    tags[f"v_ot{n}_{s}"] = inc(v.drain(), "dve")
                VE(ve_skipC)
                continue
            def te_ot(te, n=n, s=s):
                if s == 0:
                    w(te, "dve", tags[f"v_oev{n}"])
                    if n == 0:
                        w(te, "gd", tags["gd_Wo"])
                if s >= 2:
                    w(te, "dve", tags[f"v_ot{n}_{s - 2}"])
                slot = s % 2
                te.transpose(psb(OTT[slot], OTT[slot] + 64),
                             o_tm[:, 128 * s:128 * (s + 1)], identb[:])
                tags[f"pe_ot{n}_{s}"] = inc(te.drain(), "pe")
            TE(te_ot)

            def ve_ot(v, n=n, s=s):
                if s == 0 and n > 0:
                    w(v, "pe", tags[f"pe_wo{n - 1}"])
                w(v, "pe", tags[f"pe_ot{n}_{s}"])
                slot = s % 2
                v.tensor_copy(o_fm[:, 128 * s:128 * (s + 1)],
                              psb(OTT[slot], OTT[slot] + 64))
                tags[f"v_ot{n}_{s}"] = inc(v.drain(), "dve")
            VE(ve_ot)

        def te_wo(te, n=n):
            if p4sub < 4:
                tags[f"pe_wo{n}"] = cnt["pe"]
                return
            if n > 0:
                w(te, "dve", tags[f"v_x1_{n - 1}"])
            for s in range(NS):
                w(te, "dve", tags[f"v_ot{n}_{s}"])
                te.matmul(PS[:, 3072:3584], o_fm[:, 128 * s:128 * (s + 1)],
                          Wsh[:, 1024 * s:1024 * s + 512],
                          start=(s == 0), stop=(s == NS - 1), skip_group_check=True)
                te.matmul(PS[:, 3584:4096], o_fm[:, 128 * s:128 * (s + 1)],
                          Wsh[:, 1024 * s + 512:1024 * (s + 1)],
                          start=(s == 0), stop=(s == NS - 1), skip_group_check=True)
            tags[f"pe_wo{n}"] = inc(te.drain(), "pe")
        TE(te_wo)

        def ve_x1(v, n=n):
            if p4sub >= 4:
                w(v, "pe", tags[f"pe_wo{n}"])
            v.drain()
            if p4sub >= 4:
                v.tensor_tensor(stream[:, 1024 * n:1024 * (n + 1)],
                                stream[:, 1024 * n:1024 * (n + 1)],
                                PS[:, 3072:4096], ALU.add)
            tags[f"v_x1_{n}"] = inc(v.drain(), "dve")
        VE(ve_x1)

    barrier()
    if phases < 5:
        return debug_tail()

    STT_ = 4
    n_st = (NC + STT_ - 1) // STT_

    for st in range(n_st):
        tiles = list(range(STT_ * st, min(STT_ * (st + 1), NC)))

        for i, t in enumerate(tiles):
            def ln2_ve(v, t=t, i=i, st=st):
                if i == 0 and st > 0:
                    w(v, "dve", tags[f"v_y{STT_ * st - 1}"])
                if t > 0:
                    w(v, "dve", tags[f"v_q1{t - 1}"])
                v.reduce_sum(t1a[:], stream[:, 1024 * t:1024 * (t + 1)], axis=AX.X)
                v.drain()
                v.tensor_scalar(t1c[:], t1a[:], 1.0 / DM, None, ALU.mult)
                v.drain()
                v.tensor_scalar(xc[:], stream[:, 1024 * t:1024 * (t + 1)],
                                t1c[:], None, ALU.subtract)
                v.reduce_max(q5a[:], xc[:], axis=AX.X, apply_absolute_value=True)
                tags[f"v_l2a{t}"] = inc(v.drain(), "dve")
            VE(ln2_ve)

            def ln2_sc(sc, t=t):
                w(sc, "dve", tags[f"v_l2a{t}"])
                if t > 0:
                    w(sc, "dve", tags[f"v_q1{t - 1}"])
                    w(sc, "pe", tags[f"pe_q1{t - 1}_{NS - 1}"])
                sc.activation(f_bf[:], xc[:], AF.Square, accum_out=t1b[:])
                sc.activation(f_bf[:], xc[:], AF.Sign)
                tags[f"a_l2{t}"] = inc(sc.drain(), "act")
            SC(ln2_sc)

            def ln2_ve2(v, t=t):
                w(v, "act", tags[f"a_l2{t}"])
                v.tensor_scalar(t1d[:], t1b[:], 1.0 / DM, None, ALU.mult)
                tags[f"v_l2b{t}"] = inc(v.drain(), "dve")
            VE(ln2_ve2)

            def ln2_sc2(sc, t=t):
                w(sc, "dve", tags[f"v_l2b{t}"])
                sc.activation(t1a[:], t1d[:], AF.Ln, bias=consts[:, C_EPS:C_EPS + 1])
                sc.drain()
                sc.activation(t1d[:], t1a[:], AF.Exp, scale=-0.5)
                tags[f"a_l2b{t}"] = inc(sc.drain(), "act")
            SC(ln2_sc2)

            def ln2_ve3(v, t=t):
                w(v, "act", tags[f"a_l2b{t}"])
                v.tensor_tensor(q5b[:], q5a[:], t1d[:], ALU.mult)
                v.drain()
                v.tensor_scalar(q5b[:], q5b[:], 1e-5, None, ALU.max)
                v.drain()
                v.reciprocal(q5c[:], q5b[:])
                v.drain()
                v.tensor_tensor(q5d[:], q5c[:], t1d[:], ALU.mult)
                v.drain()
                v.tensor_scalar(q5d[:], q5d[:], 127.0, None, ALU.mult)
                v.scalar_tensor_tensor(cgv[:, t:t + 1], q5b[:], 1.0 / 127.0,
                                       mwth[:, 0:1], ALU.mult, ALU.mult)
                v.scalar_tensor_tensor(cuv[:, t:t + 1], q5b[:], 1.0 / 127.0,
                                       mwth[:, 1:2], ALU.mult, ALU.mult)
                tags[f"v_l2c{t}"] = inc(v.drain(), "dve")
            VE(ln2_ve3)

            def q1_sc(sc, t=t):
                w(sc, "dve", tags[f"v_l2c{t}"])
                sc.activation(f_tmp[:], xc[:], AF.Copy, scale=q5d[:, 0:1],
                              bias=MAGIC)
                sc.drain()
                sc.activation(xc[:], f_tmp[:], AF.Copy, bias=-MAGIC)
                tags[f"a_q1{t}"] = inc(sc.drain(), "act")
            SC(q1_sc)

            def q1_ve(v, t=t):
                w(v, "act", tags[f"a_q1{t}"])
                v.scalar_tensor_tensor(f_tmp[:], f_bf[:], DELTA, xc[:],
                                       ALU.mult, ALU.add)
                v.tensor_copy(xc[:].bitcast(I32), f_tmp[:])
                v.tensor_copy(f_bf[:], xc[:].bitcast(I32))
                tags[f"v_q1{t}"] = inc(v.drain(), "dve")
            VE(q1_ve)

            for s in range(NS):
                def q1_te(te, t=t, i=i, st=st, s=s):
                    gi = t * NS + s
                    if s == 0:
                        w(te, "dve", tags[f"v_q1{t}"])
                        if i == 0 and st > 0:
                            w(te, "dve", tags[f"v_y{STT_ * st - 1}"])
                    if s >= 2:
                        w(te, "dve", tags[f"v_q1e{t}_{s - 2}"])
                    slot = gi % 2
                    te.transpose(psb(QT[slot], QT[slot] + 64),
                                 f_bf[:, 128 * s:128 * (s + 1)], identb[:])
                    tags[f"pe_q1{t}_{s}"] = inc(te.drain(), "pe")
                TE(q1_te)

                def q1_ve2(v, t=t, i=i, s=s):
                    gi = t * NS + s
                    w(v, "pe", tags[f"pe_q1{t}_{s}"])
                    slot = gi % 2
                    v.tensor_copy(q1_fm[:, 512 * s + 128 * i:512 * s + 128 * (i + 1)],
                                  psb(QT[slot], QT[slot] + 64))
                    tags[f"v_q1e{t}_{s}"] = inc(v.drain(), "dve")
                VE(q1_ve2)

        for m8 in range(8):
            def gu_sy(sync, m8=m8, st=st):
                if m8 > 0:
                    w(sync, "pe", tags[f"pe_gu{st}_{m8 - 1}"])
                elif st > 0:
                    w(sync, "pe", tags[f"pe_dn{st - 1}_7"])
                w(sync, "gd", tags[f"gd_t{len(wslabs) - 1}"])
                tags[f"ld_tg{st}_{m8}"] = dinc(sync.dma_start(
                    out=tg_sb[:].rearrange("p (s n) -> p s n", s=NS),
                    in_=scr_g.ap()[:, 512 * m8:512 * (m8 + 1)].rearrange(
                        "(s p) n -> p s n", p=128)), "ld")
            SY(gu_sy)

            def gu_gp(gp, m8=m8, st=st):
                if m8 > 0:
                    w(gp, "pe", tags[f"pe_gu{st}_{m8 - 1}"])
                elif st > 0:
                    w(gp, "pe", tags[f"pe_dn{st - 1}_7"])
                tags[f"ld_tu{st}_{m8}"] = dinc(gp.dma_start(
                    out=tu_sb[:].rearrange("p (s n) -> p s n", s=NS),
                    in_=scr_u.ap()[:, 512 * m8:512 * (m8 + 1)].rearrange(
                        "(s p) n -> p s n", p=128)), "gd")
            GP(gu_gp)

            for i, t in enumerate(tiles):
                def gu_te(te, m8=m8, i=i, t=t, st=st, tiles=tiles):
                    if m8 == 0 and i == 0:
                        for t2 in tiles:
                            w(te, "dve", tags[f"v_q1e{t2}_{NS - 1}"])
                        if st > 0:
                            w(te, "dve", tags[f"v_y{STT_ * st - 1}"])
                    w(te, "ld", tags[f"ld_tg{st}_{m8}"])
                    w(te, "gd", tags[f"ld_tu{st}_{m8}"])
                    if i >= 2:
                        w(te, "dve", tags.get(f"v_h{st}_{m8}_{tiles[i - 2]}"))
                    elif m8 > 0:
                        pi = tiles[i + 2] if i + 2 < len(tiles) else tiles[-1]
                        w(te, "dve", tags.get(f"v_h{st}_{m8 - 1}_{pi}"))
                    ga = 1024 * (i % 2)
                    gb = ga + 512
                    for s in range(NS):
                        lhs = q1_fm[:, 512 * s + 128 * i:512 * s + 128 * (i + 1)]
                        te.matmul(PS[:, ga:ga + 512], lhs,
                                  tg_sb[:, 512 * s:512 * (s + 1)],
                                  start=(s == 0), stop=(s == NS - 1),
                                  skip_group_check=True)
                    for s in range(NS):
                        lhs = q1_fm[:, 512 * s + 128 * i:512 * s + 128 * (i + 1)]
                        te.matmul(PS[:, gb:gb + 512], lhs,
                                  tu_sb[:, 512 * s:512 * (s + 1)],
                                  start=(s == 0), stop=(s == NS - 1),
                                  skip_group_check=True)
                    tags[f"pe_g{st}_{m8}_{t}"] = inc(te.drain(), "pe")
                    if i == len(tiles) - 1:
                        tags[f"pe_gu{st}_{m8}"] = tags[f"pe_g{st}_{m8}_{t}"]
                TE(gu_te)

                def gu_sc(sc, m8=m8, i=i, t=t, st=st, tiles=tiles):
                    w(sc, "pe", tags[f"pe_g{st}_{m8}_{t}"])
                    if i >= 2:
                        w(sc, "dve", tags.get(f"v_h{st}_{m8}_{tiles[i - 2]}"))
                    elif m8 > 0:
                        pi = tiles[i + 2] if i + 2 < len(tiles) else tiles[-1]
                        w(sc, "dve", tags.get(f"v_h{st}_{m8 - 1}_{pi}"))
                    ga = 1024 * (i % 2)
                    sc.activation(xc[:, 512 * (i % 2):512 * (i % 2) + 512],
                                  PS[:, ga:ga + 512], AF.Silu, scale=cgv[:, t:t + 1])
                    tags[f"a_h{st}_{m8}_{t}"] = inc(sc.drain(), "act")
                SC(gu_sc)

                def gu_ve(v, m8=m8, i=i, t=t, st=st):
                    w(v, "act", tags[f"a_h{st}_{m8}_{t}"])
                    gb = 1024 * (i % 2) + 512
                    v.scalar_tensor_tensor(
                        h_arena[:, INNER * i + 512 * m8:INNER * i + 512 * (m8 + 1)],
                        PS[:, gb:gb + 512], cuv[:, t:t + 1],
                        xc[:, 512 * (i % 2):512 * (i % 2) + 512], ALU.mult, ALU.mult)
                    tags[f"v_h{st}_{m8}_{t}"] = inc(v.drain(), "dve")
                VE(gu_ve)

        for i, t in enumerate(tiles):
            def hq_ve(v, i=i, t=t, st=st):
                w(v, "dve", tags[f"v_h{st}_7_{t}"])
                if t > 0:
                    w(v, "act", tags.get(f"a_hz{t - 1}_3"))
                v.reduce_max(q5a[:], h_arena[:, INNER * i:INNER * (i + 1)],
                             axis=AX.X, apply_absolute_value=True)
                v.drain()
                v.tensor_scalar(q5b[:], q5a[:], 1e-5, None, ALU.max)
                v.drain()
                v.reciprocal(q5c[:], q5b[:])
                v.drain()
                v.tensor_scalar(q5d[:], q5c[:], 127.0, None, ALU.mult)
                v.scalar_tensor_tensor(cdv[:, t:t + 1], q5b[:], 1.0 / 127.0,
                                       mwth[:, 2:3], ALU.mult, ALU.mult)
                tags[f"v_hq{t}"] = inc(v.drain(), "dve")
            VE(hq_ve)

            for qr in range(4):
                def hqq_sc(sc, i=i, t=t, qr=qr):
                    hsl = h_arena[:, INNER * i + 1024 * qr:INNER * i + 1024 * (qr + 1)]
                    if qr == 0:
                        w(sc, "dve", tags[f"v_hq{t}"])
                    else:
                        w(sc, "dve", tags[f"v_hqq{t}_{qr - 1}"])
                    sc.activation(f_bf[:], hsl, AF.Sign)
                    sc.activation(f_tmp[:], hsl, AF.Copy, scale=q5d[:, 0:1],
                                  bias=MAGIC)
                    sc.drain()
                    sc.activation(xc[:], f_tmp[:], AF.Copy, bias=-MAGIC)
                    tags[f"a_hz{t}_{qr}"] = inc(sc.drain(), "act")
                SC(hqq_sc)

                def hqq_ve(v, i=i, t=t, qr=qr):
                    w(v, "act", tags[f"a_hz{t}_{qr}"])
                    if qr == 0 and t > 0:
                        w(v, "pe", tags.get(f"pe_hf{t - 1}_31"))
                    v.scalar_tensor_tensor(f_tmp[:], f_bf[:], DELTA, xc[:],
                                           ALU.mult, ALU.add)
                    v.tensor_copy(xc[:].bitcast(I32), f_tmp[:])
                    v.tensor_copy(hq_tm[:, 1024 * qr:1024 * (qr + 1)],
                                  xc[:].bitcast(I32))
                    tags[f"v_hqq{t}_{qr}"] = inc(v.drain(), "dve")
                VE(hqq_ve)

            for s2 in range(32):
                def hq_te(te, i=i, t=t, s2=s2):
                    if s2 == 0:
                        w(te, "dve", tags[f"v_hqq{t}_3"])
                    if s2 >= 2:
                        w(te, "dve", tags[f"v_hf{t}_{s2 - 2}"])
                    slot = s2 % 2
                    te.transpose(psb(QT[slot], QT[slot] + 64),
                                 hq_tm[:, 128 * s2:128 * (s2 + 1)], identb[:])
                    tags[f"pe_hf{t}_{s2}"] = inc(te.drain(), "pe")
                TE(hq_te)

                def hq_ve3(v, i=i, t=t, s2=s2):
                    hq_fm = h_arena[:, INNER * i:INNER * i + 2048].bitcast(BF16)
                    w(v, "pe", tags[f"pe_hf{t}_{s2}"])
                    slot = s2 % 2
                    v.tensor_copy(hq_fm[:, 128 * s2:128 * (s2 + 1)],
                                  psb(QT[slot], QT[slot] + 64))
                    tags[f"v_hf{t}_{s2}"] = inc(v.drain(), "dve")
                VE(hq_ve3)

        for m8 in range(8):
            def dn_sy(sync, m8=m8, st=st):
                if m8 >= 2:
                    w(sync, "pe", tags[f"pe_dn{st}_{m8 - 2}"])
                else:
                    w(sync, "pe", tags[f"pe_gu{st}_7"])
                buf = td_sb if m8 % 2 == 0 else td_sb2
                tags[f"ld_td{st}_{m8}"] = dinc(sync.dma_start(
                    out=buf[:].rearrange("p (s n) -> p s n", s=4),
                    in_=scr_d.ap()[512 * m8:512 * (m8 + 1), :].rearrange(
                        "(s p) n -> p s n", p=128)), "ld")
            SY(dn_sy)

            def dn_te(te, m8=m8, st=st, tiles=tiles):
                td_cur = td_sb if m8 % 2 == 0 else td_sb2
                w(te, "ld", tags[f"ld_td{st}_{m8}"])
                if m8 == 0:
                    for t2 in tiles:
                        w(te, "dve", tags[f"v_hf{t2}_31"])
                        w(te, "dve", tags[f"v_h{st}_7_{t2}"])
                for i2, t2 in enumerate(tiles):
                    hq_fm = h_arena[:, INNER * i2:INNER * i2 + 2048].bitcast(BF16)
                    for o2 in range(2):
                        for s4 in range(4):
                            s2 = 4 * m8 + s4
                            te.matmul(
                                PS[:, 1024 * i2 + 512 * o2:1024 * i2 + 512 * (o2 + 1)],
                                hq_fm[:, 128 * s2:128 * (s2 + 1)],
                                td_cur[:, 1024 * s4 + 512 * o2:1024 * s4 + 512 * (o2 + 1)],
                                start=(m8 == 0 and s4 == 0),
                                stop=(m8 == 7 and s4 == 3),
                                skip_group_check=True)
                tags[f"pe_dn{st}_{m8}"] = inc(te.drain(), "pe")
            TE(dn_te)

        for i, t in enumerate(tiles):
            def y_ve(v, i=i, t=t, st=st):
                w(v, "pe", tags[f"pe_dn{st}_7"])
                w(v, "st", tags.get(f"st_y{t - 1}"))
                for o2 in range(2):
                    v.scalar_tensor_tensor(
                        y_sb[:, 512 * o2:512 * (o2 + 1)],
                        PS[:, 1024 * i + 512 * o2:1024 * i + 512 * (o2 + 1)],
                        cdv[:, t:t + 1],
                        stream[:, 1024 * t + 512 * o2:1024 * t + 512 * (o2 + 1)],
                        ALU.mult, ALU.add)
                tags[f"v_y{t}"] = inc(v.drain(), "dve")
            VE(y_ve)

            def y_gp(gp, i=i, t=t):
                w(gp, "dve", tags[f"v_y{t}"])
                tags[f"st_y{t}"] = dinc(gp.dma_start(
                    out=dy[128 * t:128 * (t + 1), :], in_=y_sb[:]), "st")
            GP(y_gp)

    def fin(gp):
        gp.wait_ge(sem["st"], cnt["st"])
    GP(fin)

    ctx.close()
    return nc



def _host_consts():
    c = np.zeros((128, 16), np.float32)
    c[:, C_EPS] = EPS
    c[:, C_MAGIC] = MAGIC
    c[:, C_NMAGIC] = -MAGIC
    c[:, C_ONE] = 1.0
    return c


def _selector_consts():
    SELHv = np.zeros((128, H * CK), np.float32)
    for h in range(H):
        SELHv[h, 128 * h:128 * (h + 1)] = 1.0
    QCv = np.zeros((16, 128), np.float32)
    for h in range(16):
        QCv[h, 64 * (h % 2):64 * (h % 2) + 64] = 1.0
    M8v = np.zeros((16, 8), np.float32)
    for h in range(16):
        M8v[h, h // 2] = 1.0
    S16v = np.zeros((128, NS * 128), np.float32)
    for s in range(NS):
        for p in range(128):
            S16v[2 * s + p // 64, 128 * s + p] = 1.0
    maskv = np.full((128, 128), -1e9, np.float32)
    for k in range(128):
        maskv[k, k + 1:] = 0.0
    return SELHv, QCv, M8v, S16v.astype(ml_dtypes.bfloat16), maskv


def make_in_maps(inputs, NC=16, n_cores=8):
    x = np.asarray(inputs["L0_attn_out"], np.float32)
    B = x.shape[0]
    spb = n_cores // B
    TPC = NC * CK
    WwT = np.ascontiguousarray(np.asarray(inputs["L0_attn_W_write"], np.float32).T)
    WgbaT = np.ascontiguousarray(np.concatenate(
        [np.asarray(inputs["L0_attn_W_gate"], np.float32),
         np.asarray(inputs["L0_attn_W_beta"], np.float32),
         np.asarray(inputs["L0_attn_W_alpha"], np.float32)], axis=0).T)
    WoT = np.ascontiguousarray(np.asarray(inputs["L0_attn_W_out"], np.float32).T)
    wgT = np.ascontiguousarray(np.asarray(inputs["L0_ffn_w_gate"], np.float32).T)
    wuT = np.ascontiguousarray(np.asarray(inputs["L0_ffn_w_up"], np.float32).T)
    wdT = np.ascontiguousarray(np.asarray(inputs["L0_ffn_w_down"], np.float32).T)
    a16v = np.ascontiguousarray(
        np.asarray(inputs["L0_attn_A_log"], np.float32).reshape(16, 1))
    dtbv = np.ascontiguousarray(
        np.asarray(inputs["L0_attn_dt_bias"], np.float32).reshape(16, 1))
    SELHv, QCv, M8v, S16v, maskv = _selector_consts()
    common = dict(
        WwT=WwT, WgbaT=WgbaT, WoT=WoT, wgT=wgT, wuT=wuT, wdT=wdT,
        a16=a16v, dtb16=dtbv, consts=_host_consts(),
        identb=np.eye(128, dtype=ml_dtypes.bfloat16),
        identf=np.eye(128, dtype=np.float32),
        maskst=maskv, SELH=SELHv, QC32=QCv, M8c=M8v, SEL16b=S16v,
        ones1f=np.ones((1, 128), np.float32),
        ones128=np.ones((128, 1), np.float32),
    )
    in_maps = []
    for c in range(n_cores):
        b, s = c // spb, c % spb
        x_sl = np.ascontiguousarray(x[b, s * TPC:(s + 1) * TPC])
        x_prev = (np.zeros((1, DM), np.float32) if s == 0
                  else np.ascontiguousarray(x[b, s * TPC - 1:s * TPC]))
        mprev = np.zeros((1, 8), np.float32)
        for j in range(n_cores):
            if j // spb == b and j % spb < s:
                mprev[0, j] = 1.0
        m = dict(common)
        m.update(x_sl=x_sl, x_prev=x_prev,
                 mprev16=np.ascontiguousarray(np.tile(mprev, (16, 1))))
        in_maps.append(m)
    return in_maps


def kernel(**inputs):
    x = np.asarray(inputs["L0_attn_out"])
    B, T_full, _ = x.shape
    n_cores = 8
    NC = (B * T_full) // (n_cores * CK)
    nc = build_program(NC)
    in_maps = make_in_maps(inputs, NC=NC, n_cores=n_cores)
    res = run_bass_kernel_spmd(nc, in_maps, core_ids=list(range(n_cores)))
    TPC = NC * CK
    out = np.empty((B, T_full, DM), np.float32)
    spb = n_cores // B
    for c in range(n_cores):
        b, s = c // spb, c % spb
        out[b, s * TPC:(s + 1) * TPC] = res.results[c]["y_sl"]
    return out
